# version 12
# baseline (speedup 1.0000x reference)
import os
import numpy as np
from contextlib import ExitStack

EMIT_DRAINS = os.environ.get("BASS_SIM_FP64") == "1" or os.environ.get("KERNEL_SIM_DRAINS") == "1"
HW_DRAINS = set(os.environ.get("KERNEL_HW_DRAINS", "p1").split(","))

import concourse.bass as bass
import concourse.mybir as mybir
from concourse.bass_utils import run_bass_kernel_spmd

F32 = mybir.dt.float32
F32R = mybir.dt.float32r
ALU = mybir.AluOpType
AF = mybir.ActivationFunctionType

D_MODEL = 1024
D_STATE = 16
D_CONV = 4
DH = 1024
IH = 2048
NCORES = 8
EPS = 1e-5
INV_D = 1.0 / D_MODEL

ENGS = ("sync", "gpsimd", "tensor", "vector", "scalar")


class Emitter:
    def __init__(self, nc, es):
        self.nc = nc
        self.streams = {e: [] for e in ENGS}
        self.cnt = {}
        self.sems = {}
        self.nbar = 0
        self.phase = "pro"
        self._es = es

    def sem(self, name):
        self.sems[name] = self._es.enter_context(self.nc.semaphore(name))
        self.cnt[name] = 0

    def op(self, eng, fn, inc=None, by=1):
        if inc is None:
            self.streams[eng].append(fn)
            return None
        s = self.sems[inc]
        self.cnt[inc] += by
        def wrapped(e):
            fn(e).then_inc(s, by)
        self.streams[eng].append(wrapped)
        return self.cnt[inc]

    def wait(self, eng, name, val, force=False):
        if val <= 0 and not force:
            return
        s = self.sems[name]
        self.streams[eng].append(lambda e: e.wait_ge(s, max(val, 0)))

    BAR_ENGS = ("gpsimd", "tensor", "vector", "scalar")

    def barrier(self):
        self.nbar += 1
        tgt = self.nbar * len(self.BAR_ENGS)
        s = self.sems["bar"]
        for eng in self.BAR_ENGS:
            self.streams[eng].append(lambda e: e.nop().then_inc(s, 1))
            self.streams[eng].append(lambda e: e.wait_ge(s, tgt))

    def drain(self, eng, hw=False):
        if EMIT_DRAINS or eng == "scalar" or hw or "all" in HW_DRAINS \
                or self.phase in HW_DRAINS:
            self.streams[eng].append(lambda e: e.drain())


def build_program(L):
    Tc = min(512, L)
    NCH = L // Tc
    HB = Tc // 2
    NG = DH // 128
    NDM = D_MODEL // 128
    NM1 = IH // 128
    NTS = Tc // 128

    NA_SLOT = 8
    NB_SLOT = 8

    nc = bass.Bass("TRN2", target_bir_lowering=False)

    x_hbm = nc.dram_tensor("x", [L, D_MODEL], F32, kind="ExternalInput").ap()
    w_in = nc.dram_tensor("w_in", [D_MODEL, 2 * DH], F32R, kind="ExternalInput").ap()
    w1 = nc.dram_tensor("w1", [D_MODEL, IH], F32R, kind="ExternalInput").ap()
    w2 = nc.dram_tensor("w2", [IH, D_MODEL], F32R, kind="ExternalInput").ap()
    w_out = nc.dram_tensor("w_out", [DH, D_MODEL], F32R, kind="ExternalInput").ap()
    w_xp = nc.dram_tensor("w_xp", [DH, 65], F32R, kind="ExternalInput").ap()
    pcons = nc.dram_tensor("pcons", [128, 104], F32, kind="ExternalInput").ap()
    a_log = nc.dram_tensor("a_log", [128, 128], F32, kind="ExternalInput").ap()
    ident_in = nc.dram_tensor("identc", [128, 128], F32, kind="ExternalInput").ap()
    sel16_in = nc.dram_tensor("sel16c", [16, 2048], F32R, kind="ExternalInput").ap()
    identr_in = nc.dram_tensor("identrc", [128, 128], F32R, kind="ExternalInput").ap()
    onesr_in = nc.dram_tensor("onesrc", [128, 1], F32R, kind="ExternalInput").ap()
    ones_in = nc.dram_tensor("onesc", [128, 1], F32, kind="ExternalInput").ap()
    out_hbm = nc.dram_tensor("out", [L, D_MODEL], F32, kind="ExternalOutput").ap()

    xpj_snd = nc.dram_tensor("xpj_snd", [NCH, 65, Tc], F32R)
    xpj_rcv = nc.dram_tensor("xpj_rcv", [NCH, 65, Tc], F32R)
    op_snd = nc.dram_tensor("op_snd", [NCH, 128, NDM * Tc], F32)
    op_rcv = nc.dram_tensor("op_rcv", [NCH, 128, NDM * Tc], F32)
    GROUPS = [[0, 1], [2, 3], [4, 5], [6, 7]]

    es = ExitStack()
    sb = lambda nm, shp, dt=F32: es.enter_context(nc.sbuf_tensor(nm, shp, dt))
    ps = lambda nm, shp: es.enter_context(nc.psum_tensor(nm, shp, F32))

    with es:
        pc = sb("pc", [128, 104])
        aneg = sb("aneg", [128, 128])
        identT = sb("identT", [128, 128])
        identR = sb("identR", [128, 128], F32R)
        sel16 = sb("sel16", [16, 2048], F32R)
        onesT = sb("onesT", [128, 1])
        onesR = sb("onesR", [128, 1], F32R)
        wxp_sb = sb("wxp_sb", [128, 8 * 65], F32R)

        wpa = sb("wpa", [128, NA_SLOT * 512], F32R)
        wpb = sb("wpb", [128, NB_SLOT * 256], F32R)

        tok = sb("tok", [128, 2 * D_MODEL])
        scr_r = sb("scr_r", [128, NDM * Tc], F32R)
        xn = sb("xn", [128, NDM * Tc], F32R)
        xn2 = sb("xn2", [128, NG * Tc], F32R)
        r1h = sb("r1h", [128, NDM * Tc])
        sq = sb("sq", [128, Tc])

        xh_rot = sb("xh_rot", [128, Tc + 3])
        halo = sb("halo", [128, NG * 3])
        cacc = sb("cacc", [128, Tc])
        row1 = cacc[0:1, 0:Tc]
        row2 = sq[0:1, 0:Tc]
        xc = sb("xc", [128, NG * Tc], F32R)
        sz = sb("sz", [128, NG * Tc])
        dt_sb = sb("dt_sb", [128, NG * Tc])
        xpjB = sb("xpjB", [16, Tc], F32R)
        xpjC = sb("xpjC", [16, Tc], F32R)

        da_b = sb("da_b", [128, 3 * Tc])
        dbx_b = sb("dbx_b", [128, 2 * Tc])
        h32_b = sb("h32_b", [128, 2 * Tc])
        hc_b = sb("hc_b", [128, 4 * Tc], F32R)
        state = sb("state", [128, 128])

        cst = sb("cst", [128, 4])
        ysz = sb("ysz", [128, NG * Tc], F32R)
        bst = ysz[0:16, 0:Tc]
        xpjp = ysz[0:65, Tc:2 * Tc]

        pA = ps("pA", [128, 512])
        pB = ps("pB", [128, 512])
        pT = ps("pT", [128, 512])
        pS = ps("pS", [128, 512])
        pY = [ps(f"pY{i}", [128, 512]) for i in range(4)]

        nc.const_aps.register(F32, EPS, cst[:, 0:1])
        nc.const_aps.register(F32, 1.0, cst[:, 1:2])
        nc.const_aps.register(F32, -0.5, cst[:, 2:3])
        nc.const_aps.register(F32, -1.0, cst[:, 3:4])

        em = Emitter(nc, es)
        for s in ("bar", "ldw", "ldwb", "ldx", "lda", "mm", "ev", "dv", "da",
                  "bc", "sc", "hc", "dbxs", "yx", "cc", "io", "st", "ffs"):
            em.sem(s)

        for t, src in ((pc, pcons), (aneg, a_log), (identT, ident_in),
                       (identR, identr_in), (sel16, sel16_in)):
            em.op("sync", lambda e, t=t, src=src: e.dma_start(out=t[:], in_=src),
                  inc="lda", by=16)
        em.op("sync", lambda e: e.dma_start(out=onesT[:], in_=ones_in),
              inc="lda", by=16)
        em.op("sync", lambda e: e.dma_start(out=onesR[:], in_=onesr_in[:, 0:1]),
              inc="lda", by=16)
        em.op("sync", lambda e: e.dma_start(
            out=wxp_sb[:, 0:8 * 65].rearrange("p (g c) -> p g c", c=65),
            in_=w_xp.rearrange("(g p) c -> p g c", p=128)), inc="lda", by=16)
        v_lda = em.cnt["lda"]
        em.wait("scalar", "lda", v_lda)
        em.op("scalar", lambda e: e.activation(aneg[:], aneg[:], AF.Exp), inc="ev")
        ev0 = em.cnt["ev"]
        em.wait("vector", "ev", ev0)
        em.op("vector", lambda e: e.tensor_scalar(aneg[:], aneg[:], -1.0, None, ALU.mult), inc="dv")
        em.drain("vector")
        em.op("vector", lambda e: e.memset(state[:], 0.0), inc="dv")
        em.drain("vector")
        em.op("vector", lambda e: e.memset(halo[:], 0.0), inc="dv")
        em.drain("vector")
        em.op("vector", lambda e: e.memset(cacc[:], 0.0), inc="dv")
        em.drain("vector")
        em.op("vector", lambda e: e.tensor_copy(ysz[0:16, 0:Tc], cacc[0:16, 0:Tc]), inc="dv")
        em.drain("vector")
        em.op("vector", lambda e: e.memset(cst[:, 0:1], EPS), inc="dv")
        em.op("vector", lambda e: e.memset(cst[:, 1:2], 1.0), inc="dv")
        em.op("vector", lambda e: e.memset(cst[:, 2:3], -0.5), inc="dv")
        em.op("vector", lambda e: e.memset(cst[:, 3:4], -1.0), inc="dv")
        em.drain("vector")
        em.wait("scalar", "dv", em.cnt["dv"])
        em.wait("tensor", "lda", v_lda)
        em.wait("tensor", "dv", em.cnt["dv"])

        class LoadQ:
            def __init__(self, pool, sem_name, nslot, width=512):
                self.pool = pool
                self.sem = sem_name
                self.width = width
                self.ring = nslot // 4
                self.pending = []
                self.n_loaded = 0
                self.cons = []

            def add(self, w, kslice, cslice):
                self.pending.append((w, kslice, cslice))

            def record(self, mm_val):
                self.cons.append(mm_val)
                self.pump()

            def pump(self):
                while self.pending:
                    g_ = self.n_loaded // 4
                    if g_ - self.ring >= len(self.cons):
                        break
                    if self.n_loaded % 4 == 0:
                        if g_ >= self.ring:
                            em.wait("sync", "mm", self.cons[g_ - self.ring])
                        if g_ >= 1:
                            em.wait("sync", self.sem, self.n_loaded * 16)
                    w, ks, cs = self.pending.pop(0)
                    s_ = self.n_loaded % (self.ring * 4)
                    wd = self.width
                    em.op("sync", lambda e, s_=s_, w=w, ks=ks, cs=cs, wd=wd: e.dma_start(
                        out=self.pool[:, s_ * wd:(s_ + 1) * wd],
                        in_=w[ks, cs]), inc=self.sem, by=16)
                    self.n_loaded += 1

        qa = LoadQ(wpa, "ldw", NA_SLOT)
        qb = LoadQ(wpb, "ldwb", NB_SLOT, width=256)

        def load_w_in(_c):
            for mg in range(4):
                for k in range(8):
                    qa.add(w_in, slice(k * 128, (k + 1) * 128),
                           slice(mg * 512, (mg + 1) * 512))
            qa.pump()

        def load_w1(_c):
            for mg in range(4):
                for k in range(8):
                    qa.add(w1, slice(k * 128, (k + 1) * 128),
                           slice(mg * 512, (mg + 1) * 512))
            qa.pump()

        def load_w2_wout(_c):
            for k in range(16):
                for q in range(4):
                    qb.add(w2, slice(k * 128, (k + 1) * 128),
                           slice(q * 256, (q + 1) * 256))
            for k in range(8):
                for q in range(4):
                    qb.add(w_out, slice(k * 128, (k + 1) * 128),
                           slice(q * 256, (q + 1) * 256))
            qb.pump()

        class XQ:
            def __init__(self):
                self.pending = []
                self.n_loaded = 0
                self.cons = []

            def add(self, c, ts):
                self.pending.append((c, ts))

            def record(self, mm_val):
                self.cons.append(mm_val)
                self.pump()

            def pump(self):
                while self.pending:
                    if self.n_loaded - 2 >= len(self.cons):
                        break
                    if self.n_loaded >= 2:
                        em.wait("sync", "mm", self.cons[self.n_loaded - 2])
                    if self.n_loaded >= 1:
                        em.wait("sync", "ldx", self.n_loaded * 16)
                    c_, ts_ = self.pending.pop(0)
                    sl_ = self.n_loaded % 2
                    em.op("sync", lambda e, c_=c_, ts_=ts_, sl_=sl_: e.dma_start(
                        out=tok[:, sl_ * D_MODEL:(sl_ + 1) * D_MODEL],
                        in_=x_hbm[c_ * Tc + ts_ * 128: c_ * Tc + (ts_ + 1) * 128, :]),
                        inc="ldx", by=16)
                    self.n_loaded += 1

        xq = XQ()

        def load_x(c):
            for ts_ in range(NTS):
                xq.add(c, ts_)
            xq.pump()

        load_x(0)
        load_w_in(0)
        em.barrier()

        st_marks = []
        cc_op_marks = {}

        def emit_p9(cp):
            em.wait("sync", "cc", cc_op_marks[cp])
            em.op("sync", lambda e, cp=cp: e.dma_start(
                out=dt_sb[:, 0:NDM * Tc], in_=op_rcv.ap()[cp, :, :]), inc="io", by=16)
            io_os = em.cnt["io"]
            em.wait("tensor", "io", io_os)
            for ts_ in range(NTS):
                sl_ = ts_ % 2
                if ts_ >= 1:
                    em.wait("tensor", "ev", em.cnt["ev"])
                for dm in range(NDM):
                    bk = (pY[2], pY[3])[dm // 4]
                    em.op("tensor", lambda e, bk=bk, ts_=ts_, dm=dm, cp=cp: e.matmul(
                        bk[:, (dm % 4) * 128:(dm % 4 + 1) * 128],
                        dt_sb[:, dm * Tc + ts_ * 128: dm * Tc + (ts_ + 1) * 128],
                        identT[:], start=(dm % 4 == 0), stop=(dm % 4 == 3),
                        is_transpose=True),
                        inc=("mm" if dm == NDM - 1 else None))
                mmv = em.cnt["mm"]
                em.wait("scalar", "mm", mmv)
                if len(st_marks) >= 2:
                    em.wait("scalar", "st", st_marks[-2])
                em.op("scalar", lambda e, sl_=sl_: e.copy(
                    tok[:, sl_ * D_MODEL: sl_ * D_MODEL + 512], pY[2][:, 0:512]),
                    inc="ev")
                em.op("scalar", lambda e, sl_=sl_: e.copy(
                    tok[:, sl_ * D_MODEL + 512: (sl_ + 1) * D_MODEL], pY[3][:, 0:512]),
                    inc="ev")
                ev_ts = em.cnt["ev"]
                em.wait("sync", "ev", ev_ts)
                if st_marks:
                    em.wait("sync", "st", st_marks[-1])
                em.op("sync", lambda e, cp=cp, ts_=ts_, sl_=sl_: e.dma_start(
                    out=out_hbm[cp * Tc + ts_ * 128: cp * Tc + (ts_ + 1) * 128, :],
                    in_=tok[:, sl_ * D_MODEL:(sl_ + 1) * D_MODEL]), inc="st", by=16)
                st_marks.append(em.cnt["st"])
            return em.cnt["mm"]

        for c in range(NCH):
            em.phase = "p1"
            for ts_ in range(NTS):
                gi = c * NTS + ts_
                sl_ = gi % 2
                bks = ((pA, pB), (pY[0], pY[1]))[ts_ % 2]
                em.wait("tensor", "ldx", (gi + 1) * 16)
                if ts_ >= 2:
                    em.wait("tensor", "ev", em.cnt["ev"])
                for dm in range(NDM):
                    bk = bks[dm // 4]
                    em.op("tensor", lambda e, bk=bk, sl_=sl_, dm=dm: e.matmul(
                        bk[:, (dm % 4) * 128:(dm % 4 + 1) * 128],
                        tok[:, sl_ * D_MODEL + dm * 128: sl_ * D_MODEL + (dm + 1) * 128],
                        identT[:], start=(dm % 4 == 0), stop=(dm % 4 == 3),
                        is_transpose=True),
                        inc=("mm" if dm == NDM - 1 else None))
                mmv = em.cnt["mm"]
                xq.record(mmv)
                em.wait("scalar", "mm", mmv)
                for dm in range(NDM):
                    bk = bks[dm // 4]
                    em.op("scalar", lambda e, bk=bk, dm=dm, ts_=ts_: e.copy(
                        scr_r[:, dm * Tc + ts_ * 128: dm * Tc + (ts_ + 1) * 128],
                        bk[:, (dm % 4) * 128:(dm % 4 + 1) * 128]), inc="ev")
            ev_xfm = em.cnt["ev"]
            em.wait("tensor", "ev", ev_xfm)
            for dm in range(NDM):
                em.op("tensor", lambda e, dm=dm: e.matmul(
                    pT[0:1, 0:Tc], onesR[:], scr_r[:, dm * Tc:(dm + 1) * Tc],
                    start=(dm == 0), stop=(dm == NDM - 1)),
                    inc=("mm" if dm == NDM - 1 else None))
            sq_mm = []
            for dm in range(NDM):
                em.wait("scalar", "ev", ev_xfm)
                if dm >= 1:
                    em.wait("scalar", "mm", sq_mm[dm - 1])
                em.op("scalar", lambda e, dm=dm: e.activation(
                    sq[:, 0:Tc],
                    scr_r[:, dm * Tc:(dm + 1) * Tc], AF.Square), inc="ev")
                em.drain("scalar")
                em.wait("tensor", "ev", em.cnt["ev"])
                em.op("tensor", lambda e, dm=dm: e.matmul(
                    pT[32:33, 0:Tc], onesT[:], sq[:, 0:Tc],
                    start=(dm == 0), stop=(dm == NDM - 1)), inc="mm")
                sq_mm.append(em.cnt["mm"])
            mm_st = em.cnt["mm"]

            def row_stats_bcast(mm_wait, psrc0, psrc1):
                em.wait("vector", "mm", mm_wait)
                em.op("vector", lambda e: e.tensor_scalar(
                    row1, psrc1, INV_D, None, ALU.mult), inc="dv")
                em.drain("vector")
                em.op("vector", lambda e: e.tensor_scalar(
                    bst[0:1, :], psrc0, INV_D, None, ALU.mult), inc="dv")
                em.drain("vector")
                dv_mu = em.cnt["dv"]
                em.wait("tensor", "dv", dv_mu)
                em.op("tensor", lambda e: e.matmul(
                    pA[:, 0:Tc], sel16[:, 0:128], bst[:, 0:Tc], start=True, stop=True), inc="mm")
                mm_mu = em.cnt["mm"]
                em.op("vector", lambda e: e.tensor_mul(
                    row2, bst[0:1, :], bst[0:1, :]), inc="dv")
                em.drain("vector")
                em.op("vector", lambda e: e.tensor_sub(row1, row1, row2), inc="dv")
                em.drain("vector")
                dvv = em.cnt["dv"]
                em.wait("scalar", "dv", dvv)
                em.op("scalar", lambda e: e.activation(row2, row1, AF.Ln, bias=EPS), inc="ev")
                em.drain("scalar")
                em.op("scalar", lambda e: e.activation(row2, row2, AF.Exp, scale=-0.5), inc="ev")
                evv = em.cnt["ev"]
                em.wait("vector", "ev", evv)
                em.wait("vector", "mm", mm_mu)
                em.op("vector", lambda e: e.tensor_copy(bst[0:1, :], row2), inc="dv")
                em.drain("vector")
                dv_r = em.cnt["dv"]
                em.wait("tensor", "dv", dv_r)
                em.op("tensor", lambda e: e.matmul(
                    pB[:, 0:Tc], sel16[:, 0:128], bst[:, 0:Tc], start=True, stop=True), inc="mm")
                return em.cnt["mm"]

            mm_bc1 = row_stats_bcast(mm_st, pT[0:1, 0:Tc], pT[32:33, 0:Tc])
            em.wait("vector", "mm", mm_bc1)
            for dm in range(NDM):
                em.op("vector", lambda e, dm=dm: e.tensor_sub(
                    cacc[:, 0:Tc], scr_r[:, dm * Tc:(dm + 1) * Tc], pA[:, 0:Tc]))
                em.drain("vector")
                em.op("vector", lambda e, dm=dm: e.tensor_mul(
                    xn[:, dm * Tc:(dm + 1) * Tc], cacc[:, 0:Tc], pB[:, 0:Tc]), inc="dv")
                em.drain("vector")
            dv_xn = em.cnt["dv"]
            em.wait("tensor", "dv", dv_xn)
            for dm in range(NDM):
                em.op("tensor", lambda e, dm=dm: e.matmul(
                    pS[0:1, 0:Tc], onesR[:], xn[:, dm * Tc:(dm + 1) * Tc],
                    start=(dm == 0), stop=(dm == NDM - 1)),
                    inc=("mm" if dm == NDM - 1 else None))
            sq_mm2 = []
            for dm in range(NDM):
                em.wait("scalar", "dv", dv_xn)
                if dm >= 1:
                    em.wait("scalar", "mm", sq_mm2[dm - 1])
                em.op("scalar", lambda e, dm=dm: e.activation(
                    sq[:, 0:Tc],
                    xn[:, dm * Tc:(dm + 1) * Tc], AF.Square), inc="ev")
                em.drain("scalar")
                em.wait("tensor", "ev", em.cnt["ev"])
                em.op("tensor", lambda e, dm=dm: e.matmul(
                    pS[32:33, 0:Tc], onesT[:], sq[:, 0:Tc],
                    start=(dm == 0), stop=(dm == NDM - 1)), inc="mm")
                sq_mm2.append(em.cnt["mm"])
            mm_st2 = em.cnt["mm"]
            mm_bc2 = row_stats_bcast(mm_st2, pS[0:1, 0:Tc], pS[32:33, 0:Tc])
            em.wait("vector", "mm", mm_bc2)
            for dm in range(NDM):
                em.op("vector", lambda e, dm=dm: e.tensor_sub(
                    cacc[:, 0:Tc], xn[:, dm * Tc:(dm + 1) * Tc], pA[:, 0:Tc]))
                em.drain("vector")
                em.op("vector", lambda e, dm=dm: e.tensor_mul(
                    sq[:, 0:Tc], cacc[:, 0:Tc], pB[:, 0:Tc]))
                em.drain("vector")
                em.op("vector", lambda e, dm=dm: e.tensor_scalar(
                    xn2[:, dm * Tc:(dm + 1) * Tc], sq[:, 0:Tc],
                    pc[:, dm:dm + 1], pc[:, 8 + dm:9 + dm], ALU.mult, ALU.add), inc="dv")
                em.drain("vector")
                em.op("vector", lambda e, dm=dm: e.tensor_add(
                    cacc[:, 0:Tc], scr_r[:, dm * Tc:(dm + 1) * Tc],
                    xn[:, dm * Tc:(dm + 1) * Tc]))
                em.drain("vector")
                em.op("vector", lambda e, dm=dm: e.tensor_scalar(
                    r1h[:, dm * Tc:(dm + 1) * Tc], cacc[:, 0:Tc],
                    0.5, pc[:, 48 + dm:49 + dm], ALU.mult, ALU.add), inc="dv")
                em.drain("vector")
            em.barrier()

            em.phase = "p2"
            base_a = c * 96
            for m in range(2 * NG):
                mg, mi = divmod(m, 4)
                pb = (pA, pB)[m % 2]
                em.wait("tensor", "ldw", (base_a + (mg + 1) * 8) * 16)
                if m >= 2:
                    em.wait("tensor", "ev", em.cnt["ev"])
                for k in range(8):
                    sl = (base_a + mg * 8 + k) % NA_SLOT
                    src = xn2
                    em.op("tensor", lambda e, pb=pb, sl=sl, mi=mi, k=k, src=src: e.matmul(
                        pb[:, 0:Tc],
                        wpa[:, sl * 512 + mi * 128: sl * 512 + (mi + 1) * 128],
                        src[:, k * Tc:(k + 1) * Tc],
                        start=(k == 0), stop=(k == 7)),
                        inc=("mm" if k == 7 else None))
                mmv = em.cnt["mm"]
                if m == 4 * mg + 3:
                    qa.record(mmv)
                    qa.record(mmv)
                em.wait("scalar", "mm", mmv)
                if m < NG:
                    g = m
                    hb = 0
                    em.wait("scalar", "dv", em.cnt["dv"]) if g >= 1 else None
                    em.op("scalar", lambda e, pb=pb, hb=hb: e.copy(
                        xh_rot[:, hb + 3:hb + 3 + Tc], pb[:, 0:Tc]), inc="ev")
                    ev_xh = em.cnt["ev"]
                    em.wait("vector", "ev", ev_xh)
                    em.op("vector", lambda e, hb=hb, g=g: e.tensor_copy(
                        xh_rot[:, hb:hb + 3], halo[:, g * 3:(g + 1) * 3]))
                    em.drain("vector")
                    em.op("vector", lambda e, hb=hb, g=g: e.tensor_scalar(
                        cacc[:, 0:Tc], xh_rot[:, hb:hb + Tc],
                        pc[:, 56 + g * 4:57 + g * 4], None, ALU.mult))
                    em.drain("vector")
                    for kk in range(1, D_CONV):
                        em.op("vector", lambda e, hb=hb, g=g, kk=kk: e.scalar_tensor_tensor(
                            cacc[:, 0:Tc], xh_rot[:, hb + kk:hb + kk + Tc],
                            pc[:, 56 + g * 4 + kk:57 + g * 4 + kk],
                            cacc[:, 0:Tc], ALU.mult, ALU.add))
                        em.drain("vector")
                    em.op("vector", lambda e, g=g: e.tensor_scalar(
                        sq[:, 0:Tc], cacc[:, 0:Tc], 1.0,
                        pc[:, 40 + g:41 + g], ALU.mult, ALU.add))
                    em.drain("vector")
                    em.op("vector", lambda e, hb=hb, g=g: e.tensor_copy(
                        halo[:, g * 3:(g + 1) * 3], xh_rot[:, hb + Tc:hb + Tc + 3]),
                        inc="dv")
                    em.drain("vector")
                    em.wait("scalar", "dv", em.cnt["dv"])
                    em.op("scalar", lambda e, g=g: e.activation(
                        xc[:, g * Tc:(g + 1) * Tc], sq[:, 0:Tc], AF.Silu),
                        inc="ev")
                else:
                    g = m - NG
                    em.op("scalar", lambda e, pb=pb, g=g: e.activation(
                        sz[:, g * Tc:(g + 1) * Tc], pb[:, 0:Tc], AF.Silu), inc="ev")
            ev_p2 = em.cnt["ev"]
            em.barrier()

            p9_mm = 0
            p9_ev = 0
            if c >= 1:
                em.phase = "p9"
                p9_mm = emit_p9(c - 1)
                p9_ev = em.cnt["ev"]
            em.phase = "p3"
            for g in range(NG):
                em.op("tensor", lambda e, g=g: e.matmul(
                    pT[0:65, 0:Tc], wxp_sb[:, g * 65:(g + 1) * 65],
                    xc[:, g * Tc:(g + 1) * Tc],
                    start=(g == 0), stop=(g == NG - 1)),
                    inc=("mm" if g == NG - 1 else None))
            mm_xpj = em.cnt["mm"]
            em.wait("scalar", "mm", mm_xpj)
            em.op("scalar", lambda e: e.copy(xpjp, pT[0:65, 0:Tc]), inc="ev")
            ev_xpj = em.cnt["ev"]
            em.wait("sync", "ev", ev_xpj)
            em.op("sync", lambda e, c=c: e.dma_start(
                out=xpj_snd.ap()[c, :, :], in_=xpjp), inc="io", by=16)
            io_snd = em.cnt["io"]
            em.wait("gpsimd", "io", io_snd)
            em.op("gpsimd", lambda e, c=c: e.collective_compute(
                "AllReduce", ALU.add, replica_groups=GROUPS,
                ins=[xpj_snd.ap()[c, :, :]], outs=[xpj_rcv.ap()[c, :, :]]), inc="cc")
            cc_xpj = em.cnt["cc"]
            load_w1(c)
            load_w1(c)
            load_w2_wout(c)
            load_w2_wout(c)
            em.wait("sync", "cc", cc_xpj)
            em.op("sync", lambda e, c=c: e.dma_start(
                out=xpjB[:, :], in_=xpj_rcv.ap()[c, 0:16, :]), inc="io", by=16)
            em.op("sync", lambda e, c=c: e.dma_start(
                out=xpjC[:, :], in_=xpj_rcv.ap()[c, 32:48, :]), inc="io", by=16)
            em.op("sync", lambda e, c=c: e.dma_start(
                out=bst[0:1, :], in_=xpj_rcv.ap()[c, 64:65, :]), inc="io", by=16)
            io_rcv = em.cnt["io"]
            em.wait("tensor", "io", io_rcv)
            em.op("tensor", lambda e: e.matmul(
                pT[:, 0:Tc], sel16[:, 0:128], bst[:, 0:Tc],
                start=True, stop=True), inc="mm")
            mm_xi = em.cnt["mm"]
            for g in range(NG):
                zs = g % 2
                em.wait("vector", "mm", max(mm_xi, p9_mm))
                em.op("vector", lambda e, g=g, zs=zs: e.tensor_scalar(
                    da_b[:, zs * Tc:(zs + 1) * Tc], pT[:, 0:Tc],
                    pc[:, 16 + g:17 + g], pc[:, 24 + g:25 + g],
                    ALU.mult, ALU.add), inc="dv")
                em.drain("vector")
                em.op("vector", lambda e, zs=zs: e.scalar_tensor_tensor(
                    dbx_b[:, zs * Tc:(zs + 1) * Tc], da_b[:, zs * Tc:(zs + 1) * Tc],
                    -1.0, da_b[:, zs * Tc:(zs + 1) * Tc], ALU.mult, ALU.max), inc="dv")
                em.drain("vector")
                em.wait("scalar", "dv", em.cnt["dv"])
                em.op("scalar", lambda e, zs=zs: e.activation(
                    dbx_b[:, zs * Tc:(zs + 1) * Tc], dbx_b[:, zs * Tc:(zs + 1) * Tc],
                    AF.Exp, scale=-1.0), inc="ev")
                em.drain("scalar")
                em.op("scalar", lambda e, zs=zs: e.activation(
                    dbx_b[:, zs * Tc:(zs + 1) * Tc], dbx_b[:, zs * Tc:(zs + 1) * Tc],
                    AF.Ln, bias=1.0), inc="ev")
                em.wait("vector", "ev", em.cnt["ev"])
                em.op("vector", lambda e, g=g, zs=zs: e.scalar_tensor_tensor(
                    dt_sb[:, g * Tc:(g + 1) * Tc], da_b[:, zs * Tc:(zs + 1) * Tc],
                    0.0, dbx_b[:, zs * Tc:(zs + 1) * Tc], ALU.max, ALU.add), inc="dv")
                em.drain("vector")
                em.op("vector", lambda e, g=g: e.tensor_mul(
                    xn2[:, g * Tc:(g + 1) * Tc], dt_sb[:, g * Tc:(g + 1) * Tc],
                    xc[:, g * Tc:(g + 1) * Tc]), inc="dv")
                em.drain("vector")
            em.barrier()

            em.phase = "p5"
            b_sc = em.cnt["sc"]
            b_yx = em.cnt["yx"]
            yac_done = lambda ip: b_yx + (ip - 2 * (1 + ip // 32)) + 1 \
                if False else None
            yac_mark = {}
            dvq_gate = 0
            ffn1_q = []
            for m in range(NM1):
                ffn1_q.append(m)
            ffn1_mm = {}
            ffs_marks = []
            a_base_fb0 = base_a + 32

            def emit_ffn1_m(m, fb, dst, a_base):
                mg, mi = divmod(m, 4)
                pb = (pY[2], pY[3])[m % 2]
                em.wait("tensor", "ldw", (a_base + (mg + 1) * 8) * 16)
                if fb == 0 and m < 2 and p9_ev:
                    em.wait("tensor", "ev", p9_ev)
                if len(ffs_marks) >= 2:
                    em.wait("tensor", "ffs", ffs_marks[-2])
                for k in range(8):
                    sl = (a_base + mg * 8 + k) % NA_SLOT
                    em.op("tensor", lambda e, pb=pb, sl=sl, mi=mi, k=k, fb=fb: e.matmul(
                        pb[:, 0:HB],
                        wpa[:, sl * 512 + mi * 128: sl * 512 + (mi + 1) * 128],
                        xn[:, k * Tc + fb * HB: k * Tc + fb * HB + HB],
                        start=(k == 0), stop=(k == 7)),
                        inc=("mm" if k == 7 else None))
                mmv = em.cnt["mm"]
                if m == 4 * mg + 3:
                    qa.record(mmv)
                    qa.record(mmv)
                em.wait("scalar", "mm", mmv)
                em.op("scalar", lambda e, pb=pb, m=m, dst=dst: e.activation(
                    dst[:, m * HB:(m + 1) * HB], pb[:, 0:HB], AF.Relu,
                    bias=pc[:, 88 + m:89 + m]), inc="ffs")
                ffs_marks.append(em.cnt["ffs"])

            for hf in range(2):
                for qp in range(2):
                    blk = hf * 2 + qp
                    for n in range(D_STATE):
                        i0 = blk * 32 + n * 2
                        if i0 > 0:
                            em.wait("tensor", "sc", b_sc + i0)
                        em.op("tensor", lambda e, n=n: e.matmul(
                            pA[:, 0:Tc], sel16[:, n * 128:(n + 1) * 128],
                            xpjB[:, :], start=True, stop=True), inc="bc")
                        pCb = (pB, pT)[n % 2]
                        if i0 >= 4:
                            em.wait("tensor", "sc", b_sc + i0 - 2)
                        em.op("tensor", lambda e, n=n, pCb=pCb: e.matmul(
                            pCb[:, 0:Tc], sel16[:, n * 128:(n + 1) * 128],
                            xpjC[:, :], start=True, stop=True), inc="bc")
                        bcv = em.cnt["bc"]
                        if n >= 1:
                            if n == 1 and dvq_gate:
                                em.wait("tensor", "dv", dvq_gate)
                            for g2 in range(2):
                                ip = blk * 32 + (n - 1) * 2 + g2
                                em.wait("tensor", "sc", b_sc + ip + 1)
                                em.op("tensor", lambda e, g2=g2, ip=ip, nn=n - 1: e.matmul(
                                    pY[g2][:, 0:Tc], identR[:],
                                    hc_b[:, (ip % 4) * Tc:(ip % 4 + 1) * Tc],
                                    start=(nn == 0), stop=(nn == D_STATE - 1)), inc="yx")
                                yac_mark[ip] = em.cnt["yx"]
                        if n % 2 == 0 and ffn1_q:
                            emit_ffn1_m(ffn1_q.pop(0), 0, scr_r, a_base_fb0)
                        for g2 in range(2):
                            g = hf * 4 + qp * 2 + g2
                            i = blk * 32 + n * 2 + g2
                            sl = i % 3
                            s2 = i % 2
                            s4 = i % 4
                            acol = g * 16 + n
                            if i >= 3:
                                em.wait("scalar", "sc", b_sc + i - 2)
                            em.op("scalar", lambda e, g=g, acol=acol, sl=sl: e.activation(
                                da_b[:, sl * Tc:(sl + 1) * Tc],
                                dt_sb[:, g * Tc:(g + 1) * Tc], AF.Exp,
                                scale=aneg[:, acol:acol + 1]), inc="da")
                            dav = em.cnt["da"]
                            em.wait("vector", "bc", bcv)
                            em.op("vector", lambda e, g=g, s2=s2: e.tensor_mul(
                                dbx_b[:, s2 * Tc:(s2 + 1) * Tc],
                                xn2[:, g * Tc:(g + 1) * Tc], pA[:, 0:Tc]))
                            em.drain("vector")
                            em.wait("vector", "da", dav)
                            em.op("vector", lambda e, sl=sl, s2=s2, i=i: e.tensor_tensor_scan(
                                h32_b[:, s2 * Tc:(s2 + 1) * Tc],
                                da_b[:, sl * Tc:(sl + 1) * Tc],
                                dbx_b[:, s2 * Tc:(s2 + 1) * Tc],
                                state[:, i:i + 1], ALU.mult, ALU.add))
                            em.drain("vector")
                            if i >= 4:
                                em.wait("vector", "yx", yac_mark[i - 4])
                            em.op("vector", lambda e, s2=s2, s4=s4, pCb=pCb: e.tensor_mul(
                                hc_b[:, s4 * Tc:(s4 + 1) * Tc],
                                h32_b[:, s2 * Tc:(s2 + 1) * Tc], pCb[:, 0:Tc]))
                            em.drain("vector")
                            em.op("vector", lambda e, s2=s2, i=i: e.tensor_copy(
                                state[:, i:i + 1],
                                h32_b[:, s2 * Tc + Tc - 1: s2 * Tc + Tc]), inc="sc")
                            em.drain("vector")
                    for g2 in range(2):
                        ip = blk * 32 + 15 * 2 + g2
                        em.wait("tensor", "sc", b_sc + ip + 1)
                        em.op("tensor", lambda e, g2=g2, ip=ip: e.matmul(
                            pY[g2][:, 0:Tc], identR[:],
                            hc_b[:, (ip % 4) * Tc:(ip % 4 + 1) * Tc],
                            start=False, stop=True), inc="yx")
                        yac_mark[ip] = em.cnt["yx"]
                    yx_blk = em.cnt["yx"]
                    for g2 in range(2):
                        g = hf * 4 + qp * 2 + g2
                        em.wait("vector", "yx", yx_blk)
                        em.op("vector", lambda e, g=g: e.tensor_scalar(
                            cacc[:, 0:Tc], xc[:, g * Tc:(g + 1) * Tc],
                            pc[:, 32 + g:33 + g], None, ALU.mult))
                        em.drain("vector")
                        em.op("vector", lambda e, g=g, g2=g2: e.scalar_tensor_tensor(
                            sq[:, 0:Tc], cacc[:, 0:Tc], 1.0,
                            pY[g2][:, 0:Tc], ALU.mult, ALU.add))
                        em.drain("vector")
                        em.op("vector", lambda e, g=g: e.tensor_mul(
                            ysz[:, g * Tc:(g + 1) * Tc], sq[:, 0:Tc],
                            sz[:, g * Tc:(g + 1) * Tc]), inc="dv")
                        em.drain("vector")
                    dvq_gate = em.cnt["dv"]
            em.barrier()

            em.phase = "p7"
            base_b = c * 192
            obank = (pA, pB, pT, pS)

            def emit_out_pass(fb, hsrc, ffs_tgt):
                bb = base_b + fb * 96
                em.wait("tensor", "ffs", ffs_tgt)
                for k in range(16):
                    so = bb + 4 * k
                    em.wait("tensor", "ldwb", (so + 4) * 16)
                    for dm in range(NDM):
                        bank = obank[dm % 4]
                        colo = 0 if dm < 4 else HB
                        sl = (so + dm // 2) % NB_SLOT
                        em.op("tensor", lambda e, k=k, dm=dm, bank=bank, colo=colo, sl=sl, hsrc=hsrc: e.matmul(
                            bank[:, colo:colo + HB],
                            wpb[:, sl * 256 + (dm % 2) * 128:sl * 256 + (dm % 2 + 1) * 128],
                            hsrc[:, k * HB:(k + 1) * HB],
                            start=(k == 0 and dm < 4), stop=False),
                            inc=("mm" if dm == NDM - 1 else None))
                    qb.record(em.cnt["mm"])
                for k in range(8):
                    so = bb + 64 + 4 * k
                    em.wait("tensor", "ldwb", (so + 4) * 16)
                    for dm in range(NDM):
                        bank = obank[dm % 4]
                        colo = 0 if dm < 4 else HB
                        sl = (so + dm // 2) % NB_SLOT
                        em.op("tensor", lambda e, k=k, dm=dm, bank=bank, colo=colo, sl=sl, fb=fb: e.matmul(
                            bank[:, colo:colo + HB],
                            wpb[:, sl * 256 + (dm % 2) * 128:sl * 256 + (dm % 2 + 1) * 128],
                            ysz[:, k * Tc + fb * HB: k * Tc + fb * HB + HB],
                            start=False, stop=(k == 7 and dm >= 4)),
                            inc=("mm" if dm == NDM - 1 else None))
                    qb.record(em.cnt["mm"])
                mm_out = em.cnt["mm"]
                em.wait("vector", "mm", mm_out)
                for dm in range(NDM):
                    bank = obank[dm % 4]
                    colo = 0 if dm < 4 else HB
                    em.op("vector", lambda e, dm=dm, fb=fb, bank=bank, colo=colo: e.scalar_tensor_tensor(
                        dt_sb[:, dm * Tc + fb * HB: dm * Tc + fb * HB + HB],
                        r1h[:, dm * Tc + fb * HB: dm * Tc + fb * HB + HB],
                        1.0, bank[:, colo:colo + HB], ALU.mult, ALU.add), inc="dv")
                    em.drain("vector")
                return em.cnt["dv"]

            ffs_fb0 = em.cnt["ffs"]
            a_base_fb1 = base_a + 64
            for m in range(NM1):
                emit_ffn1_m(m, 1, xn2, a_base_fb1)
            ffs_fb1 = em.cnt["ffs"]
            dv0 = emit_out_pass(0, scr_r, ffs_fb0)
            em.wait("tensor", "dv", dv0)
            dv1 = emit_out_pass(1, xn2, ffs_fb1)
            em.wait("sync", "dv", dv1)
            em.op("sync", lambda e, c=c: e.dma_start(
                out=op_snd.ap()[c, :, :], in_=dt_sb[:, 0:NDM * Tc]), inc="io", by=16)
            io_op = em.cnt["io"]
            em.wait("gpsimd", "io", io_op)
            em.op("gpsimd", lambda e, c=c: e.collective_compute(
                "AllReduce", ALU.add, replica_groups=GROUPS,
                ins=[op_snd.ap()[c, :, :]], outs=[op_rcv.ap()[c, :, :]]), inc="cc")
            cc_op_marks[c] = em.cnt["cc"]
            em.barrier()
            if c == NCH - 1:
                emit_p9(c)
            if c + 1 < NCH:
                em.wait("sync", "st", em.cnt["st"])
                load_x(c + 1)
                load_w_in(c + 1)
            em.barrier()

        em.wait("sync", "st", em.cnt["st"])

        with nc.Block() as blk:
            @blk.sync
            def _(eng):
                for fn in em.streams["sync"]:
                    fn(eng)

            @blk.gpsimd
            def _(eng):
                for fn in em.streams["gpsimd"]:
                    fn(eng)

            @blk.tensor
            def _(eng):
                for fn in em.streams["tensor"]:
                    fn(eng)

            @blk.vector
            def _(eng):
                for fn in em.streams["vector"]:
                    fn(eng)

            @blk.scalar
            def _(eng):
                for fn in em.streams["scalar"]:
                    fn(eng)

    return nc


NSCAN = 128


def kernel(**inputs):
    x = np.ascontiguousarray(np.asarray(inputs["L0_attn_x"], dtype=np.float32))
    B, L, _ = x.shape
    w_in_full = np.asarray(inputs["L0_attn_W_in"], dtype=np.float32)
    conv_w = np.asarray(inputs["L0_attn_conv_w"], dtype=np.float32)
    conv_b = np.asarray(inputs["L0_attn_conv_b"], dtype=np.float32)
    w_xp_full = np.asarray(inputs["L0_attn_W_xp"], dtype=np.float32)
    w_dt = np.asarray(inputs["L0_attn_W_dt"], dtype=np.float32).reshape(-1)
    b_dt = np.asarray(inputs["L0_attn_b_dt"], dtype=np.float32)
    a_log = np.asarray(inputs["L0_attn_A_log"], dtype=np.float32)
    d_par = np.asarray(inputs["L0_attn_D"], dtype=np.float32)
    w_out_full = np.asarray(inputs["L0_attn_W_out"], dtype=np.float32)
    ln_g = np.asarray(inputs["L0_attn_ln_g"], dtype=np.float32)
    ln_b = np.asarray(inputs["L0_attn_ln_b"], dtype=np.float32)
    w1_full = np.asarray(inputs["L0_ffn_W1"], dtype=np.float32)
    b1 = np.asarray(inputs["L0_ffn_b1"], dtype=np.float32)
    w2_full = np.asarray(inputs["L0_ffn_W2"], dtype=np.float32)
    b2 = np.asarray(inputs["L0_ffn_b2"], dtype=np.float32)

    nc = build_program(L)

    ident = np.eye(128, dtype=np.float32)
    sel16 = np.zeros((16, 2048), dtype=np.float32)
    for n in range(16):
        sel16[n, n * 128:(n + 1) * 128] = 1.0

    in_maps = []
    for r in range(NCORES):
        b = r // 2
        H = r % 2
        ch = slice(H * DH, (H + 1) * DH)
        ih = slice(H * IH, (H + 1) * IH)
        w_in_c = np.concatenate(
            [w_in_full[:, H * DH:(H + 1) * DH],
             w_in_full[:, 2 * DH + H * DH: 2 * DH + (H + 1) * DH]], axis=1)
        pco = np.zeros((128, 104), dtype=np.float32)
        pco[:, 0:8] = ln_g.reshape(8, 128).T
        pco[:, 8:16] = ln_b.reshape(8, 128).T
        pco[:, 16:24] = w_dt[ch].reshape(8, 128).T
        pco[:, 24:32] = b_dt[ch].reshape(8, 128).T
        pco[:, 32:40] = d_par[ch].reshape(8, 128).T
        pco[:, 40:48] = conv_b[ch].reshape(8, 128).T
        pco[:, 48:56] = (0.5 * b2).reshape(8, 128).T
        cw = conv_w[ch].reshape(8, 128, D_CONV)
        for g in range(8):
            pco[:, 56 + g * 4:56 + (g + 1) * 4] = cw[g]
        pco[:, 88:104] = b1[ih].reshape(16, 128).T
        alog_c = np.zeros((128, 128), dtype=np.float32)
        al = a_log[ch].reshape(8, 128, D_STATE)
        for g in range(8):
            alog_c[:, g * 16:(g + 1) * 16] = al[g]
        wxp_pad = np.zeros((DH, 65), dtype=np.float32)
        wxp_pad[:, 0:16] = w_xp_full[ch, 0:16]
        wxp_pad[:, 32:48] = w_xp_full[ch, 16:32]
        wxp_pad[:, 64] = w_xp_full[ch, 32]
        in_maps.append({
            "x": np.ascontiguousarray(x[b]),
            "w_in": np.ascontiguousarray(w_in_c),
            "w1": np.ascontiguousarray(w1_full[:, ih]),
            "w2": np.ascontiguousarray(w2_full[ih, :]),
            "w_out": np.ascontiguousarray(w_out_full[ch, :]),
            "w_xp": wxp_pad,
            "pcons": pco,
            "a_log": alog_c,
            "identc": ident,
            "identrc": ident,
            "sel16c": sel16,
            "onesc": np.ones((128, 1), dtype=np.float32),
            "onesrc": np.ones((128, 1), dtype=np.float32),
        })

    res = run_bass_kernel_spmd(nc, in_maps, core_ids=list(range(NCORES)))
    out = np.empty((B, L, D_MODEL), dtype=np.float32)
    for b in range(B):
        out[b] = res.results[2 * b]["out"]
    return out


# version 13
# speedup vs baseline: 1.0873x; 1.0873x over previous
import os
import numpy as np
from contextlib import ExitStack

EMIT_DRAINS = os.environ.get("BASS_SIM_FP64") == "1" or os.environ.get("KERNEL_SIM_DRAINS") == "1"
HW_DRAINS = set(os.environ.get("KERNEL_HW_DRAINS", "p1").split(","))

import concourse.bass as bass
import concourse.mybir as mybir
from concourse.bass_utils import run_bass_kernel_spmd

F32 = mybir.dt.float32
F32R = mybir.dt.float32r
ALU = mybir.AluOpType
AF = mybir.ActivationFunctionType

D_MODEL = 1024
D_STATE = 16
D_CONV = 4
DH = 1024
IH = 2048
NCORES = 8
EPS = 1e-5
INV_D = 1.0 / D_MODEL

ENGS = ("sync", "gpsimd", "tensor", "vector", "scalar")


class Emitter:
    def __init__(self, nc, es):
        self.nc = nc
        self.streams = {e: [] for e in ENGS}
        self.cnt = {}
        self.sems = {}
        self.nbar = 0
        self.phase = "pro"
        self._es = es

    def sem(self, name):
        self.sems[name] = self._es.enter_context(self.nc.semaphore(name))
        self.cnt[name] = 0

    def op(self, eng, fn, inc=None, by=1):
        if inc is None:
            self.streams[eng].append(fn)
            return None
        s = self.sems[inc]
        self.cnt[inc] += by
        def wrapped(e):
            fn(e).then_inc(s, by)
        self.streams[eng].append(wrapped)
        return self.cnt[inc]

    def wait(self, eng, name, val, force=False):
        if val <= 0 and not force:
            return
        s = self.sems[name]
        self.streams[eng].append(lambda e: e.wait_ge(s, max(val, 0)))

    BAR_ENGS = ("gpsimd", "tensor", "vector", "scalar")

    def barrier(self):
        self.nbar += 1
        tgt = self.nbar * len(self.BAR_ENGS)
        s = self.sems["bar"]
        for eng in self.BAR_ENGS:
            self.streams[eng].append(lambda e: e.nop().then_inc(s, 1))
            self.streams[eng].append(lambda e: e.wait_ge(s, tgt))

    def drain(self, eng, hw=False):
        if EMIT_DRAINS or eng == "scalar" or hw or "all" in HW_DRAINS \
                or self.phase in HW_DRAINS:
            self.streams[eng].append(lambda e: e.drain())


def build_program(L):
    Tc = min(512, L)
    NCH = L // Tc
    HB = Tc // 2
    NG = DH // 128
    NDM = D_MODEL // 128
    NM1 = IH // 128
    NTS = Tc // 128

    NA_SLOT = 8
    NB_SLOT = 8

    nc = bass.Bass("TRN2", target_bir_lowering=False)

    x_hbm = nc.dram_tensor("x", [L, D_MODEL], F32, kind="ExternalInput").ap()
    w_in = nc.dram_tensor("w_in", [D_MODEL, 2 * DH], F32R, kind="ExternalInput").ap()
    w1 = nc.dram_tensor("w1", [D_MODEL, IH], F32R, kind="ExternalInput").ap()
    w2 = nc.dram_tensor("w2", [IH, D_MODEL], F32R, kind="ExternalInput").ap()
    w_out = nc.dram_tensor("w_out", [DH, D_MODEL], F32R, kind="ExternalInput").ap()
    w_xp = nc.dram_tensor("w_xp", [DH, 65], F32R, kind="ExternalInput").ap()
    pcons = nc.dram_tensor("pcons", [128, 104], F32, kind="ExternalInput").ap()
    a_log = nc.dram_tensor("a_log", [128, 128], F32, kind="ExternalInput").ap()
    ident_in = nc.dram_tensor("identc", [128, 128], F32, kind="ExternalInput").ap()
    sel16_in = nc.dram_tensor("sel16c", [16, 2048], F32R, kind="ExternalInput").ap()
    identr_in = nc.dram_tensor("identrc", [128, 128], F32R, kind="ExternalInput").ap()
    onesr_in = nc.dram_tensor("onesrc", [128, 1], F32R, kind="ExternalInput").ap()
    ones_in = nc.dram_tensor("onesc", [128, 1], F32, kind="ExternalInput").ap()
    out_hbm = nc.dram_tensor("out", [L, D_MODEL], F32, kind="ExternalOutput").ap()

    xpj_snd = nc.dram_tensor("xpj_snd", [NCH, 65, Tc], F32R)
    xpj_rcv = nc.dram_tensor("xpj_rcv", [NCH, 65, Tc], F32R)
    op_snd = nc.dram_tensor("op_snd", [NCH, 128, NDM * Tc], F32)
    op_rcv = nc.dram_tensor("op_rcv", [NCH, 128, NDM * Tc], F32)
    GROUPS = [[0, 1], [2, 3], [4, 5], [6, 7]]

    es = ExitStack()
    sb = lambda nm, shp, dt=F32: es.enter_context(nc.sbuf_tensor(nm, shp, dt))
    ps = lambda nm, shp: es.enter_context(nc.psum_tensor(nm, shp, F32))

    with es:
        pc = sb("pc", [128, 104])
        aneg = sb("aneg", [128, 128])
        identT = sb("identT", [128, 128])
        identR = sb("identR", [128, 128], F32R)
        sel16 = sb("sel16", [16, 2048], F32R)
        onesT = sb("onesT", [128, 1])
        onesR = sb("onesR", [128, 1], F32R)
        wxp_sb = sb("wxp_sb", [128, 8 * 65], F32R)

        wpa = sb("wpa", [128, NA_SLOT * 512], F32R)
        wpb = sb("wpb", [128, NB_SLOT * 256], F32R)

        tok = sb("tok", [128, 2 * D_MODEL])
        scr_r = sb("scr_r", [128, NDM * Tc], F32R)
        xn = sb("xn", [128, NDM * Tc], F32R)
        xn2 = sb("xn2", [128, NG * Tc], F32R)
        r1h = sb("r1h", [128, NDM * Tc])
        sq = sb("sq", [128, Tc])

        xh_rot = sb("xh_rot", [128, Tc + 3])
        halo = sb("halo", [128, NG * 3])
        cacc = sb("cacc", [128, Tc])
        row1 = cacc[0:1, 0:Tc]
        row2 = sq[0:1, 0:Tc]
        xc = sb("xc", [128, NG * Tc], F32R)
        sz = sb("sz", [128, NG * Tc])
        dt_sb = sb("dt_sb", [128, NG * Tc])
        xpjB = sb("xpjB", [16, Tc], F32R)
        xpjC = sb("xpjC", [16, Tc], F32R)

        da_b = sb("da_b", [128, 3 * Tc])
        dbx_b = sb("dbx_b", [128, 2 * Tc])
        h32_b = sb("h32_b", [128, 2 * Tc])
        hc_b = sb("hc_b", [128, 4 * Tc], F32R)
        state = sb("state", [128, 128])

        cst = sb("cst", [128, 4])
        ysz = sb("ysz", [128, NG * Tc], F32R)
        bst = ysz[0:16, 0:Tc]
        xpjp = ysz[0:65, Tc:2 * Tc]

        pA = ps("pA", [128, 512])
        pB = ps("pB", [128, 512])
        pT = ps("pT", [128, 512])
        pS = ps("pS", [128, 512])
        pY = [ps(f"pY{i}", [128, 512]) for i in range(4)]

        nc.const_aps.register(F32, EPS, cst[:, 0:1])
        nc.const_aps.register(F32, 1.0, cst[:, 1:2])
        nc.const_aps.register(F32, -0.5, cst[:, 2:3])
        nc.const_aps.register(F32, -1.0, cst[:, 3:4])

        em = Emitter(nc, es)
        for s in ("bar", "ldw", "ldwb", "ldx", "lda", "mm", "ev", "ev2", "dv", "da",
                  "bc", "sc", "hc", "dbxs", "yx", "cc", "io", "st", "ffs"):
            em.sem(s)

        for t, src in ((pc, pcons), (aneg, a_log), (identT, ident_in),
                       (identR, identr_in), (sel16, sel16_in)):
            em.op("sync", lambda e, t=t, src=src: e.dma_start(out=t[:], in_=src),
                  inc="lda", by=16)
        em.op("sync", lambda e: e.dma_start(out=onesT[:], in_=ones_in),
              inc="lda", by=16)
        em.op("sync", lambda e: e.dma_start(out=onesR[:], in_=onesr_in[:, 0:1]),
              inc="lda", by=16)
        em.op("sync", lambda e: e.dma_start(
            out=wxp_sb[:, 0:8 * 65].rearrange("p (g c) -> p g c", c=65),
            in_=w_xp.rearrange("(g p) c -> p g c", p=128)), inc="lda", by=16)
        v_lda = em.cnt["lda"]
        em.wait("scalar", "lda", v_lda)
        em.op("scalar", lambda e: e.activation(aneg[:], aneg[:], AF.Exp), inc="ev")
        ev0 = em.cnt["ev"]
        em.wait("vector", "ev", ev0)
        em.op("vector", lambda e: e.tensor_scalar(aneg[:], aneg[:], -1.0, None, ALU.mult), inc="dv")
        em.drain("vector")
        em.op("vector", lambda e: e.memset(state[:], 0.0), inc="dv")
        em.drain("vector")
        em.op("vector", lambda e: e.memset(halo[:], 0.0), inc="dv")
        em.drain("vector")
        em.op("vector", lambda e: e.memset(cacc[:], 0.0), inc="dv")
        em.drain("vector")
        em.op("vector", lambda e: e.tensor_copy(ysz[0:16, 0:Tc], cacc[0:16, 0:Tc]), inc="dv")
        em.drain("vector")
        em.op("vector", lambda e: e.memset(cst[:, 0:1], EPS), inc="dv")
        em.op("vector", lambda e: e.memset(cst[:, 1:2], 1.0), inc="dv")
        em.op("vector", lambda e: e.memset(cst[:, 2:3], -0.5), inc="dv")
        em.op("vector", lambda e: e.memset(cst[:, 3:4], -1.0), inc="dv")
        em.drain("vector")
        em.wait("scalar", "dv", em.cnt["dv"])
        em.wait("tensor", "lda", v_lda)
        em.wait("tensor", "dv", em.cnt["dv"])

        class LoadQ:
            def __init__(self, pool, sem_name, nslot, width=512):
                self.pool = pool
                self.sem = sem_name
                self.width = width
                self.ring = nslot // 4
                self.pending = []
                self.n_loaded = 0
                self.cons = []

            def add(self, w, kslice, cslice):
                self.pending.append((w, kslice, cslice))

            def record(self, mm_val):
                self.cons.append(mm_val)
                self.pump()

            def pump(self):
                while self.pending:
                    g_ = self.n_loaded // 4
                    if g_ - self.ring >= len(self.cons):
                        break
                    if self.n_loaded % 4 == 0:
                        if g_ >= self.ring:
                            em.wait("sync", "mm", self.cons[g_ - self.ring])
                        if g_ >= 1:
                            em.wait("sync", self.sem, self.n_loaded * 16)
                    w, ks, cs = self.pending.pop(0)
                    s_ = self.n_loaded % (self.ring * 4)
                    wd = self.width
                    em.op("sync", lambda e, s_=s_, w=w, ks=ks, cs=cs, wd=wd: e.dma_start(
                        out=self.pool[:, s_ * wd:(s_ + 1) * wd],
                        in_=w[ks, cs]), inc=self.sem, by=16)
                    self.n_loaded += 1

        qa = LoadQ(wpa, "ldw", NA_SLOT)
        qb = LoadQ(wpb, "ldwb", NB_SLOT, width=256)

        def load_w_in(_c):
            for mg in range(4):
                for k in range(8):
                    qa.add(w_in, slice(k * 128, (k + 1) * 128),
                           slice(mg * 512, (mg + 1) * 512))
            qa.pump()

        def load_w1(_c):
            for mg in range(4):
                for k in range(8):
                    qa.add(w1, slice(k * 128, (k + 1) * 128),
                           slice(mg * 512, (mg + 1) * 512))
            qa.pump()

        def load_w2_wout(_c):
            for k in range(16):
                for q in range(4):
                    qb.add(w2, slice(k * 128, (k + 1) * 128),
                           slice(q * 256, (q + 1) * 256))
            for k in range(8):
                for q in range(4):
                    qb.add(w_out, slice(k * 128, (k + 1) * 128),
                           slice(q * 256, (q + 1) * 256))
            qb.pump()

        class XQ:
            def __init__(self):
                self.pending = []
                self.n_loaded = 0
                self.cons = []

            def add(self, c, ts):
                self.pending.append((c, ts))

            def record(self, mm_val):
                self.cons.append(mm_val)
                self.pump()

            def pump(self):
                while self.pending:
                    if self.n_loaded - 2 >= len(self.cons):
                        break
                    if self.n_loaded >= 2:
                        em.wait("sync", "mm", self.cons[self.n_loaded - 2])
                    if self.n_loaded >= 1:
                        em.wait("sync", "ldx", self.n_loaded * 16)
                    c_, ts_ = self.pending.pop(0)
                    sl_ = self.n_loaded % 2
                    em.op("sync", lambda e, c_=c_, ts_=ts_, sl_=sl_: e.dma_start(
                        out=tok[:, sl_ * D_MODEL:(sl_ + 1) * D_MODEL],
                        in_=x_hbm[c_ * Tc + ts_ * 128: c_ * Tc + (ts_ + 1) * 128, :]),
                        inc="ldx", by=16)
                    self.n_loaded += 1

        xq = XQ()

        def load_x(c):
            for ts_ in range(NTS):
                xq.add(c, ts_)
            xq.pump()

        load_x(0)
        load_w_in(0)
        em.barrier()

        st_marks = []
        cc_op_marks = {}

        def emit_p9(cp):
            em.wait("sync", "cc", cc_op_marks[cp])
            em.op("sync", lambda e, cp=cp: e.dma_start(
                out=dt_sb[:, 0:NDM * Tc], in_=op_rcv.ap()[cp, :, :]), inc="io", by=16)
            io_os = em.cnt["io"]
            em.wait("tensor", "io", io_os)
            for ts_ in range(NTS):
                sl_ = ts_ % 2
                if ts_ >= 1:
                    em.wait("tensor", "ev", em.cnt["ev"])
                for dm in range(NDM):
                    bk = (pY[2], pY[3])[dm // 4]
                    em.op("tensor", lambda e, bk=bk, ts_=ts_, dm=dm, cp=cp: e.matmul(
                        bk[:, (dm % 4) * 128:(dm % 4 + 1) * 128],
                        dt_sb[:, dm * Tc + ts_ * 128: dm * Tc + (ts_ + 1) * 128],
                        identT[:], start=(dm % 4 == 0), stop=(dm % 4 == 3),
                        is_transpose=True),
                        inc=("mm" if dm == NDM - 1 else None))
                mmv = em.cnt["mm"]
                em.wait("scalar", "mm", mmv)
                if len(st_marks) >= 2:
                    em.wait("scalar", "st", st_marks[-2])
                em.op("scalar", lambda e, sl_=sl_: e.copy(
                    tok[:, sl_ * D_MODEL: sl_ * D_MODEL + 512], pY[2][:, 0:512]),
                    inc="ev")
                em.op("scalar", lambda e, sl_=sl_: e.copy(
                    tok[:, sl_ * D_MODEL + 512: (sl_ + 1) * D_MODEL], pY[3][:, 0:512]),
                    inc="ev")
                ev_ts = em.cnt["ev"]
                em.wait("sync", "ev", ev_ts)
                if st_marks:
                    em.wait("sync", "st", st_marks[-1])
                em.op("sync", lambda e, cp=cp, ts_=ts_, sl_=sl_: e.dma_start(
                    out=out_hbm[cp * Tc + ts_ * 128: cp * Tc + (ts_ + 1) * 128, :],
                    in_=tok[:, sl_ * D_MODEL:(sl_ + 1) * D_MODEL]), inc="st", by=16)
                st_marks.append(em.cnt["st"])
            return em.cnt["mm"]

        for c in range(NCH):
            em.phase = "p1"
            for ts_ in range(NTS):
                gi = c * NTS + ts_
                sl_ = gi % 2
                bks = ((pA, pB), (pY[0], pY[1]))[ts_ % 2]
                em.wait("tensor", "ldx", (gi + 1) * 16)
                if ts_ >= 2:
                    em.wait("tensor", "ev", em.cnt["ev"])
                for dm in range(NDM):
                    bk = bks[dm // 4]
                    em.op("tensor", lambda e, bk=bk, sl_=sl_, dm=dm: e.matmul(
                        bk[:, (dm % 4) * 128:(dm % 4 + 1) * 128],
                        tok[:, sl_ * D_MODEL + dm * 128: sl_ * D_MODEL + (dm + 1) * 128],
                        identT[:], start=(dm % 4 == 0), stop=(dm % 4 == 3),
                        is_transpose=True),
                        inc=("mm" if dm == NDM - 1 else None))
                mmv = em.cnt["mm"]
                xq.record(mmv)
                em.wait("scalar", "mm", mmv)
                for dm in range(NDM):
                    bk = bks[dm // 4]
                    em.op("scalar", lambda e, bk=bk, dm=dm, ts_=ts_: e.copy(
                        scr_r[:, dm * Tc + ts_ * 128: dm * Tc + (ts_ + 1) * 128],
                        bk[:, (dm % 4) * 128:(dm % 4 + 1) * 128]), inc="ev")
            ev_xfm = em.cnt["ev"]
            em.wait("tensor", "ev", ev_xfm)
            for dm in range(NDM):
                em.op("tensor", lambda e, dm=dm: e.matmul(
                    pT[0:1, 0:Tc], onesR[:], scr_r[:, dm * Tc:(dm + 1) * Tc],
                    start=(dm == 0), stop=(dm == NDM - 1)),
                    inc=("mm" if dm == NDM - 1 else None))
            sq_mm = []
            for dm in range(NDM):
                em.wait("scalar", "ev", ev_xfm)
                if dm >= 1:
                    em.wait("scalar", "mm", sq_mm[dm - 1])
                em.op("scalar", lambda e, dm=dm: e.activation(
                    sq[:, 0:Tc],
                    scr_r[:, dm * Tc:(dm + 1) * Tc], AF.Square), inc="ev")
                em.drain("scalar")
                em.wait("tensor", "ev", em.cnt["ev"])
                em.op("tensor", lambda e, dm=dm: e.matmul(
                    pT[32:33, 0:Tc], onesT[:], sq[:, 0:Tc],
                    start=(dm == 0), stop=(dm == NDM - 1)), inc="mm")
                sq_mm.append(em.cnt["mm"])
            mm_st = em.cnt["mm"]

            def row_stats_bcast(mm_wait, psrc0, psrc1):
                em.wait("vector", "mm", mm_wait)
                em.op("vector", lambda e: e.tensor_scalar(
                    row1, psrc1, INV_D, None, ALU.mult), inc="dv")
                em.drain("vector")
                em.op("vector", lambda e: e.tensor_scalar(
                    bst[0:1, :], psrc0, INV_D, None, ALU.mult), inc="dv")
                em.drain("vector")
                dv_mu = em.cnt["dv"]
                em.wait("tensor", "dv", dv_mu)
                em.op("tensor", lambda e: e.matmul(
                    pA[:, 0:Tc], sel16[:, 0:128], bst[:, 0:Tc], start=True, stop=True), inc="mm")
                mm_mu = em.cnt["mm"]
                em.op("vector", lambda e: e.tensor_mul(
                    row2, bst[0:1, :], bst[0:1, :]), inc="dv")
                em.drain("vector")
                em.op("vector", lambda e: e.tensor_sub(row1, row1, row2), inc="dv")
                em.drain("vector")
                dvv = em.cnt["dv"]
                em.wait("scalar", "dv", dvv)
                em.op("scalar", lambda e: e.activation(row2, row1, AF.Ln, bias=EPS), inc="ev")
                em.drain("scalar")
                em.op("scalar", lambda e: e.activation(row2, row2, AF.Exp, scale=-0.5), inc="ev")
                evv = em.cnt["ev"]
                em.wait("vector", "ev", evv)
                em.wait("vector", "mm", mm_mu)
                em.op("vector", lambda e: e.tensor_copy(bst[0:1, :], row2), inc="dv")
                em.drain("vector")
                dv_r = em.cnt["dv"]
                em.wait("tensor", "dv", dv_r)
                em.op("tensor", lambda e: e.matmul(
                    pB[:, 0:Tc], sel16[:, 0:128], bst[:, 0:Tc], start=True, stop=True), inc="mm")
                return em.cnt["mm"]

            mm_bc1 = row_stats_bcast(mm_st, pT[0:1, 0:Tc], pT[32:33, 0:Tc])
            em.wait("vector", "mm", mm_bc1)
            for dm in range(NDM):
                em.op("vector", lambda e, dm=dm: e.tensor_sub(
                    cacc[:, 0:Tc], scr_r[:, dm * Tc:(dm + 1) * Tc], pA[:, 0:Tc]))
                em.drain("vector")
                em.op("vector", lambda e, dm=dm: e.tensor_mul(
                    xn[:, dm * Tc:(dm + 1) * Tc], cacc[:, 0:Tc], pB[:, 0:Tc]), inc="dv")
                em.drain("vector")
            dv_xn = em.cnt["dv"]
            em.wait("tensor", "dv", dv_xn)
            for dm in range(NDM):
                em.op("tensor", lambda e, dm=dm: e.matmul(
                    pS[0:1, 0:Tc], onesR[:], xn[:, dm * Tc:(dm + 1) * Tc],
                    start=(dm == 0), stop=(dm == NDM - 1)),
                    inc=("mm" if dm == NDM - 1 else None))
            sq_mm2 = []
            for dm in range(NDM):
                em.wait("scalar", "dv", dv_xn)
                if dm >= 1:
                    em.wait("scalar", "mm", sq_mm2[dm - 1])
                em.op("scalar", lambda e, dm=dm: e.activation(
                    sq[:, 0:Tc],
                    xn[:, dm * Tc:(dm + 1) * Tc], AF.Square), inc="ev")
                em.drain("scalar")
                em.wait("tensor", "ev", em.cnt["ev"])
                em.op("tensor", lambda e, dm=dm: e.matmul(
                    pS[32:33, 0:Tc], onesT[:], sq[:, 0:Tc],
                    start=(dm == 0), stop=(dm == NDM - 1)), inc="mm")
                sq_mm2.append(em.cnt["mm"])
            mm_st2 = em.cnt["mm"]
            mm_bc2 = row_stats_bcast(mm_st2, pS[0:1, 0:Tc], pS[32:33, 0:Tc])
            em.wait("vector", "mm", mm_bc2)
            for dm in range(NDM):
                em.op("vector", lambda e, dm=dm: e.tensor_sub(
                    cacc[:, 0:Tc], xn[:, dm * Tc:(dm + 1) * Tc], pA[:, 0:Tc]))
                em.drain("vector")
                em.op("vector", lambda e, dm=dm: e.tensor_mul(
                    sq[:, 0:Tc], cacc[:, 0:Tc], pB[:, 0:Tc]))
                em.drain("vector")
                em.op("vector", lambda e, dm=dm: e.tensor_scalar(
                    xn2[:, dm * Tc:(dm + 1) * Tc], sq[:, 0:Tc],
                    pc[:, dm:dm + 1], pc[:, 8 + dm:9 + dm], ALU.mult, ALU.add), inc="dv")
                em.drain("vector")
                em.op("vector", lambda e, dm=dm: e.tensor_add(
                    cacc[:, 0:Tc], scr_r[:, dm * Tc:(dm + 1) * Tc],
                    xn[:, dm * Tc:(dm + 1) * Tc]))
                em.drain("vector")
                em.op("vector", lambda e, dm=dm: e.tensor_scalar(
                    r1h[:, dm * Tc:(dm + 1) * Tc], cacc[:, 0:Tc],
                    0.5, pc[:, 48 + dm:49 + dm], ALU.mult, ALU.add), inc="dv")
                em.drain("vector")
            em.barrier()

            em.phase = "p2"
            base_a = c * 96
            for m in range(2 * NG):
                mg, mi = divmod(m, 4)
                pb = (pA, pB)[m % 2]
                em.wait("tensor", "ldw", (base_a + (mg + 1) * 8) * 16)
                if m >= 2:
                    em.wait("tensor", "ev", em.cnt["ev"])
                    em.wait("tensor", "ev2", em.cnt["ev2"])
                for k in range(8):
                    sl = (base_a + mg * 8 + k) % NA_SLOT
                    src = xn2
                    em.op("tensor", lambda e, pb=pb, sl=sl, mi=mi, k=k, src=src: e.matmul(
                        pb[:, 0:Tc],
                        wpa[:, sl * 512 + mi * 128: sl * 512 + (mi + 1) * 128],
                        src[:, k * Tc:(k + 1) * Tc],
                        start=(k == 0), stop=(k == 7)),
                        inc=("mm" if k == 7 else None))
                mmv = em.cnt["mm"]
                if m == 4 * mg + 3:
                    qa.record(mmv)
                    qa.record(mmv)
                em.wait("scalar", "mm", mmv)
                if m < NG:
                    g = m
                    hb = 0
                    em.wait("scalar", "dv", em.cnt["dv"]) if g >= 1 else None
                    em.op("scalar", lambda e, pb=pb, hb=hb: e.copy(
                        xh_rot[:, hb + 3:hb + 3 + Tc], pb[:, 0:Tc]), inc="ev")
                    ev_xh = em.cnt["ev"]
                    em.wait("vector", "ev", ev_xh)
                    em.op("vector", lambda e, hb=hb, g=g: e.tensor_copy(
                        xh_rot[:, hb:hb + 3], halo[:, g * 3:(g + 1) * 3]))
                    em.drain("vector")
                    em.op("vector", lambda e, hb=hb, g=g: e.tensor_scalar(
                        cacc[:, 0:Tc], xh_rot[:, hb:hb + Tc],
                        pc[:, 56 + g * 4:57 + g * 4], None, ALU.mult))
                    em.drain("vector")
                    for kk in range(1, D_CONV):
                        em.op("vector", lambda e, hb=hb, g=g, kk=kk: e.scalar_tensor_tensor(
                            cacc[:, 0:Tc], xh_rot[:, hb + kk:hb + kk + Tc],
                            pc[:, 56 + g * 4 + kk:57 + g * 4 + kk],
                            cacc[:, 0:Tc], ALU.mult, ALU.add))
                        em.drain("vector")
                    em.op("vector", lambda e, g=g: e.tensor_scalar(
                        sq[:, 0:Tc], cacc[:, 0:Tc], 1.0,
                        pc[:, 40 + g:41 + g], ALU.mult, ALU.add))
                    em.drain("vector")
                    em.op("vector", lambda e, hb=hb, g=g: e.tensor_copy(
                        halo[:, g * 3:(g + 1) * 3], xh_rot[:, hb + Tc:hb + Tc + 3]),
                        inc="dv")
                    em.drain("vector")
                    em.wait("scalar", "dv", em.cnt["dv"])
                    em.wait("scalar", "ev2", em.cnt["ev2"])
                    em.op("scalar", lambda e: e.activation(
                        da_b[:, 0:Tc], sq[:, 0:Tc], AF.Tanh, scale=0.5),
                        inc="ev")
                    em.drain("scalar")
                    em.wait("vector", "ev", em.cnt["ev"])
                    em.op("vector", lambda e: e.tensor_scalar(
                        da_b[:, Tc:2 * Tc], da_b[:, 0:Tc], 0.5, 0.5,
                        ALU.mult, ALU.add))
                    em.drain("vector")
                    em.op("vector", lambda e, g=g: e.tensor_mul(
                        xc[:, g * Tc:(g + 1) * Tc], sq[:, 0:Tc],
                        da_b[:, Tc:2 * Tc]), inc="ev2")
                    em.drain("vector")
                else:
                    g = m - NG
                    em.wait("scalar", "ev2", em.cnt["ev2"])
                    em.op("scalar", lambda e, pb=pb: e.activation(
                        da_b[:, 0:Tc], pb[:, 0:Tc], AF.Tanh, scale=0.5), inc="ev")
                    em.drain("scalar")
                    em.wait("vector", "ev", em.cnt["ev"])
                    em.op("vector", lambda e: e.tensor_scalar(
                        da_b[:, Tc:2 * Tc], da_b[:, 0:Tc], 0.5, 0.5,
                        ALU.mult, ALU.add))
                    em.drain("vector")
                    em.op("vector", lambda e, pb=pb, g=g: e.tensor_mul(
                        sz[:, g * Tc:(g + 1) * Tc], pb[:, 0:Tc],
                        da_b[:, Tc:2 * Tc]), inc="ev2")
                    em.drain("vector")
            ev_p2 = em.cnt["ev"]
            em.barrier()

            p9_mm = 0
            p9_ev = 0
            if c >= 1:
                em.phase = "p9"
                p9_mm = emit_p9(c - 1)
                p9_ev = em.cnt["ev"]
            em.phase = "p3"
            for g in range(NG):
                em.op("tensor", lambda e, g=g: e.matmul(
                    pT[0:65, 0:Tc], wxp_sb[:, g * 65:(g + 1) * 65],
                    xc[:, g * Tc:(g + 1) * Tc],
                    start=(g == 0), stop=(g == NG - 1)),
                    inc=("mm" if g == NG - 1 else None))
            mm_xpj = em.cnt["mm"]
            em.wait("scalar", "mm", mm_xpj)
            em.op("scalar", lambda e: e.copy(xpjp, pT[0:65, 0:Tc]), inc="ev")
            ev_xpj = em.cnt["ev"]
            em.wait("sync", "ev", ev_xpj)
            em.op("sync", lambda e, c=c: e.dma_start(
                out=xpj_snd.ap()[c, :, :], in_=xpjp), inc="io", by=16)
            io_snd = em.cnt["io"]
            em.wait("gpsimd", "io", io_snd)
            em.op("gpsimd", lambda e, c=c: e.collective_compute(
                "AllReduce", ALU.add, replica_groups=GROUPS,
                ins=[xpj_snd.ap()[c, :, :]], outs=[xpj_rcv.ap()[c, :, :]]), inc="cc")
            cc_xpj = em.cnt["cc"]
            load_w1(c)
            load_w1(c)
            load_w2_wout(c)
            em.wait("sync", "cc", cc_xpj)
            em.op("sync", lambda e, c=c: e.dma_start(
                out=xpjB[:, :], in_=xpj_rcv.ap()[c, 0:16, :]), inc="io", by=16)
            em.op("sync", lambda e, c=c: e.dma_start(
                out=xpjC[:, :], in_=xpj_rcv.ap()[c, 32:48, :]), inc="io", by=16)
            em.op("sync", lambda e, c=c: e.dma_start(
                out=bst[0:1, :], in_=xpj_rcv.ap()[c, 64:65, :]), inc="io", by=16)
            io_rcv = em.cnt["io"]
            em.wait("tensor", "io", io_rcv)
            em.op("tensor", lambda e: e.matmul(
                pT[:, 0:Tc], sel16[:, 0:128], bst[:, 0:Tc],
                start=True, stop=True), inc="mm")
            mm_xi = em.cnt["mm"]
            for g in range(NG):
                zs = g % 2
                em.wait("vector", "mm", max(mm_xi, p9_mm))
                em.op("vector", lambda e, g=g, zs=zs: e.tensor_scalar(
                    da_b[:, zs * Tc:(zs + 1) * Tc], pT[:, 0:Tc],
                    pc[:, 16 + g:17 + g], pc[:, 24 + g:25 + g],
                    ALU.mult, ALU.add), inc="dv")
                em.drain("vector")
                em.op("vector", lambda e, zs=zs: e.scalar_tensor_tensor(
                    dbx_b[:, zs * Tc:(zs + 1) * Tc], da_b[:, zs * Tc:(zs + 1) * Tc],
                    -1.0, da_b[:, zs * Tc:(zs + 1) * Tc], ALU.mult, ALU.max), inc="dv")
                em.drain("vector")
                em.wait("scalar", "dv", em.cnt["dv"])
                em.op("scalar", lambda e, zs=zs: e.activation(
                    dbx_b[:, zs * Tc:(zs + 1) * Tc], dbx_b[:, zs * Tc:(zs + 1) * Tc],
                    AF.Exp, scale=-1.0), inc="ev")
                em.drain("scalar")
                em.op("scalar", lambda e, zs=zs: e.activation(
                    dbx_b[:, zs * Tc:(zs + 1) * Tc], dbx_b[:, zs * Tc:(zs + 1) * Tc],
                    AF.Ln, bias=1.0), inc="ev")
                em.wait("vector", "ev", em.cnt["ev"])
                em.op("vector", lambda e, g=g, zs=zs: e.scalar_tensor_tensor(
                    dt_sb[:, g * Tc:(g + 1) * Tc], da_b[:, zs * Tc:(zs + 1) * Tc],
                    0.0, dbx_b[:, zs * Tc:(zs + 1) * Tc], ALU.max, ALU.add), inc="dv")
                em.drain("vector")
                em.op("vector", lambda e, g=g: e.tensor_mul(
                    xn2[:, g * Tc:(g + 1) * Tc], dt_sb[:, g * Tc:(g + 1) * Tc],
                    xc[:, g * Tc:(g + 1) * Tc]), inc="dv")
                em.drain("vector")
            em.barrier()

            em.phase = "p5"
            b_sc = em.cnt["sc"]
            b_yx = em.cnt["yx"]
            yac_done = lambda ip: b_yx + (ip - 2 * (1 + ip // 32)) + 1 \
                if False else None
            yac_mark = {}
            dvq_gate = 0
            ffn1_q = []
            for m in range(NM1):
                ffn1_q.append(m)
            ffn1_mm = {}
            ffs_marks = []
            a_base_fb0 = base_a + 32

            def emit_ffn1_m(m, fb, dst, a_base):
                mg, mi = divmod(m, 4)
                pb = (pY[2], pY[3])[m % 2]
                em.wait("tensor", "ldw", (a_base + (mg + 1) * 8) * 16)
                if fb == 0 and m < 2 and p9_ev:
                    em.wait("tensor", "ev", p9_ev)
                if len(ffs_marks) >= 2:
                    em.wait("tensor", "ffs", ffs_marks[-2])
                for k in range(8):
                    sl = (a_base + mg * 8 + k) % NA_SLOT
                    em.op("tensor", lambda e, pb=pb, sl=sl, mi=mi, k=k, fb=fb: e.matmul(
                        pb[:, 0:HB],
                        wpa[:, sl * 512 + mi * 128: sl * 512 + (mi + 1) * 128],
                        xn[:, k * Tc + fb * HB: k * Tc + fb * HB + HB],
                        start=(k == 0), stop=(k == 7)),
                        inc=("mm" if k == 7 else None))
                mmv = em.cnt["mm"]
                if m == 4 * mg + 3:
                    qa.record(mmv)
                    qa.record(mmv)
                em.wait("scalar", "mm", mmv)
                em.op("scalar", lambda e, pb=pb, m=m, dst=dst: e.activation(
                    dst[:, m * HB:(m + 1) * HB], pb[:, 0:HB], AF.Relu,
                    bias=pc[:, 88 + m:89 + m]), inc="ffs")
                ffs_marks.append(em.cnt["ffs"])

            for hf in range(2):
                for qp in range(2):
                    blk = hf * 2 + qp
                    for n in range(D_STATE):
                        i0 = blk * 32 + n * 2
                        if i0 > 0:
                            em.wait("tensor", "sc", b_sc + i0)
                        em.op("tensor", lambda e, n=n: e.matmul(
                            pA[:, 0:Tc], sel16[:, n * 128:(n + 1) * 128],
                            xpjB[:, :], start=True, stop=True), inc="bc")
                        pCb = (pB, pT)[n % 2]
                        if i0 >= 4:
                            em.wait("tensor", "sc", b_sc + i0 - 2)
                        em.op("tensor", lambda e, n=n, pCb=pCb: e.matmul(
                            pCb[:, 0:Tc], sel16[:, n * 128:(n + 1) * 128],
                            xpjC[:, :], start=True, stop=True), inc="bc")
                        bcv = em.cnt["bc"]
                        if n >= 1:
                            if n == 1 and dvq_gate:
                                em.wait("tensor", "dv", dvq_gate)
                            for g2 in range(2):
                                ip = blk * 32 + (n - 1) * 2 + g2
                                em.wait("tensor", "sc", b_sc + ip + 1)
                                em.op("tensor", lambda e, g2=g2, ip=ip, nn=n - 1: e.matmul(
                                    pY[g2][:, 0:Tc], identR[:],
                                    hc_b[:, (ip % 4) * Tc:(ip % 4 + 1) * Tc],
                                    start=(nn == 0), stop=(nn == D_STATE - 1)), inc="yx")
                                yac_mark[ip] = em.cnt["yx"]
                        if n % 2 == 0 and ffn1_q:
                            emit_ffn1_m(ffn1_q.pop(0), 0, scr_r, a_base_fb0)
                        for g2 in range(2):
                            g = hf * 4 + qp * 2 + g2
                            i = blk * 32 + n * 2 + g2
                            sl = i % 3
                            s2 = i % 2
                            s4 = i % 4
                            acol = g * 16 + n
                            if i >= 3:
                                em.wait("scalar", "sc", b_sc + i - 2)
                            em.op("scalar", lambda e, g=g, acol=acol, sl=sl: e.activation(
                                da_b[:, sl * Tc:(sl + 1) * Tc],
                                dt_sb[:, g * Tc:(g + 1) * Tc], AF.Exp,
                                scale=aneg[:, acol:acol + 1]), inc="da")
                            dav = em.cnt["da"]
                            em.wait("vector", "bc", bcv)
                            em.op("vector", lambda e, g=g, s2=s2: e.tensor_mul(
                                dbx_b[:, s2 * Tc:(s2 + 1) * Tc],
                                xn2[:, g * Tc:(g + 1) * Tc], pA[:, 0:Tc]))
                            em.drain("vector")
                            em.wait("vector", "da", dav)
                            em.op("vector", lambda e, sl=sl, s2=s2, i=i: e.tensor_tensor_scan(
                                h32_b[:, s2 * Tc:(s2 + 1) * Tc],
                                da_b[:, sl * Tc:(sl + 1) * Tc],
                                dbx_b[:, s2 * Tc:(s2 + 1) * Tc],
                                state[:, i:i + 1], ALU.mult, ALU.add))
                            em.drain("vector")
                            if i >= 4:
                                em.wait("vector", "yx", yac_mark[i - 4])
                            em.op("vector", lambda e, s2=s2, s4=s4, pCb=pCb: e.tensor_mul(
                                hc_b[:, s4 * Tc:(s4 + 1) * Tc],
                                h32_b[:, s2 * Tc:(s2 + 1) * Tc], pCb[:, 0:Tc]))
                            em.drain("vector")
                            em.op("vector", lambda e, s2=s2, i=i: e.tensor_copy(
                                state[:, i:i + 1],
                                h32_b[:, s2 * Tc + Tc - 1: s2 * Tc + Tc]), inc="sc")
                            em.drain("vector")
                    for g2 in range(2):
                        ip = blk * 32 + 15 * 2 + g2
                        em.wait("tensor", "sc", b_sc + ip + 1)
                        em.op("tensor", lambda e, g2=g2, ip=ip: e.matmul(
                            pY[g2][:, 0:Tc], identR[:],
                            hc_b[:, (ip % 4) * Tc:(ip % 4 + 1) * Tc],
                            start=False, stop=True), inc="yx")
                        yac_mark[ip] = em.cnt["yx"]
                    yx_blk = em.cnt["yx"]
                    for g2 in range(2):
                        g = hf * 4 + qp * 2 + g2
                        em.wait("vector", "yx", yx_blk)
                        em.op("vector", lambda e, g=g: e.tensor_scalar(
                            cacc[:, 0:Tc], xc[:, g * Tc:(g + 1) * Tc],
                            pc[:, 32 + g:33 + g], None, ALU.mult))
                        em.drain("vector")
                        em.op("vector", lambda e, g=g, g2=g2: e.scalar_tensor_tensor(
                            sq[:, 0:Tc], cacc[:, 0:Tc], 1.0,
                            pY[g2][:, 0:Tc], ALU.mult, ALU.add))
                        em.drain("vector")
                        em.op("vector", lambda e, g=g: e.tensor_mul(
                            ysz[:, g * Tc:(g + 1) * Tc], sq[:, 0:Tc],
                            sz[:, g * Tc:(g + 1) * Tc]), inc="dv")
                        em.drain("vector")
                    dvq_gate = em.cnt["dv"]
            em.barrier()

            em.phase = "p7"
            base_b = c * 96
            obank = (pA, pB, pT, pS)

            def emit_out_pass_both(ffs_tgt):
                bb = base_b
                em.wait("tensor", "ffs", ffs_tgt)
                obk = (pA, pB, pT, pS, pY[0], pY[1], pY[2], pY[3])
                def bank_of(fb, dm):
                    return obk[fb * 4 + dm % 4], (0 if dm < 4 else HB)
                for k in range(16):
                    so = bb + 4 * k
                    em.wait("tensor", "ldwb", (so + 4) * 16)
                    for dm in range(NDM):
                        sl = (so + dm // 2) % NB_SLOT
                        for fb, hsrc in ((0, scr_r), (1, xn2)):
                            bank, colo = bank_of(fb, dm)
                            em.op("tensor", lambda e, k=k, dm=dm, bank=bank, colo=colo, sl=sl, hsrc=hsrc: e.matmul(
                                bank[:, colo:colo + HB],
                                wpb[:, sl * 256 + (dm % 2) * 128:sl * 256 + (dm % 2 + 1) * 128],
                                hsrc[:, k * HB:(k + 1) * HB],
                                start=(k == 0 and dm < 4), stop=False),
                                inc=("mm" if (dm == NDM - 1 and fb == 1) else None))
                    qb.record(em.cnt["mm"])
                for k in range(8):
                    so = bb + 64 + 4 * k
                    em.wait("tensor", "ldwb", (so + 4) * 16)
                    for dm in range(NDM):
                        sl = (so + dm // 2) % NB_SLOT
                        for fb in (0, 1):
                            bank, colo = bank_of(fb, dm)
                            em.op("tensor", lambda e, k=k, dm=dm, bank=bank, colo=colo, sl=sl, fb=fb: e.matmul(
                                bank[:, colo:colo + HB],
                                wpb[:, sl * 256 + (dm % 2) * 128:sl * 256 + (dm % 2 + 1) * 128],
                                ysz[:, k * Tc + fb * HB: k * Tc + fb * HB + HB],
                                start=False, stop=(k == 7 and dm >= 4)),
                                inc=("mm" if (dm == NDM - 1 and fb == 1) else None))
                    qb.record(em.cnt["mm"])
                mm_out = em.cnt["mm"]
                em.wait("vector", "mm", mm_out)
                for fb in (0, 1):
                    for dm in range(NDM):
                        bank, colo = bank_of(fb, dm)
                        em.op("vector", lambda e, dm=dm, fb=fb, bank=bank, colo=colo: e.scalar_tensor_tensor(
                            dt_sb[:, dm * Tc + fb * HB: dm * Tc + fb * HB + HB],
                            r1h[:, dm * Tc + fb * HB: dm * Tc + fb * HB + HB],
                            1.0, bank[:, colo:colo + HB], ALU.mult, ALU.add), inc="dv")
                        em.drain("vector")
                return em.cnt["dv"]

            a_base_fb1 = base_a + 64
            for m in range(NM1):
                emit_ffn1_m(m, 1, xn2, a_base_fb1)
            ffs_fb1 = em.cnt["ffs"]
            dv1 = emit_out_pass_both(ffs_fb1)
            em.wait("sync", "dv", dv1)
            em.op("sync", lambda e, c=c: e.dma_start(
                out=op_snd.ap()[c, :, :], in_=dt_sb[:, 0:NDM * Tc]), inc="io", by=16)
            io_op = em.cnt["io"]
            em.wait("gpsimd", "io", io_op)
            em.op("gpsimd", lambda e, c=c: e.collective_compute(
                "AllReduce", ALU.add, replica_groups=GROUPS,
                ins=[op_snd.ap()[c, :, :]], outs=[op_rcv.ap()[c, :, :]]), inc="cc")
            cc_op_marks[c] = em.cnt["cc"]
            em.barrier()
            if c == NCH - 1:
                emit_p9(c)
            if c + 1 < NCH:
                em.wait("sync", "st", em.cnt["st"])
                load_x(c + 1)
                load_w_in(c + 1)
            em.barrier()

        em.wait("sync", "st", em.cnt["st"])

        with nc.Block() as blk:
            @blk.sync
            def _(eng):
                for fn in em.streams["sync"]:
                    fn(eng)

            @blk.gpsimd
            def _(eng):
                for fn in em.streams["gpsimd"]:
                    fn(eng)

            @blk.tensor
            def _(eng):
                for fn in em.streams["tensor"]:
                    fn(eng)

            @blk.vector
            def _(eng):
                for fn in em.streams["vector"]:
                    fn(eng)

            @blk.scalar
            def _(eng):
                for fn in em.streams["scalar"]:
                    fn(eng)

    return nc


NSCAN = 128


def kernel(**inputs):
    x = np.ascontiguousarray(np.asarray(inputs["L0_attn_x"], dtype=np.float32))
    B, L, _ = x.shape
    w_in_full = np.asarray(inputs["L0_attn_W_in"], dtype=np.float32)
    conv_w = np.asarray(inputs["L0_attn_conv_w"], dtype=np.float32)
    conv_b = np.asarray(inputs["L0_attn_conv_b"], dtype=np.float32)
    w_xp_full = np.asarray(inputs["L0_attn_W_xp"], dtype=np.float32)
    w_dt = np.asarray(inputs["L0_attn_W_dt"], dtype=np.float32).reshape(-1)
    b_dt = np.asarray(inputs["L0_attn_b_dt"], dtype=np.float32)
    a_log = np.asarray(inputs["L0_attn_A_log"], dtype=np.float32)
    d_par = np.asarray(inputs["L0_attn_D"], dtype=np.float32)
    w_out_full = np.asarray(inputs["L0_attn_W_out"], dtype=np.float32)
    ln_g = np.asarray(inputs["L0_attn_ln_g"], dtype=np.float32)
    ln_b = np.asarray(inputs["L0_attn_ln_b"], dtype=np.float32)
    w1_full = np.asarray(inputs["L0_ffn_W1"], dtype=np.float32)
    b1 = np.asarray(inputs["L0_ffn_b1"], dtype=np.float32)
    w2_full = np.asarray(inputs["L0_ffn_W2"], dtype=np.float32)
    b2 = np.asarray(inputs["L0_ffn_b2"], dtype=np.float32)

    nc = build_program(L)

    ident = np.eye(128, dtype=np.float32)
    sel16 = np.zeros((16, 2048), dtype=np.float32)
    for n in range(16):
        sel16[n, n * 128:(n + 1) * 128] = 1.0

    in_maps = []
    for r in range(NCORES):
        b = r // 2
        H = r % 2
        ch = slice(H * DH, (H + 1) * DH)
        ih = slice(H * IH, (H + 1) * IH)
        w_in_c = np.concatenate(
            [w_in_full[:, H * DH:(H + 1) * DH],
             w_in_full[:, 2 * DH + H * DH: 2 * DH + (H + 1) * DH]], axis=1)
        pco = np.zeros((128, 104), dtype=np.float32)
        pco[:, 0:8] = ln_g.reshape(8, 128).T
        pco[:, 8:16] = ln_b.reshape(8, 128).T
        pco[:, 16:24] = w_dt[ch].reshape(8, 128).T
        pco[:, 24:32] = b_dt[ch].reshape(8, 128).T
        pco[:, 32:40] = d_par[ch].reshape(8, 128).T
        pco[:, 40:48] = conv_b[ch].reshape(8, 128).T
        pco[:, 48:56] = (0.5 * b2).reshape(8, 128).T
        cw = conv_w[ch].reshape(8, 128, D_CONV)
        for g in range(8):
            pco[:, 56 + g * 4:56 + (g + 1) * 4] = cw[g]
        pco[:, 88:104] = b1[ih].reshape(16, 128).T
        alog_c = np.zeros((128, 128), dtype=np.float32)
        al = a_log[ch].reshape(8, 128, D_STATE)
        for g in range(8):
            alog_c[:, g * 16:(g + 1) * 16] = al[g]
        wxp_pad = np.zeros((DH, 65), dtype=np.float32)
        wxp_pad[:, 0:16] = w_xp_full[ch, 0:16]
        wxp_pad[:, 32:48] = w_xp_full[ch, 16:32]
        wxp_pad[:, 64] = w_xp_full[ch, 32]
        in_maps.append({
            "x": np.ascontiguousarray(x[b]),
            "w_in": np.ascontiguousarray(w_in_c),
            "w1": np.ascontiguousarray(w1_full[:, ih]),
            "w2": np.ascontiguousarray(w2_full[ih, :]),
            "w_out": np.ascontiguousarray(w_out_full[ch, :]),
            "w_xp": wxp_pad,
            "pcons": pco,
            "a_log": alog_c,
            "identc": ident,
            "identrc": ident,
            "sel16c": sel16,
            "onesc": np.ones((128, 1), dtype=np.float32),
            "onesrc": np.ones((128, 1), dtype=np.float32),
        })

    res = run_bass_kernel_spmd(nc, in_maps, core_ids=list(range(NCORES)))
    out = np.empty((B, L, D_MODEL), dtype=np.float32)
    for b in range(B):
        out[b] = res.results[2 * b]["out"]
    return out


# version 16
# speedup vs baseline: 1.0890x; 1.0016x over previous
import os
import numpy as np
from contextlib import ExitStack

EMIT_DRAINS = os.environ.get("BASS_SIM_FP64") == "1" or os.environ.get("KERNEL_SIM_DRAINS") == "1"
HW_DRAINS = set(os.environ.get("KERNEL_HW_DRAINS", "p1").split(","))

import concourse.bass as bass
import concourse.mybir as mybir
from concourse.bass_utils import run_bass_kernel_spmd

F32 = mybir.dt.float32
F32R = mybir.dt.float32r
ALU = mybir.AluOpType
AF = mybir.ActivationFunctionType

D_MODEL = 1024
D_STATE = 16
D_CONV = 4
DH = 1024
IH = 2048
NCORES = 8
EPS = 1e-5
INV_D = 1.0 / D_MODEL

ENGS = ("sync", "gpsimd", "tensor", "vector", "scalar")


class Emitter:
    def __init__(self, nc, es):
        self.nc = nc
        self.streams = {e: [] for e in ENGS}
        self.cnt = {}
        self.sems = {}
        self.nbar = 0
        self.phase = "pro"
        self._es = es

    def sem(self, name):
        self.sems[name] = self._es.enter_context(self.nc.semaphore(name))
        self.cnt[name] = 0

    def op(self, eng, fn, inc=None, by=1):
        if inc is None:
            self.streams[eng].append(fn)
            return None
        s = self.sems[inc]
        self.cnt[inc] += by
        def wrapped(e):
            fn(e).then_inc(s, by)
        self.streams[eng].append(wrapped)
        return self.cnt[inc]

    def wait(self, eng, name, val, force=False):
        if val <= 0 and not force:
            return
        s = self.sems[name]
        self.streams[eng].append(lambda e: e.wait_ge(s, max(val, 0)))

    BAR_ENGS = ("gpsimd", "tensor", "vector", "scalar")

    def barrier(self):
        self.nbar += 1
        tgt = self.nbar * len(self.BAR_ENGS)
        s = self.sems["bar"]
        for eng in self.BAR_ENGS:
            self.streams[eng].append(lambda e: e.nop().then_inc(s, 1))
            self.streams[eng].append(lambda e: e.wait_ge(s, tgt))

    def drain(self, eng, hw=False):
        if EMIT_DRAINS or eng == "scalar" or hw or "all" in HW_DRAINS \
                or self.phase in HW_DRAINS:
            self.streams[eng].append(lambda e: e.drain())


def build_program(L):
    Tc = min(512, L)
    NCH = L // Tc
    HB = Tc // 2
    NG = DH // 128
    NDM = D_MODEL // 128
    NM1 = IH // 128
    NTS = Tc // 128

    NA_SLOT = 8
    NB_SLOT = 8

    nc = bass.Bass("TRN2", target_bir_lowering=False)

    x_hbm = nc.dram_tensor("x", [L, D_MODEL], F32, kind="ExternalInput").ap()
    w_in = nc.dram_tensor("w_in", [D_MODEL, 2 * DH], F32R, kind="ExternalInput").ap()
    w1 = nc.dram_tensor("w1", [D_MODEL, IH], F32R, kind="ExternalInput").ap()
    w2 = nc.dram_tensor("w2", [IH, D_MODEL], F32R, kind="ExternalInput").ap()
    w_out = nc.dram_tensor("w_out", [DH, D_MODEL], F32R, kind="ExternalInput").ap()
    w_xp = nc.dram_tensor("w_xp", [DH, 65], F32R, kind="ExternalInput").ap()
    pcons = nc.dram_tensor("pcons", [128, 104], F32, kind="ExternalInput").ap()
    a_log = nc.dram_tensor("a_log", [128, 128], F32, kind="ExternalInput").ap()
    ident_in = nc.dram_tensor("identc", [128, 128], F32, kind="ExternalInput").ap()
    sel16_in = nc.dram_tensor("sel16c", [16, 2048], F32R, kind="ExternalInput").ap()
    identr_in = nc.dram_tensor("identrc", [128, 128], F32R, kind="ExternalInput").ap()
    onesr_in = nc.dram_tensor("onesrc", [128, 1], F32R, kind="ExternalInput").ap()
    ones_in = nc.dram_tensor("onesc", [128, 1], F32, kind="ExternalInput").ap()
    out_hbm = nc.dram_tensor("out", [L, D_MODEL], F32, kind="ExternalOutput").ap()

    xpj_snd = nc.dram_tensor("xpj_snd", [NCH, 65, Tc], F32R)
    xpj_rcv = nc.dram_tensor("xpj_rcv", [NCH, 65, Tc], F32R)
    op_snd = nc.dram_tensor("op_snd", [NCH, 128, NDM * Tc], F32)
    op_rcv = nc.dram_tensor("op_rcv", [NCH, 128, NDM * Tc], F32)
    GROUPS = [[0, 1], [2, 3], [4, 5], [6, 7]]

    es = ExitStack()
    sb = lambda nm, shp, dt=F32: es.enter_context(nc.sbuf_tensor(nm, shp, dt))
    ps = lambda nm, shp: es.enter_context(nc.psum_tensor(nm, shp, F32))

    with es:
        pc = sb("pc", [128, 104])
        aneg = sb("aneg", [128, 128])
        identT = sb("identT", [128, 128])
        identR = sb("identR", [128, 128], F32R)
        sel16 = sb("sel16", [16, 2048], F32R)
        onesT = sb("onesT", [128, 1])
        onesR = sb("onesR", [128, 1], F32R)
        wxp_sb = sb("wxp_sb", [128, 8 * 65], F32R)

        wpa = sb("wpa", [128, NA_SLOT * 512], F32R)
        wpb = sb("wpb", [128, NB_SLOT * 256], F32R)

        tok = sb("tok", [128, 2 * D_MODEL])
        scr_r = sb("scr_r", [128, NDM * Tc], F32R)
        xn = sb("xn", [128, NDM * Tc], F32R)
        xn2 = sb("xn2", [128, NG * Tc], F32R)
        r1h = sb("r1h", [128, NDM * Tc])
        sq = sb("sq", [128, Tc])

        xh_rot = sb("xh_rot", [128, Tc + 3])
        halo = sb("halo", [128, NG * 3])
        cacc = sb("cacc", [128, Tc])
        row1 = cacc[0:1, 0:Tc]
        row2 = sq[0:1, 0:Tc]
        xc = sb("xc", [128, NG * Tc], F32R)
        sz = sb("sz", [128, NG * Tc])
        dt_sb = sb("dt_sb", [128, NG * Tc])
        xpjB = sb("xpjB", [16, Tc], F32R)
        xpjC = sb("xpjC", [16, Tc], F32R)

        da_b = sb("da_b", [128, 3 * Tc])
        dbx_b = sb("dbx_b", [128, 2 * Tc])
        h32_b = sb("h32_b", [128, 2 * Tc])
        hc_b = sb("hc_b", [128, 4 * Tc], F32R)
        state = sb("state", [128, 128])

        cst = sb("cst", [128, 4])
        ysz = sb("ysz", [128, NG * Tc], F32R)
        bst = ysz[0:16, 0:Tc]
        xpjp = ysz[0:65, Tc:2 * Tc]

        pA = ps("pA", [128, 512])
        pB = ps("pB", [128, 512])
        pT = ps("pT", [128, 512])
        pS = ps("pS", [128, 512])
        pY = [ps(f"pY{i}", [128, 512]) for i in range(4)]

        nc.const_aps.register(F32, EPS, cst[:, 0:1])
        nc.const_aps.register(F32, 1.0, cst[:, 1:2])
        nc.const_aps.register(F32, -0.5, cst[:, 2:3])
        nc.const_aps.register(F32, -1.0, cst[:, 3:4])

        em = Emitter(nc, es)
        for s in ("bar", "ldw", "ldwb", "ldx", "lda", "mm", "ev", "ev2", "dv", "da",
                  "bc", "sc", "hc", "dbxs", "yx", "cc", "io", "st", "ffs"):
            em.sem(s)

        for t, src in ((pc, pcons), (aneg, a_log), (identT, ident_in),
                       (identR, identr_in), (sel16, sel16_in)):
            em.op("sync", lambda e, t=t, src=src: e.dma_start(out=t[:], in_=src),
                  inc="lda", by=16)
        em.op("sync", lambda e: e.dma_start(out=onesT[:], in_=ones_in),
              inc="lda", by=16)
        em.op("sync", lambda e: e.dma_start(out=onesR[:], in_=onesr_in[:, 0:1]),
              inc="lda", by=16)
        em.op("sync", lambda e: e.dma_start(
            out=wxp_sb[:, 0:8 * 65].rearrange("p (g c) -> p g c", c=65),
            in_=w_xp.rearrange("(g p) c -> p g c", p=128)), inc="lda", by=16)
        v_lda = em.cnt["lda"]
        em.wait("scalar", "lda", v_lda)
        em.op("scalar", lambda e: e.activation(aneg[:], aneg[:], AF.Exp), inc="ev")
        ev0 = em.cnt["ev"]
        em.wait("vector", "ev", ev0)
        em.op("vector", lambda e: e.tensor_scalar(aneg[:], aneg[:], -1.0, None, ALU.mult), inc="dv")
        em.drain("vector")
        em.op("vector", lambda e: e.memset(state[:], 0.0), inc="dv")
        em.drain("vector")
        em.op("vector", lambda e: e.memset(halo[:], 0.0), inc="dv")
        em.drain("vector")
        em.op("vector", lambda e: e.memset(cacc[:], 0.0), inc="dv")
        em.drain("vector")
        em.op("vector", lambda e: e.tensor_copy(ysz[0:16, 0:Tc], cacc[0:16, 0:Tc]), inc="dv")
        em.drain("vector")
        em.op("vector", lambda e: e.memset(cst[:, 0:1], EPS), inc="dv")
        em.op("vector", lambda e: e.memset(cst[:, 1:2], 1.0), inc="dv")
        em.op("vector", lambda e: e.memset(cst[:, 2:3], -0.5), inc="dv")
        em.op("vector", lambda e: e.memset(cst[:, 3:4], -1.0), inc="dv")
        em.drain("vector")
        em.wait("scalar", "dv", em.cnt["dv"])
        em.wait("tensor", "lda", v_lda)
        em.wait("tensor", "dv", em.cnt["dv"])

        class LoadQ:
            def __init__(self, pool, sem_name, nslot, width=512):
                self.pool = pool
                self.sem = sem_name
                self.width = width
                self.ring = nslot // 4
                self.pending = []
                self.n_loaded = 0
                self.cons = []

            def add(self, w, kslice, cslice):
                self.pending.append((w, kslice, cslice))

            def record(self, mm_val):
                self.cons.append(mm_val)
                self.pump()

            def pump(self):
                while self.pending:
                    g_ = self.n_loaded // 4
                    if g_ - self.ring >= len(self.cons):
                        break
                    if self.n_loaded % 4 == 0:
                        if g_ >= self.ring:
                            em.wait("sync", "mm", self.cons[g_ - self.ring])
                        if g_ >= 1:
                            em.wait("sync", self.sem, self.n_loaded * 16)
                    w, ks, cs = self.pending.pop(0)
                    s_ = self.n_loaded % (self.ring * 4)
                    wd = self.width
                    em.op("sync", lambda e, s_=s_, w=w, ks=ks, cs=cs, wd=wd: e.dma_start(
                        out=self.pool[:, s_ * wd:(s_ + 1) * wd],
                        in_=w[ks, cs]), inc=self.sem, by=16)
                    self.n_loaded += 1

        qa = LoadQ(wpa, "ldw", NA_SLOT)
        qb = LoadQ(wpb, "ldwb", NB_SLOT, width=256)

        def load_w_in(_c):
            for mg in range(4):
                for k in range(8):
                    qa.add(w_in, slice(k * 128, (k + 1) * 128),
                           slice(mg * 512, (mg + 1) * 512))
            qa.pump()

        def load_w1(_c):
            for mg in range(4):
                for k in range(8):
                    qa.add(w1, slice(k * 128, (k + 1) * 128),
                           slice(mg * 512, (mg + 1) * 512))
            qa.pump()

        def load_w2_wout(_c):
            for k in range(16):
                for q in range(4):
                    qb.add(w2, slice(k * 128, (k + 1) * 128),
                           slice(q * 256, (q + 1) * 256))
            for k in range(8):
                for q in range(4):
                    qb.add(w_out, slice(k * 128, (k + 1) * 128),
                           slice(q * 256, (q + 1) * 256))
            qb.pump()

        class XQ:
            def __init__(self):
                self.pending = []
                self.n_loaded = 0
                self.cons = []

            def add(self, c, ts):
                self.pending.append((c, ts))

            def record(self, mm_val):
                self.cons.append(mm_val)
                self.pump()

            def pump(self):
                while self.pending:
                    if self.n_loaded - 2 >= len(self.cons):
                        break
                    if self.n_loaded >= 2:
                        em.wait("sync", "mm", self.cons[self.n_loaded - 2])
                    if self.n_loaded >= 1:
                        em.wait("sync", "ldx", self.n_loaded * 16)
                    c_, ts_ = self.pending.pop(0)
                    sl_ = self.n_loaded % 2
                    em.op("sync", lambda e, c_=c_, ts_=ts_, sl_=sl_: e.dma_start(
                        out=tok[:, sl_ * D_MODEL:(sl_ + 1) * D_MODEL],
                        in_=x_hbm[c_ * Tc + ts_ * 128: c_ * Tc + (ts_ + 1) * 128, :]),
                        inc="ldx", by=16)
                    self.n_loaded += 1

        xq = XQ()

        def load_x(c):
            for ts_ in range(NTS):
                xq.add(c, ts_)
            xq.pump()

        load_x(0)
        load_w_in(0)
        em.barrier()

        st_marks = []
        cc_op_marks = {}

        def emit_p9_ts(cp, ts_):
            if ts_ == 0:
                em.wait("sync", "cc", cc_op_marks[cp])
            em.wait("sync", "io", em.cnt["io"])
            if st_marks:
                em.wait("sync", "st", st_marks[-1])
            em.op("sync", lambda e, cp=cp, ts_=ts_: e.dma_start(
                out=tok[:, 0:D_MODEL].rearrange("p (dm c) -> p dm c", c=128),
                in_=op_rcv.ap()[cp, :, :].rearrange(
                    "p (dm t) -> p dm t", t=Tc)[:, :, ts_ * 128:(ts_ + 1) * 128]),
                inc="io", by=16)
            io_ts = em.cnt["io"]
            em.wait("tensor", "io", io_ts)
            for half in range(2):
                if not (ts_ == 0 and half == 0):
                    em.wait("tensor", "ev", em.cnt["ev"])
                for dmq in range(4):
                    dm = half * 4 + dmq
                    em.op("tensor", lambda e, dm=dm, dmq=dmq: e.matmul(
                        pS[:, dmq * 128:(dmq + 1) * 128],
                        tok[:, dm * 128:(dm + 1) * 128],
                        identT[:], start=(dmq == 0), stop=(dmq == 3),
                        is_transpose=True),
                        inc=("mm" if dmq == 3 else None))
                mmv = em.cnt["mm"]
                em.wait("scalar", "mm", mmv)
                if len(st_marks) >= 1:
                    em.wait("scalar", "st", st_marks[-1])
                em.op("scalar", lambda e, half=half: e.copy(
                    tok[:, D_MODEL + half * 512: D_MODEL + (half + 1) * 512],
                    pS[:, 0:512]), inc="ev")
            ev_ts = em.cnt["ev"]
            em.wait("sync", "ev", ev_ts)
            em.op("sync", lambda e, cp=cp, ts_=ts_: e.dma_start(
                out=out_hbm[cp * Tc + ts_ * 128: cp * Tc + (ts_ + 1) * 128, :],
                in_=tok[:, D_MODEL:2 * D_MODEL]), inc="st", by=16)
            st_marks.append(em.cnt["st"])

        def emit_p9(cp):
            for ts_ in range(NTS):
                emit_p9_ts(cp, ts_)
            return em.cnt["mm"]

        for c in range(NCH):
            em.phase = "p1"
            for ts_ in range(NTS):
                gi = c * NTS + ts_
                sl_ = gi % 2
                bks = ((pA, pB), (pY[0], pY[1]))[ts_ % 2]
                em.wait("tensor", "ldx", (gi + 1) * 16)
                if ts_ >= 2:
                    em.wait("tensor", "ev", em.cnt["ev"])
                for dm in range(NDM):
                    bk = bks[dm // 4]
                    em.op("tensor", lambda e, bk=bk, sl_=sl_, dm=dm: e.matmul(
                        bk[:, (dm % 4) * 128:(dm % 4 + 1) * 128],
                        tok[:, sl_ * D_MODEL + dm * 128: sl_ * D_MODEL + (dm + 1) * 128],
                        identT[:], start=(dm % 4 == 0), stop=(dm % 4 == 3),
                        is_transpose=True),
                        inc=("mm" if dm == NDM - 1 else None))
                mmv = em.cnt["mm"]
                xq.record(mmv)
                em.wait("scalar", "mm", mmv)
                for dm in range(NDM):
                    bk = bks[dm // 4]
                    em.op("scalar", lambda e, bk=bk, dm=dm, ts_=ts_: e.copy(
                        scr_r[:, dm * Tc + ts_ * 128: dm * Tc + (ts_ + 1) * 128],
                        bk[:, (dm % 4) * 128:(dm % 4 + 1) * 128]), inc="ev")
            ev_xfm = em.cnt["ev"]
            em.wait("tensor", "ev", ev_xfm)
            for dm in range(NDM):
                em.op("tensor", lambda e, dm=dm: e.matmul(
                    pT[0:1, 0:Tc], onesR[:], scr_r[:, dm * Tc:(dm + 1) * Tc],
                    start=(dm == 0), stop=(dm == NDM - 1)),
                    inc=("mm" if dm == NDM - 1 else None))
            sq_mm = []
            for dm in range(NDM):
                em.wait("scalar", "ev", ev_xfm)
                if dm >= 1:
                    em.wait("scalar", "mm", sq_mm[dm - 1])
                em.op("scalar", lambda e, dm=dm: e.activation(
                    sq[:, 0:Tc],
                    scr_r[:, dm * Tc:(dm + 1) * Tc], AF.Square), inc="ev")
                em.drain("scalar")
                em.wait("tensor", "ev", em.cnt["ev"])
                em.op("tensor", lambda e, dm=dm: e.matmul(
                    pT[32:33, 0:Tc], onesT[:], sq[:, 0:Tc],
                    start=(dm == 0), stop=(dm == NDM - 1)), inc="mm")
                sq_mm.append(em.cnt["mm"])
            mm_st = em.cnt["mm"]

            def row_stats_bcast(mm_wait, psrc0, psrc1):
                em.wait("vector", "mm", mm_wait)
                em.op("vector", lambda e: e.tensor_scalar(
                    row1, psrc1, INV_D, None, ALU.mult), inc="dv")
                em.drain("vector")
                em.op("vector", lambda e: e.tensor_scalar(
                    bst[0:1, :], psrc0, INV_D, None, ALU.mult), inc="dv")
                em.drain("vector")
                dv_mu = em.cnt["dv"]
                em.wait("tensor", "dv", dv_mu)
                em.op("tensor", lambda e: e.matmul(
                    pA[:, 0:Tc], sel16[:, 0:128], bst[:, 0:Tc], start=True, stop=True), inc="mm")
                mm_mu = em.cnt["mm"]
                em.op("vector", lambda e: e.tensor_mul(
                    row2, bst[0:1, :], bst[0:1, :]), inc="dv")
                em.drain("vector")
                em.op("vector", lambda e: e.tensor_sub(row1, row1, row2), inc="dv")
                em.drain("vector")
                dvv = em.cnt["dv"]
                em.wait("scalar", "dv", dvv)
                em.op("scalar", lambda e: e.activation(row2, row1, AF.Ln, bias=EPS), inc="ev")
                em.drain("scalar")
                em.op("scalar", lambda e: e.activation(row2, row2, AF.Exp, scale=-0.5), inc="ev")
                evv = em.cnt["ev"]
                em.wait("vector", "ev", evv)
                em.wait("vector", "mm", mm_mu)
                em.op("vector", lambda e: e.tensor_copy(bst[0:1, :], row2), inc="dv")
                em.drain("vector")
                dv_r = em.cnt["dv"]
                em.wait("tensor", "dv", dv_r)
                em.op("tensor", lambda e: e.matmul(
                    pB[:, 0:Tc], sel16[:, 0:128], bst[:, 0:Tc], start=True, stop=True), inc="mm")
                return em.cnt["mm"]

            mm_bc1 = row_stats_bcast(mm_st, pT[0:1, 0:Tc], pT[32:33, 0:Tc])
            em.wait("vector", "mm", mm_bc1)
            for dm in range(NDM):
                em.op("vector", lambda e, dm=dm: e.tensor_sub(
                    cacc[:, 0:Tc], scr_r[:, dm * Tc:(dm + 1) * Tc], pA[:, 0:Tc]))
                em.drain("vector")
                em.op("vector", lambda e, dm=dm: e.tensor_mul(
                    xn[:, dm * Tc:(dm + 1) * Tc], cacc[:, 0:Tc], pB[:, 0:Tc]), inc="dv")
                em.drain("vector")
            dv_xn = em.cnt["dv"]
            em.wait("tensor", "dv", dv_xn)
            for dm in range(NDM):
                em.op("tensor", lambda e, dm=dm: e.matmul(
                    pS[0:1, 0:Tc], onesR[:], xn[:, dm * Tc:(dm + 1) * Tc],
                    start=(dm == 0), stop=(dm == NDM - 1)),
                    inc=("mm" if dm == NDM - 1 else None))
            sq_mm2 = []
            for dm in range(NDM):
                em.wait("scalar", "dv", dv_xn)
                if dm >= 1:
                    em.wait("scalar", "mm", sq_mm2[dm - 1])
                em.op("scalar", lambda e, dm=dm: e.activation(
                    sq[:, 0:Tc],
                    xn[:, dm * Tc:(dm + 1) * Tc], AF.Square), inc="ev")
                em.drain("scalar")
                em.wait("tensor", "ev", em.cnt["ev"])
                em.op("tensor", lambda e, dm=dm: e.matmul(
                    pS[32:33, 0:Tc], onesT[:], sq[:, 0:Tc],
                    start=(dm == 0), stop=(dm == NDM - 1)), inc="mm")
                sq_mm2.append(em.cnt["mm"])
            mm_st2 = em.cnt["mm"]
            mm_bc2 = row_stats_bcast(mm_st2, pS[0:1, 0:Tc], pS[32:33, 0:Tc])
            em.wait("vector", "mm", mm_bc2)
            for dm in range(NDM):
                em.op("vector", lambda e, dm=dm: e.tensor_sub(
                    cacc[:, 0:Tc], xn[:, dm * Tc:(dm + 1) * Tc], pA[:, 0:Tc]))
                em.drain("vector")
                em.op("vector", lambda e, dm=dm: e.tensor_mul(
                    sq[:, 0:Tc], cacc[:, 0:Tc], pB[:, 0:Tc]))
                em.drain("vector")
                em.op("vector", lambda e, dm=dm: e.tensor_scalar(
                    xn2[:, dm * Tc:(dm + 1) * Tc], sq[:, 0:Tc],
                    pc[:, dm:dm + 1], pc[:, 8 + dm:9 + dm], ALU.mult, ALU.add), inc="dv")
                em.drain("vector")
                em.op("vector", lambda e, dm=dm: e.tensor_add(
                    cacc[:, 0:Tc], scr_r[:, dm * Tc:(dm + 1) * Tc],
                    xn[:, dm * Tc:(dm + 1) * Tc]))
                em.drain("vector")
                em.op("vector", lambda e, dm=dm: e.tensor_scalar(
                    r1h[:, dm * Tc:(dm + 1) * Tc], cacc[:, 0:Tc],
                    0.5, pc[:, 48 + dm:49 + dm], ALU.mult, ALU.add), inc="dv")
                em.drain("vector")
            em.barrier()

            em.phase = "p2"
            base_a = c * 96
            for m in range(2 * NG):
                mg, mi = divmod(m, 4)
                pb = (pA, pB)[m % 2]
                em.wait("tensor", "ldw", (base_a + (mg + 1) * 8) * 16)
                if m >= 2:
                    em.wait("tensor", "ev", em.cnt["ev"])
                    em.wait("tensor", "ev2", em.cnt["ev2"])
                for k in range(8):
                    sl = (base_a + mg * 8 + k) % NA_SLOT
                    src = xn2
                    em.op("tensor", lambda e, pb=pb, sl=sl, mi=mi, k=k, src=src: e.matmul(
                        pb[:, 0:Tc],
                        wpa[:, sl * 512 + mi * 128: sl * 512 + (mi + 1) * 128],
                        src[:, k * Tc:(k + 1) * Tc],
                        start=(k == 0), stop=(k == 7)),
                        inc=("mm" if k == 7 else None))
                mmv = em.cnt["mm"]
                if m == 4 * mg + 3:
                    qa.record(mmv)
                    qa.record(mmv)
                em.wait("scalar", "mm", mmv)
                if m < NG:
                    g = m
                    hb = 0
                    em.wait("scalar", "dv", em.cnt["dv"]) if g >= 1 else None
                    em.op("scalar", lambda e, pb=pb, hb=hb: e.copy(
                        xh_rot[:, hb + 3:hb + 3 + Tc], pb[:, 0:Tc]), inc="ev")
                    ev_xh = em.cnt["ev"]
                    em.wait("vector", "ev", ev_xh)
                    em.op("vector", lambda e, hb=hb, g=g: e.tensor_copy(
                        xh_rot[:, hb:hb + 3], halo[:, g * 3:(g + 1) * 3]))
                    em.drain("vector")
                    em.op("vector", lambda e, hb=hb, g=g: e.tensor_scalar(
                        cacc[:, 0:Tc], xh_rot[:, hb:hb + Tc],
                        pc[:, 56 + g * 4:57 + g * 4], None, ALU.mult))
                    em.drain("vector")
                    for kk in range(1, D_CONV):
                        em.op("vector", lambda e, hb=hb, g=g, kk=kk: e.scalar_tensor_tensor(
                            cacc[:, 0:Tc], xh_rot[:, hb + kk:hb + kk + Tc],
                            pc[:, 56 + g * 4 + kk:57 + g * 4 + kk],
                            cacc[:, 0:Tc], ALU.mult, ALU.add))
                        em.drain("vector")
                    em.op("vector", lambda e, g=g: e.tensor_scalar(
                        sq[:, 0:Tc], cacc[:, 0:Tc], 1.0,
                        pc[:, 40 + g:41 + g], ALU.mult, ALU.add))
                    em.drain("vector")
                    em.op("vector", lambda e, hb=hb, g=g: e.tensor_copy(
                        halo[:, g * 3:(g + 1) * 3], xh_rot[:, hb + Tc:hb + Tc + 3]),
                        inc="dv")
                    em.drain("vector")
                    em.wait("scalar", "dv", em.cnt["dv"])
                    em.wait("scalar", "ev2", em.cnt["ev2"])
                    em.op("scalar", lambda e: e.activation(
                        da_b[:, 0:Tc], sq[:, 0:Tc], AF.Tanh, scale=0.5),
                        inc="ev")
                    em.drain("scalar")
                    em.wait("vector", "ev", em.cnt["ev"])
                    em.op("vector", lambda e: e.tensor_scalar(
                        da_b[:, Tc:2 * Tc], da_b[:, 0:Tc], 0.5, 0.5,
                        ALU.mult, ALU.add))
                    em.drain("vector")
                    em.op("vector", lambda e, g=g: e.tensor_mul(
                        xc[:, g * Tc:(g + 1) * Tc], sq[:, 0:Tc],
                        da_b[:, Tc:2 * Tc]), inc="ev2")
                    em.drain("vector")
                    em.wait("tensor", "ev2", em.cnt["ev2"])
                    em.op("tensor", lambda e, g=g: e.matmul(
                        pY[2][0:65, 0:Tc], wxp_sb[:, g * 65:(g + 1) * 65],
                        xc[:, g * Tc:(g + 1) * Tc],
                        start=(g == 0), stop=(g == NG - 1)),
                        inc=("mm" if g == NG - 1 else None))
                    if g == NG - 1:
                        mm_xpj = em.cnt["mm"]
                        em.wait("scalar", "mm", mm_xpj)
                        em.op("scalar", lambda e: e.copy(xpjp, pY[2][0:65, 0:Tc]), inc="ev")
                        ev_xpj = em.cnt["ev"]
                        em.wait("sync", "ev", ev_xpj)
                        em.op("sync", lambda e, c=c: e.dma_start(
                            out=xpj_snd.ap()[c, :, :], in_=xpjp), inc="io", by=16)
                        io_snd = em.cnt["io"]
                        em.wait("gpsimd", "io", io_snd)
                        em.op("gpsimd", lambda e, c=c: e.collective_compute(
                            "AllReduce", ALU.add, replica_groups=GROUPS,
                            ins=[xpj_snd.ap()[c, :, :]], outs=[xpj_rcv.ap()[c, :, :]]), inc="cc")
                        cc_xpj = em.cnt["cc"]
                else:
                    g = m - NG
                    em.wait("scalar", "ev2", em.cnt["ev2"])
                    em.op("scalar", lambda e, pb=pb: e.activation(
                        da_b[:, 0:Tc], pb[:, 0:Tc], AF.Tanh, scale=0.5), inc="ev")
                    em.drain("scalar")
                    em.wait("vector", "ev", em.cnt["ev"])
                    em.op("vector", lambda e: e.tensor_scalar(
                        da_b[:, Tc:2 * Tc], da_b[:, 0:Tc], 0.5, 0.5,
                        ALU.mult, ALU.add))
                    em.drain("vector")
                    em.op("vector", lambda e, pb=pb, g=g: e.tensor_mul(
                        sz[:, g * Tc:(g + 1) * Tc], pb[:, 0:Tc],
                        da_b[:, Tc:2 * Tc]), inc="ev2")
                    em.drain("vector")
            ev_p2 = em.cnt["ev"]
            em.barrier()

            em.phase = "p3"
            load_w1(c)
            load_w1(c)
            load_w2_wout(c)

            em.wait("sync", "cc", cc_xpj)
            em.op("sync", lambda e, c=c: e.dma_start(
                out=xpjB[:, :], in_=xpj_rcv.ap()[c, 0:16, :]), inc="io", by=16)
            em.op("sync", lambda e, c=c: e.dma_start(
                out=xpjC[:, :], in_=xpj_rcv.ap()[c, 32:48, :]), inc="io", by=16)
            em.op("sync", lambda e, c=c: e.dma_start(
                out=bst[0:1, :], in_=xpj_rcv.ap()[c, 64:65, :]), inc="io", by=16)
            io_rcv = em.cnt["io"]
            em.wait("tensor", "io", io_rcv)
            em.op("tensor", lambda e: e.matmul(
                pT[:, 0:Tc], sel16[:, 0:128], bst[:, 0:Tc],
                start=True, stop=True), inc="mm")
            mm_xi = em.cnt["mm"]
            for g in range(NG):
                zs = g % 2
                em.wait("vector", "mm", mm_xi)
                em.op("vector", lambda e, g=g, zs=zs: e.tensor_scalar(
                    da_b[:, zs * Tc:(zs + 1) * Tc], pT[:, 0:Tc],
                    pc[:, 16 + g:17 + g], pc[:, 24 + g:25 + g],
                    ALU.mult, ALU.add), inc="dv")
                em.drain("vector")
                em.op("vector", lambda e, zs=zs: e.scalar_tensor_tensor(
                    dbx_b[:, zs * Tc:(zs + 1) * Tc], da_b[:, zs * Tc:(zs + 1) * Tc],
                    -1.0, da_b[:, zs * Tc:(zs + 1) * Tc], ALU.mult, ALU.max), inc="dv")
                em.drain("vector")
                em.wait("scalar", "dv", em.cnt["dv"])
                em.op("scalar", lambda e, zs=zs: e.activation(
                    dbx_b[:, zs * Tc:(zs + 1) * Tc], dbx_b[:, zs * Tc:(zs + 1) * Tc],
                    AF.Exp, scale=-1.0), inc="ev")
                em.drain("scalar")
                em.op("scalar", lambda e, zs=zs: e.activation(
                    dbx_b[:, zs * Tc:(zs + 1) * Tc], dbx_b[:, zs * Tc:(zs + 1) * Tc],
                    AF.Ln, bias=1.0), inc="ev")
                em.wait("vector", "ev", em.cnt["ev"])
                em.op("vector", lambda e, g=g, zs=zs: e.scalar_tensor_tensor(
                    dt_sb[:, g * Tc:(g + 1) * Tc], da_b[:, zs * Tc:(zs + 1) * Tc],
                    0.0, dbx_b[:, zs * Tc:(zs + 1) * Tc], ALU.max, ALU.add), inc="dv")
                em.drain("vector")
                em.op("vector", lambda e, g=g: e.tensor_mul(
                    xn2[:, g * Tc:(g + 1) * Tc], dt_sb[:, g * Tc:(g + 1) * Tc],
                    xc[:, g * Tc:(g + 1) * Tc]), inc="dv")
                em.drain("vector")
            em.barrier()

            em.phase = "p5"
            b_sc = em.cnt["sc"]
            b_yx = em.cnt["yx"]
            yac_done = lambda ip: b_yx + (ip - 2 * (1 + ip // 32)) + 1 \
                if False else None
            yac_mark = {}
            dvq_gate = 0
            ffn1_q = []
            for m in range(NM1):
                ffn1_q.append(m)
            ffn1_mm = {}
            ffs_marks = []
            a_base_fb0 = base_a + 32

            def emit_ffn1_m(m, fb, dst, a_base):
                mg, mi = divmod(m, 4)
                pb = (pY[2], pY[3])[m % 2]
                em.wait("tensor", "ldw", (a_base + (mg + 1) * 8) * 16)
                if len(ffs_marks) >= 2:
                    em.wait("tensor", "ffs", ffs_marks[-2])
                for k in range(8):
                    sl = (a_base + mg * 8 + k) % NA_SLOT
                    em.op("tensor", lambda e, pb=pb, sl=sl, mi=mi, k=k, fb=fb: e.matmul(
                        pb[:, 0:HB],
                        wpa[:, sl * 512 + mi * 128: sl * 512 + (mi + 1) * 128],
                        xn[:, k * Tc + fb * HB: k * Tc + fb * HB + HB],
                        start=(k == 0), stop=(k == 7)),
                        inc=("mm" if k == 7 else None))
                mmv = em.cnt["mm"]
                if m == 4 * mg + 3:
                    qa.record(mmv)
                    qa.record(mmv)
                em.wait("scalar", "mm", mmv)
                em.op("scalar", lambda e, pb=pb, m=m, dst=dst: e.activation(
                    dst[:, m * HB:(m + 1) * HB], pb[:, 0:HB], AF.Relu,
                    bias=pc[:, 88 + m:89 + m]), inc="ffs")
                ffs_marks.append(em.cnt["ffs"])

            for hf in range(2):
                for qp in range(2):
                    blk = hf * 2 + qp
                    if c >= 1:
                        em.phase = "p9"
                        emit_p9_ts(c - 1, blk)
                        em.phase = "p5"
                    for n in range(D_STATE):
                        i0 = blk * 32 + n * 2
                        if i0 > 0:
                            em.wait("tensor", "sc", b_sc + i0)
                        em.op("tensor", lambda e, n=n: e.matmul(
                            pA[:, 0:Tc], sel16[:, n * 128:(n + 1) * 128],
                            xpjB[:, :], start=True, stop=True), inc="bc")
                        pCb = (pB, pT)[n % 2]
                        if i0 >= 4:
                            em.wait("tensor", "sc", b_sc + i0 - 2)
                        em.op("tensor", lambda e, n=n, pCb=pCb: e.matmul(
                            pCb[:, 0:Tc], sel16[:, n * 128:(n + 1) * 128],
                            xpjC[:, :], start=True, stop=True), inc="bc")
                        bcv = em.cnt["bc"]
                        if n >= 1:
                            if n == 1 and dvq_gate:
                                em.wait("tensor", "dv", dvq_gate)
                            for g2 in range(2):
                                ip = blk * 32 + (n - 1) * 2 + g2
                                em.wait("tensor", "sc", b_sc + ip + 1)
                                em.op("tensor", lambda e, g2=g2, ip=ip, nn=n - 1: e.matmul(
                                    pY[g2][:, 0:Tc], identR[:],
                                    hc_b[:, (ip % 4) * Tc:(ip % 4 + 1) * Tc],
                                    start=(nn == 0), stop=(nn == D_STATE - 1)), inc="yx")
                                yac_mark[ip] = em.cnt["yx"]
                        if n % 2 == 0 and ffn1_q:
                            emit_ffn1_m(ffn1_q.pop(0), 0, scr_r, a_base_fb0)
                        for g2 in range(2):
                            g = hf * 4 + qp * 2 + g2
                            i = blk * 32 + n * 2 + g2
                            sl = i % 3
                            s2 = i % 2
                            s4 = i % 4
                            acol = g * 16 + n
                            if i >= 3:
                                em.wait("scalar", "sc", b_sc + i - 2)
                            em.op("scalar", lambda e, g=g, acol=acol, sl=sl: e.activation(
                                da_b[:, sl * Tc:(sl + 1) * Tc],
                                dt_sb[:, g * Tc:(g + 1) * Tc], AF.Exp,
                                scale=aneg[:, acol:acol + 1]), inc="da")
                            dav = em.cnt["da"]
                            em.wait("vector", "bc", bcv)
                            em.op("vector", lambda e, g=g, s2=s2: e.tensor_mul(
                                dbx_b[:, s2 * Tc:(s2 + 1) * Tc],
                                xn2[:, g * Tc:(g + 1) * Tc], pA[:, 0:Tc]))
                            em.drain("vector")
                            em.wait("vector", "da", dav)
                            em.op("vector", lambda e, sl=sl, s2=s2, i=i: e.tensor_tensor_scan(
                                h32_b[:, s2 * Tc:(s2 + 1) * Tc],
                                da_b[:, sl * Tc:(sl + 1) * Tc],
                                dbx_b[:, s2 * Tc:(s2 + 1) * Tc],
                                state[:, i:i + 1], ALU.mult, ALU.add))
                            em.drain("vector")
                            if i >= 4:
                                em.wait("vector", "yx", yac_mark[i - 4])
                            em.op("vector", lambda e, s2=s2, s4=s4, pCb=pCb: e.tensor_mul(
                                hc_b[:, s4 * Tc:(s4 + 1) * Tc],
                                h32_b[:, s2 * Tc:(s2 + 1) * Tc], pCb[:, 0:Tc]))
                            em.drain("vector")
                            em.op("vector", lambda e, s2=s2, i=i: e.tensor_copy(
                                state[:, i:i + 1],
                                h32_b[:, s2 * Tc + Tc - 1: s2 * Tc + Tc]), inc="sc")
                            em.drain("vector")
                    for g2 in range(2):
                        ip = blk * 32 + 15 * 2 + g2
                        em.wait("tensor", "sc", b_sc + ip + 1)
                        em.op("tensor", lambda e, g2=g2, ip=ip: e.matmul(
                            pY[g2][:, 0:Tc], identR[:],
                            hc_b[:, (ip % 4) * Tc:(ip % 4 + 1) * Tc],
                            start=False, stop=True), inc="yx")
                        yac_mark[ip] = em.cnt["yx"]
                    yx_blk = em.cnt["yx"]
                    for g2 in range(2):
                        g = hf * 4 + qp * 2 + g2
                        em.wait("vector", "yx", yx_blk)
                        em.op("vector", lambda e, g=g: e.tensor_scalar(
                            cacc[:, 0:Tc], xc[:, g * Tc:(g + 1) * Tc],
                            pc[:, 32 + g:33 + g], None, ALU.mult))
                        em.drain("vector")
                        em.op("vector", lambda e, g=g, g2=g2: e.scalar_tensor_tensor(
                            sq[:, 0:Tc], cacc[:, 0:Tc], 1.0,
                            pY[g2][:, 0:Tc], ALU.mult, ALU.add))
                        em.drain("vector")
                        em.op("vector", lambda e, g=g: e.tensor_mul(
                            ysz[:, g * Tc:(g + 1) * Tc], sq[:, 0:Tc],
                            sz[:, g * Tc:(g + 1) * Tc]), inc="dv")
                        em.drain("vector")
                    dvq_gate = em.cnt["dv"]
            em.barrier()

            em.phase = "p7"
            base_b = c * 96
            obank = (pA, pB, pT, pS)

            def emit_out_pass_both(ffs_tgt):
                bb = base_b
                em.wait("tensor", "ffs", ffs_tgt)
                obk = (pA, pB, pT, pS, pY[0], pY[1], pY[2], pY[3])
                def bank_of(fb, dm):
                    return obk[fb * 4 + dm % 4], (0 if dm < 4 else HB)
                for k in range(16):
                    so = bb + 4 * k
                    em.wait("tensor", "ldwb", (so + 4) * 16)
                    for dm in range(NDM):
                        sl = (so + dm // 2) % NB_SLOT
                        for fb, hsrc in ((0, scr_r), (1, xn2)):
                            bank, colo = bank_of(fb, dm)
                            em.op("tensor", lambda e, k=k, dm=dm, bank=bank, colo=colo, sl=sl, hsrc=hsrc: e.matmul(
                                bank[:, colo:colo + HB],
                                wpb[:, sl * 256 + (dm % 2) * 128:sl * 256 + (dm % 2 + 1) * 128],
                                hsrc[:, k * HB:(k + 1) * HB],
                                start=(k == 0 and dm < 4), stop=False),
                                inc=("mm" if (dm == NDM - 1 and fb == 1) else None))
                    qb.record(em.cnt["mm"])
                for k in range(8):
                    so = bb + 64 + 4 * k
                    em.wait("tensor", "ldwb", (so + 4) * 16)
                    for dm in range(NDM):
                        sl = (so + dm // 2) % NB_SLOT
                        for fb in (0, 1):
                            bank, colo = bank_of(fb, dm)
                            em.op("tensor", lambda e, k=k, dm=dm, bank=bank, colo=colo, sl=sl, fb=fb: e.matmul(
                                bank[:, colo:colo + HB],
                                wpb[:, sl * 256 + (dm % 2) * 128:sl * 256 + (dm % 2 + 1) * 128],
                                ysz[:, k * Tc + fb * HB: k * Tc + fb * HB + HB],
                                start=False, stop=(k == 7 and dm >= 4)),
                                inc=("mm" if (dm == NDM - 1 and fb == 1) else None))
                    qb.record(em.cnt["mm"])
                mm_out = em.cnt["mm"]
                em.wait("vector", "mm", mm_out)
                for fb in (0, 1):
                    for dm in range(NDM):
                        bank, colo = bank_of(fb, dm)
                        em.op("vector", lambda e, dm=dm, fb=fb, bank=bank, colo=colo: e.scalar_tensor_tensor(
                            dt_sb[:, dm * Tc + fb * HB: dm * Tc + fb * HB + HB],
                            r1h[:, dm * Tc + fb * HB: dm * Tc + fb * HB + HB],
                            1.0, bank[:, colo:colo + HB], ALU.mult, ALU.add), inc="dv")
                        em.drain("vector")
                return em.cnt["dv"]

            a_base_fb1 = base_a + 64
            for m in range(NM1):
                emit_ffn1_m(m, 1, xn2, a_base_fb1)
            ffs_fb1 = em.cnt["ffs"]
            dv1 = emit_out_pass_both(ffs_fb1)
            em.wait("sync", "dv", dv1)
            em.op("sync", lambda e, c=c: e.dma_start(
                out=op_snd.ap()[c, :, :], in_=dt_sb[:, 0:NDM * Tc]), inc="io", by=16)
            io_op = em.cnt["io"]
            em.wait("gpsimd", "io", io_op)
            em.op("gpsimd", lambda e, c=c: e.collective_compute(
                "AllReduce", ALU.add, replica_groups=GROUPS,
                ins=[op_snd.ap()[c, :, :]], outs=[op_rcv.ap()[c, :, :]]), inc="cc")
            cc_op_marks[c] = em.cnt["cc"]
            em.barrier()
            if c == NCH - 1:
                emit_p9(c)
            if c + 1 < NCH:
                em.wait("sync", "st", em.cnt["st"])
                load_x(c + 1)
                load_w_in(c + 1)
            em.barrier()

        em.wait("sync", "st", em.cnt["st"])

        with nc.Block() as blk:
            @blk.sync
            def _(eng):
                for fn in em.streams["sync"]:
                    fn(eng)

            @blk.gpsimd
            def _(eng):
                for fn in em.streams["gpsimd"]:
                    fn(eng)

            @blk.tensor
            def _(eng):
                for fn in em.streams["tensor"]:
                    fn(eng)

            @blk.vector
            def _(eng):
                for fn in em.streams["vector"]:
                    fn(eng)

            @blk.scalar
            def _(eng):
                for fn in em.streams["scalar"]:
                    fn(eng)

    return nc


NSCAN = 128


def kernel(**inputs):
    x = np.ascontiguousarray(np.asarray(inputs["L0_attn_x"], dtype=np.float32))
    B, L, _ = x.shape
    w_in_full = np.asarray(inputs["L0_attn_W_in"], dtype=np.float32)
    conv_w = np.asarray(inputs["L0_attn_conv_w"], dtype=np.float32)
    conv_b = np.asarray(inputs["L0_attn_conv_b"], dtype=np.float32)
    w_xp_full = np.asarray(inputs["L0_attn_W_xp"], dtype=np.float32)
    w_dt = np.asarray(inputs["L0_attn_W_dt"], dtype=np.float32).reshape(-1)
    b_dt = np.asarray(inputs["L0_attn_b_dt"], dtype=np.float32)
    a_log = np.asarray(inputs["L0_attn_A_log"], dtype=np.float32)
    d_par = np.asarray(inputs["L0_attn_D"], dtype=np.float32)
    w_out_full = np.asarray(inputs["L0_attn_W_out"], dtype=np.float32)
    ln_g = np.asarray(inputs["L0_attn_ln_g"], dtype=np.float32)
    ln_b = np.asarray(inputs["L0_attn_ln_b"], dtype=np.float32)
    w1_full = np.asarray(inputs["L0_ffn_W1"], dtype=np.float32)
    b1 = np.asarray(inputs["L0_ffn_b1"], dtype=np.float32)
    w2_full = np.asarray(inputs["L0_ffn_W2"], dtype=np.float32)
    b2 = np.asarray(inputs["L0_ffn_b2"], dtype=np.float32)

    nc = build_program(L)

    ident = np.eye(128, dtype=np.float32)
    sel16 = np.zeros((16, 2048), dtype=np.float32)
    for n in range(16):
        sel16[n, n * 128:(n + 1) * 128] = 1.0

    in_maps = []
    for r in range(NCORES):
        b = r // 2
        H = r % 2
        ch = slice(H * DH, (H + 1) * DH)
        ih = slice(H * IH, (H + 1) * IH)
        w_in_c = np.concatenate(
            [w_in_full[:, H * DH:(H + 1) * DH],
             w_in_full[:, 2 * DH + H * DH: 2 * DH + (H + 1) * DH]], axis=1)
        pco = np.zeros((128, 104), dtype=np.float32)
        pco[:, 0:8] = ln_g.reshape(8, 128).T
        pco[:, 8:16] = ln_b.reshape(8, 128).T
        pco[:, 16:24] = w_dt[ch].reshape(8, 128).T
        pco[:, 24:32] = b_dt[ch].reshape(8, 128).T
        pco[:, 32:40] = d_par[ch].reshape(8, 128).T
        pco[:, 40:48] = conv_b[ch].reshape(8, 128).T
        pco[:, 48:56] = (0.5 * b2).reshape(8, 128).T
        cw = conv_w[ch].reshape(8, 128, D_CONV)
        for g in range(8):
            pco[:, 56 + g * 4:56 + (g + 1) * 4] = cw[g]
        pco[:, 88:104] = b1[ih].reshape(16, 128).T
        alog_c = np.zeros((128, 128), dtype=np.float32)
        al = a_log[ch].reshape(8, 128, D_STATE)
        for g in range(8):
            alog_c[:, g * 16:(g + 1) * 16] = al[g]
        wxp_pad = np.zeros((DH, 65), dtype=np.float32)
        wxp_pad[:, 0:16] = w_xp_full[ch, 0:16]
        wxp_pad[:, 32:48] = w_xp_full[ch, 16:32]
        wxp_pad[:, 64] = w_xp_full[ch, 32]
        in_maps.append({
            "x": np.ascontiguousarray(x[b]),
            "w_in": np.ascontiguousarray(w_in_c),
            "w1": np.ascontiguousarray(w1_full[:, ih]),
            "w2": np.ascontiguousarray(w2_full[ih, :]),
            "w_out": np.ascontiguousarray(w_out_full[ch, :]),
            "w_xp": wxp_pad,
            "pcons": pco,
            "a_log": alog_c,
            "identc": ident,
            "identrc": ident,
            "sel16c": sel16,
            "onesc": np.ones((128, 1), dtype=np.float32),
            "onesrc": np.ones((128, 1), dtype=np.float32),
        })

    res = run_bass_kernel_spmd(nc, in_maps, core_ids=list(range(NCORES)))
    out = np.empty((B, L, D_MODEL), dtype=np.float32)
    for b in range(B):
        out[b] = res.results[2 * b]["out"]
    return out


# version 19
# speedup vs baseline: 1.0927x; 1.0034x over previous
import os
import ml_dtypes
import numpy as np
from contextlib import ExitStack

EMIT_DRAINS = os.environ.get("BASS_SIM_FP64") == "1" or os.environ.get("KERNEL_SIM_DRAINS") == "1"
HW_DRAINS = set(os.environ.get("KERNEL_HW_DRAINS", "p1").split(","))

import concourse.bass as bass
import concourse.mybir as mybir
from concourse.bass_utils import run_bass_kernel_spmd

F32 = mybir.dt.float32
F32R = mybir.dt.float32r
ALU = mybir.AluOpType
AF = mybir.ActivationFunctionType

D_MODEL = 1024
D_STATE = 16
D_CONV = 4
DH = 1024
IH = 2048
NCORES = 8
EPS = 1e-5
INV_D = 1.0 / D_MODEL

ENGS = ("sync", "gpsimd", "tensor", "vector", "scalar")


class Emitter:
    def __init__(self, nc, es):
        self.nc = nc
        self.streams = {e: [] for e in ENGS}
        self.cnt = {}
        self.sems = {}
        self.nbar = 0
        self.phase = "pro"
        self._es = es

    def sem(self, name):
        self.sems[name] = self._es.enter_context(self.nc.semaphore(name))
        self.cnt[name] = 0

    def op(self, eng, fn, inc=None, by=1):
        if inc is None:
            self.streams[eng].append(fn)
            return None
        s = self.sems[inc]
        self.cnt[inc] += by
        def wrapped(e):
            fn(e).then_inc(s, by)
        self.streams[eng].append(wrapped)
        return self.cnt[inc]

    def wait(self, eng, name, val, force=False):
        if val <= 0 and not force:
            return
        s = self.sems[name]
        self.streams[eng].append(lambda e: e.wait_ge(s, max(val, 0)))

    BAR_ENGS = ("gpsimd", "tensor", "vector", "scalar")

    def barrier(self):
        self.nbar += 1
        tgt = self.nbar * len(self.BAR_ENGS)
        s = self.sems["bar"]
        for eng in self.BAR_ENGS:
            self.streams[eng].append(lambda e: e.nop().then_inc(s, 1))
            self.streams[eng].append(lambda e: e.wait_ge(s, tgt))

    def drain(self, eng, hw=False):
        if EMIT_DRAINS or eng == "scalar" or hw or "all" in HW_DRAINS \
                or self.phase in HW_DRAINS:
            self.streams[eng].append(lambda e: e.drain())


def build_program(L):
    Tc = min(512, L)
    NCH = L // Tc
    HB = Tc // 2
    NG = DH // 128
    NDM = D_MODEL // 128
    NM1 = IH // 128
    NTS = Tc // 128

    NA_SLOT = 8
    NB_SLOT = 8

    nc = bass.Bass("TRN2", target_bir_lowering=False)

    x_hbm = nc.dram_tensor("x", [L, D_MODEL], F32, kind="ExternalInput").ap()
    w_in = nc.dram_tensor("w_in", [D_MODEL, 2 * DH], F32R, kind="ExternalInput").ap()
    w1 = nc.dram_tensor("w1", [D_MODEL, IH], F32R, kind="ExternalInput").ap()
    w2 = nc.dram_tensor("w2", [IH, D_MODEL], F32R, kind="ExternalInput").ap()
    w_out = nc.dram_tensor("w_out", [DH, D_MODEL], F32R, kind="ExternalInput").ap()
    w_xp = nc.dram_tensor("w_xp", [DH, 65], F32R, kind="ExternalInput").ap()
    pcons = nc.dram_tensor("pcons", [128, 104], F32, kind="ExternalInput").ap()
    a_log = nc.dram_tensor("a_log", [128, 128], F32, kind="ExternalInput").ap()
    ident_in = nc.dram_tensor("identc", [128, 128], F32, kind="ExternalInput").ap()
    sel16_in = nc.dram_tensor("sel16c", [16, 2048], F32R, kind="ExternalInput").ap()
    identr_in = nc.dram_tensor("identrc", [128, 128], F32R, kind="ExternalInput").ap()
    identb_in = nc.dram_tensor("identbc", [128, 128], mybir.dt.bfloat16, kind="ExternalInput").ap()
    onesr_in = nc.dram_tensor("onesrc", [128, 1], F32R, kind="ExternalInput").ap()
    ones_in = nc.dram_tensor("onesc", [128, 1], F32, kind="ExternalInput").ap()
    out_hbm = nc.dram_tensor("out", [L, D_MODEL], F32, kind="ExternalOutput").ap()

    xpj_snd = nc.dram_tensor("xpj_snd", [NCH, 65, Tc], F32R)
    xpj_rcv = nc.dram_tensor("xpj_rcv", [NCH, 65, Tc], F32R)
    op_snd = nc.dram_tensor("op_snd", [NCH, 128, NDM * Tc], F32)
    op_rcv = nc.dram_tensor("op_rcv", [NCH, 128, NDM * Tc], F32)
    GROUPS = [[0, 1], [2, 3], [4, 5], [6, 7]]

    es = ExitStack()
    sb = lambda nm, shp, dt=F32: es.enter_context(nc.sbuf_tensor(nm, shp, dt))
    ps = lambda nm, shp: es.enter_context(nc.psum_tensor(nm, shp, F32))

    with es:
        pc = sb("pc", [128, 104])
        aneg = sb("aneg", [128, 128])
        identT = sb("identT", [128, 128])
        identR = sb("identR", [128, 128], F32R)
        identB = sb("identB", [128, 128], mybir.dt.bfloat16)
        sel16 = sb("sel16", [16, 2048], F32R)
        onesT = sb("onesT", [128, 1])
        onesR = sb("onesR", [128, 1], F32R)
        wxp_sb = sb("wxp_sb", [128, 8 * 65], F32R)

        wpa = sb("wpa", [128, NA_SLOT * 512], F32R)
        wpb = sb("wpb", [128, NB_SLOT * 256], F32R)

        tok = sb("tok", [128, 2 * D_MODEL])
        scr_r = sb("scr_r", [128, NDM * Tc], F32R)
        xn = sb("xn", [128, NDM * Tc], F32R)
        xn2 = sb("xn2", [128, NG * Tc], F32R)
        r1h = sb("r1h", [128, NDM * Tc])
        sq = sb("sq", [128, Tc])

        xh_rot = sb("xh_rot", [128, Tc + 3])
        halo = sb("halo", [128, NG * 3])
        cacc = sb("cacc", [128, Tc])
        row1 = cacc[0:1, 0:Tc]
        row2 = sq[0:1, 0:Tc]
        xc = sb("xc", [128, NG * Tc], F32R)
        sz = sb("sz", [128, NG * Tc])
        dt_sb = sb("dt_sb", [128, NG * Tc])
        xpjB = sb("xpjB", [16, Tc], F32R)
        xpjC = sb("xpjC", [16, Tc], F32R)

        da_b = sb("da_b", [128, 3 * Tc])
        dbx_b = sb("dbx_b", [128, 2 * Tc])
        h32_b = sb("h32_b", [128, 2 * Tc])
        hc_b = sb("hc_b", [128, 4 * Tc], mybir.dt.bfloat16)
        state = sb("state", [128, 128])

        cst = sb("cst", [128, 4])
        ysz = sb("ysz", [128, NG * Tc], F32R)
        bst = ysz[0:16, 0:Tc]
        xpjp = ysz[0:65, Tc:2 * Tc]

        pA = ps("pA", [128, 512])
        pB = ps("pB", [128, 512])
        pT = ps("pT", [128, 512])
        pS = ps("pS", [128, 512])
        pY = [ps(f"pY{i}", [128, 512]) for i in range(4)]

        nc.const_aps.register(F32, EPS, cst[:, 0:1])
        nc.const_aps.register(F32, 1.0, cst[:, 1:2])
        nc.const_aps.register(F32, -0.5, cst[:, 2:3])
        nc.const_aps.register(F32, -1.0, cst[:, 3:4])

        em = Emitter(nc, es)
        for s in ("bar", "ldw", "ldwb", "ldx", "lda", "mm", "ev", "ev2", "dv", "da",
                  "bc", "sc", "hc", "dbxs", "yx", "cc", "io", "st", "ffs"):
            em.sem(s)

        for t, src in ((pc, pcons), (aneg, a_log), (identT, ident_in),
                       (identR, identr_in), (identB, identb_in),
                       (sel16, sel16_in)):
            em.op("sync", lambda e, t=t, src=src: e.dma_start(out=t[:], in_=src),
                  inc="lda", by=16)
        em.op("sync", lambda e: e.dma_start(out=onesT[:], in_=ones_in),
              inc="lda", by=16)
        em.op("sync", lambda e: e.dma_start(out=onesR[:], in_=onesr_in[:, 0:1]),
              inc="lda", by=16)
        em.op("sync", lambda e: e.dma_start(
            out=wxp_sb[:, 0:8 * 65].rearrange("p (g c) -> p g c", c=65),
            in_=w_xp.rearrange("(g p) c -> p g c", p=128)), inc="lda", by=16)
        v_lda = em.cnt["lda"]
        em.wait("scalar", "lda", v_lda)
        em.op("scalar", lambda e: e.activation(aneg[:], aneg[:], AF.Exp), inc="ev")
        ev0 = em.cnt["ev"]
        em.wait("vector", "ev", ev0)
        em.op("vector", lambda e: e.tensor_scalar(aneg[:], aneg[:], -1.0, None, ALU.mult), inc="dv")
        em.drain("vector")
        em.op("vector", lambda e: e.memset(state[:], 0.0), inc="dv")
        em.drain("vector")
        em.op("vector", lambda e: e.memset(halo[:], 0.0), inc="dv")
        em.drain("vector")
        em.op("vector", lambda e: e.memset(cacc[:], 0.0), inc="dv")
        em.drain("vector")
        em.op("vector", lambda e: e.tensor_copy(ysz[0:16, 0:Tc], cacc[0:16, 0:Tc]), inc="dv")
        em.drain("vector")
        em.op("vector", lambda e: e.memset(cst[:, 0:1], EPS), inc="dv")
        em.op("vector", lambda e: e.memset(cst[:, 1:2], 1.0), inc="dv")
        em.op("vector", lambda e: e.memset(cst[:, 2:3], -0.5), inc="dv")
        em.op("vector", lambda e: e.memset(cst[:, 3:4], -1.0), inc="dv")
        em.drain("vector")
        em.wait("scalar", "dv", em.cnt["dv"])
        em.wait("tensor", "lda", v_lda)
        em.wait("tensor", "dv", em.cnt["dv"])

        class LoadQ:
            def __init__(self, pool, sem_name, nslot, width=512):
                self.pool = pool
                self.sem = sem_name
                self.width = width
                self.ring = nslot // 4
                self.pending = []
                self.n_loaded = 0
                self.cons = []

            def add(self, w, kslice, cslice):
                self.pending.append((w, kslice, cslice))

            def record(self, mm_val):
                self.cons.append(mm_val)
                self.pump()

            def pump(self):
                while self.pending:
                    g_ = self.n_loaded // 4
                    if g_ - self.ring >= len(self.cons):
                        break
                    if self.n_loaded % 4 == 0:
                        if g_ >= self.ring:
                            em.wait("sync", "mm", self.cons[g_ - self.ring])
                        if g_ >= 1:
                            em.wait("sync", self.sem, self.n_loaded * 16)
                    w, ks, cs = self.pending.pop(0)
                    s_ = self.n_loaded % (self.ring * 4)
                    wd = self.width
                    em.op("sync", lambda e, s_=s_, w=w, ks=ks, cs=cs, wd=wd: e.dma_start(
                        out=self.pool[:, s_ * wd:(s_ + 1) * wd],
                        in_=w[ks, cs]), inc=self.sem, by=16)
                    self.n_loaded += 1

        qa = LoadQ(wpa, "ldw", NA_SLOT)
        qb = LoadQ(wpb, "ldwb", NB_SLOT, width=256)

        def load_w_in(_c):
            for mg in range(4):
                for k in range(8):
                    qa.add(w_in, slice(k * 128, (k + 1) * 128),
                           slice(mg * 512, (mg + 1) * 512))
            qa.pump()

        def load_w1(_c):
            for mg in range(4):
                for k in range(8):
                    qa.add(w1, slice(k * 128, (k + 1) * 128),
                           slice(mg * 512, (mg + 1) * 512))
            qa.pump()

        def load_w2_wout(_c):
            for k in range(16):
                for q in range(4):
                    qb.add(w2, slice(k * 128, (k + 1) * 128),
                           slice(q * 256, (q + 1) * 256))
            for k in range(8):
                for q in range(4):
                    qb.add(w_out, slice(k * 128, (k + 1) * 128),
                           slice(q * 256, (q + 1) * 256))
            qb.pump()

        class XQ:
            def __init__(self):
                self.pending = []
                self.n_loaded = 0
                self.cons = []

            def add(self, c, ts):
                self.pending.append((c, ts))

            def record(self, mm_val):
                self.cons.append(mm_val)
                self.pump()

            def pump(self):
                while self.pending:
                    if self.n_loaded - 2 >= len(self.cons):
                        break
                    if self.n_loaded >= 2:
                        em.wait("sync", "mm", self.cons[self.n_loaded - 2])
                    if self.n_loaded >= 1:
                        em.wait("sync", "ldx", self.n_loaded * 16)
                    c_, ts_ = self.pending.pop(0)
                    sl_ = self.n_loaded % 2
                    em.op("sync", lambda e, c_=c_, ts_=ts_, sl_=sl_: e.dma_start(
                        out=tok[:, sl_ * D_MODEL:(sl_ + 1) * D_MODEL],
                        in_=x_hbm[c_ * Tc + ts_ * 128: c_ * Tc + (ts_ + 1) * 128, :]),
                        inc="ldx", by=16)
                    self.n_loaded += 1

        xq = XQ()

        def load_x(c):
            for ts_ in range(NTS):
                xq.add(c, ts_)
            xq.pump()

        load_x(0)
        load_w_in(0)
        em.barrier()

        st_marks = []
        cc_op_marks = {}

        def emit_p9_ts(cp, ts_):
            if ts_ == 0:
                em.wait("sync", "cc", cc_op_marks[cp])
            em.wait("sync", "io", em.cnt["io"])
            if st_marks:
                em.wait("sync", "st", st_marks[-1])
            em.op("sync", lambda e, cp=cp, ts_=ts_: e.dma_start(
                out=tok[:, 0:D_MODEL].rearrange("p (dm c) -> p dm c", c=128),
                in_=op_rcv.ap()[cp, :, :].rearrange(
                    "p (dm t) -> p dm t", t=Tc)[:, :, ts_ * 128:(ts_ + 1) * 128]),
                inc="io", by=16)
            io_ts = em.cnt["io"]
            em.wait("tensor", "io", io_ts)
            for half in range(2):
                if not (ts_ == 0 and half == 0):
                    em.wait("tensor", "ev", em.cnt["ev"])
                for dmq in range(4):
                    dm = half * 4 + dmq
                    em.op("tensor", lambda e, dm=dm, dmq=dmq: e.matmul(
                        pS[:, dmq * 128:(dmq + 1) * 128],
                        tok[:, dm * 128:(dm + 1) * 128],
                        identT[:], start=(dmq == 0), stop=(dmq == 3),
                        is_transpose=True),
                        inc=("mm" if dmq == 3 else None))
                mmv = em.cnt["mm"]
                em.wait("scalar", "mm", mmv)
                if len(st_marks) >= 1:
                    em.wait("scalar", "st", st_marks[-1])
                em.op("scalar", lambda e, half=half: e.copy(
                    tok[:, D_MODEL + half * 512: D_MODEL + (half + 1) * 512],
                    pS[:, 0:512]), inc="ev")
            ev_ts = em.cnt["ev"]
            em.wait("sync", "ev", ev_ts)
            em.op("sync", lambda e, cp=cp, ts_=ts_: e.dma_start(
                out=out_hbm[cp * Tc + ts_ * 128: cp * Tc + (ts_ + 1) * 128, :],
                in_=tok[:, D_MODEL:2 * D_MODEL]), inc="st", by=16)
            st_marks.append(em.cnt["st"])

        def emit_p9(cp):
            for ts_ in range(NTS):
                emit_p9_ts(cp, ts_)
            return em.cnt["mm"]

        for c in range(NCH):
            em.phase = "p1"
            for ts_ in range(NTS):
                gi = c * NTS + ts_
                sl_ = gi % 2
                bks = ((pA, pB), (pY[0], pY[1]))[ts_ % 2]
                em.wait("tensor", "ldx", (gi + 1) * 16)
                if ts_ >= 2:
                    em.wait("tensor", "ev", em.cnt["ev"])
                for dm in range(NDM):
                    bk = bks[dm // 4]
                    em.op("tensor", lambda e, bk=bk, sl_=sl_, dm=dm: e.matmul(
                        bk[:, (dm % 4) * 128:(dm % 4 + 1) * 128],
                        tok[:, sl_ * D_MODEL + dm * 128: sl_ * D_MODEL + (dm + 1) * 128],
                        identT[:], start=(dm % 4 == 0), stop=(dm % 4 == 3),
                        is_transpose=True),
                        inc=("mm" if dm == NDM - 1 else None))
                mmv = em.cnt["mm"]
                xq.record(mmv)
                em.wait("scalar", "mm", mmv)
                for dm in range(NDM):
                    bk = bks[dm // 4]
                    em.op("scalar", lambda e, bk=bk, dm=dm, ts_=ts_: e.copy(
                        scr_r[:, dm * Tc + ts_ * 128: dm * Tc + (ts_ + 1) * 128],
                        bk[:, (dm % 4) * 128:(dm % 4 + 1) * 128]), inc="ev")
            ev_xfm = em.cnt["ev"]
            em.wait("tensor", "ev", ev_xfm)
            for dm in range(NDM):
                em.op("tensor", lambda e, dm=dm: e.matmul(
                    pT[0:1, 0:Tc], onesR[:], scr_r[:, dm * Tc:(dm + 1) * Tc],
                    start=(dm == 0), stop=(dm == NDM - 1)),
                    inc=("mm" if dm == NDM - 1 else None))
            sq_mm = []
            for dm in range(NDM):
                em.wait("scalar", "ev", ev_xfm)
                if dm >= 1:
                    em.wait("scalar", "mm", sq_mm[dm - 1])
                em.op("scalar", lambda e, dm=dm: e.activation(
                    sq[:, 0:Tc],
                    scr_r[:, dm * Tc:(dm + 1) * Tc], AF.Square), inc="ev")
                em.drain("scalar")
                em.wait("tensor", "ev", em.cnt["ev"])
                em.op("tensor", lambda e, dm=dm: e.matmul(
                    pT[32:33, 0:Tc], onesT[:], sq[:, 0:Tc],
                    start=(dm == 0), stop=(dm == NDM - 1)), inc="mm")
                sq_mm.append(em.cnt["mm"])
            mm_st = em.cnt["mm"]

            def row_stats_bcast(mm_wait, psrc0, psrc1):
                em.wait("vector", "mm", mm_wait)
                em.op("vector", lambda e: e.tensor_scalar(
                    row1, psrc1, INV_D, None, ALU.mult), inc="dv")
                em.drain("vector")
                em.op("vector", lambda e: e.tensor_scalar(
                    bst[0:1, :], psrc0, INV_D, None, ALU.mult), inc="dv")
                em.drain("vector")
                dv_mu = em.cnt["dv"]
                em.wait("tensor", "dv", dv_mu)
                em.op("tensor", lambda e: e.matmul(
                    pA[:, 0:Tc], sel16[:, 0:128], bst[:, 0:Tc], start=True, stop=True), inc="mm")
                mm_mu = em.cnt["mm"]
                em.op("vector", lambda e: e.tensor_mul(
                    row2, bst[0:1, :], bst[0:1, :]), inc="dv")
                em.drain("vector")
                em.op("vector", lambda e: e.tensor_sub(row1, row1, row2), inc="dv")
                em.drain("vector")
                dvv = em.cnt["dv"]
                em.wait("scalar", "dv", dvv)
                em.op("scalar", lambda e: e.activation(row2, row1, AF.Ln, bias=EPS), inc="ev")
                em.drain("scalar")
                em.op("scalar", lambda e: e.activation(row2, row2, AF.Exp, scale=-0.5), inc="ev")
                evv = em.cnt["ev"]
                em.wait("vector", "ev", evv)
                em.wait("vector", "mm", mm_mu)
                em.op("vector", lambda e: e.tensor_copy(bst[0:1, :], row2), inc="dv")
                em.drain("vector")
                dv_r = em.cnt["dv"]
                em.wait("tensor", "dv", dv_r)
                em.op("tensor", lambda e: e.matmul(
                    pB[:, 0:Tc], sel16[:, 0:128], bst[:, 0:Tc], start=True, stop=True), inc="mm")
                return em.cnt["mm"]

            mm_bc1 = row_stats_bcast(mm_st, pT[0:1, 0:Tc], pT[32:33, 0:Tc])
            em.wait("vector", "mm", mm_bc1)
            for dm in range(NDM):
                em.op("vector", lambda e, dm=dm: e.tensor_sub(
                    cacc[:, 0:Tc], scr_r[:, dm * Tc:(dm + 1) * Tc], pA[:, 0:Tc]))
                em.drain("vector")
                em.op("vector", lambda e, dm=dm: e.tensor_mul(
                    xn[:, dm * Tc:(dm + 1) * Tc], cacc[:, 0:Tc], pB[:, 0:Tc]), inc="dv")
                em.drain("vector")
            dv_xn = em.cnt["dv"]
            em.wait("tensor", "dv", dv_xn)
            for dm in range(NDM):
                em.op("tensor", lambda e, dm=dm: e.matmul(
                    pS[0:1, 0:Tc], onesR[:], xn[:, dm * Tc:(dm + 1) * Tc],
                    start=(dm == 0), stop=(dm == NDM - 1)),
                    inc=("mm" if dm == NDM - 1 else None))
            sq_mm2 = []
            for dm in range(NDM):
                em.wait("scalar", "dv", dv_xn)
                if dm >= 1:
                    em.wait("scalar", "mm", sq_mm2[dm - 1])
                em.op("scalar", lambda e, dm=dm: e.activation(
                    sq[:, 0:Tc],
                    xn[:, dm * Tc:(dm + 1) * Tc], AF.Square), inc="ev")
                em.drain("scalar")
                em.wait("tensor", "ev", em.cnt["ev"])
                em.op("tensor", lambda e, dm=dm: e.matmul(
                    pS[32:33, 0:Tc], onesT[:], sq[:, 0:Tc],
                    start=(dm == 0), stop=(dm == NDM - 1)), inc="mm")
                sq_mm2.append(em.cnt["mm"])
            mm_st2 = em.cnt["mm"]
            mm_bc2 = row_stats_bcast(mm_st2, pS[0:1, 0:Tc], pS[32:33, 0:Tc])
            em.wait("vector", "mm", mm_bc2)
            for dm in range(NDM):
                em.op("vector", lambda e, dm=dm: e.tensor_sub(
                    cacc[:, 0:Tc], xn[:, dm * Tc:(dm + 1) * Tc], pA[:, 0:Tc]))
                em.drain("vector")
                em.op("vector", lambda e, dm=dm: e.tensor_mul(
                    sq[:, 0:Tc], cacc[:, 0:Tc], pB[:, 0:Tc]))
                em.drain("vector")
                em.op("vector", lambda e, dm=dm: e.tensor_scalar(
                    xn2[:, dm * Tc:(dm + 1) * Tc], sq[:, 0:Tc],
                    pc[:, dm:dm + 1], pc[:, 8 + dm:9 + dm], ALU.mult, ALU.add), inc="dv")
                em.drain("vector")
                em.op("vector", lambda e, dm=dm: e.tensor_add(
                    cacc[:, 0:Tc], scr_r[:, dm * Tc:(dm + 1) * Tc],
                    xn[:, dm * Tc:(dm + 1) * Tc]))
                em.drain("vector")
                em.op("vector", lambda e, dm=dm: e.tensor_scalar(
                    r1h[:, dm * Tc:(dm + 1) * Tc], cacc[:, 0:Tc],
                    0.5, pc[:, 48 + dm:49 + dm], ALU.mult, ALU.add), inc="dv")
                em.drain("vector")
            em.barrier()

            em.phase = "p2"
            base_a = c * 96
            for m in range(2 * NG):
                mg, mi = divmod(m, 4)
                pb = (pA, pB)[m % 2]
                em.wait("tensor", "ldw", (base_a + (mg + 1) * 8) * 16)
                if m >= 2:
                    em.wait("tensor", "ev", em.cnt["ev"])
                    em.wait("tensor", "ev2", em.cnt["ev2"])
                for k in range(8):
                    sl = (base_a + mg * 8 + k) % NA_SLOT
                    src = xn2
                    em.op("tensor", lambda e, pb=pb, sl=sl, mi=mi, k=k, src=src: e.matmul(
                        pb[:, 0:Tc],
                        wpa[:, sl * 512 + mi * 128: sl * 512 + (mi + 1) * 128],
                        src[:, k * Tc:(k + 1) * Tc],
                        start=(k == 0), stop=(k == 7)),
                        inc=("mm" if k == 7 else None))
                mmv = em.cnt["mm"]
                if m == 4 * mg + 3:
                    qa.record(mmv)
                    qa.record(mmv)
                em.wait("scalar", "mm", mmv)
                if m < NG:
                    g = m
                    hb = 0
                    em.wait("scalar", "dv", em.cnt["dv"]) if g >= 1 else None
                    em.op("scalar", lambda e, pb=pb, hb=hb: e.copy(
                        xh_rot[:, hb + 3:hb + 3 + Tc], pb[:, 0:Tc]), inc="ev")
                    ev_xh = em.cnt["ev"]
                    em.wait("vector", "ev", ev_xh)
                    em.op("vector", lambda e, hb=hb, g=g: e.tensor_copy(
                        xh_rot[:, hb:hb + 3], halo[:, g * 3:(g + 1) * 3]))
                    em.drain("vector")
                    em.op("vector", lambda e, hb=hb, g=g: e.tensor_scalar(
                        cacc[:, 0:Tc], xh_rot[:, hb:hb + Tc],
                        pc[:, 56 + g * 4:57 + g * 4], None, ALU.mult))
                    em.drain("vector")
                    for kk in range(1, D_CONV):
                        em.op("vector", lambda e, hb=hb, g=g, kk=kk: e.scalar_tensor_tensor(
                            cacc[:, 0:Tc], xh_rot[:, hb + kk:hb + kk + Tc],
                            pc[:, 56 + g * 4 + kk:57 + g * 4 + kk],
                            cacc[:, 0:Tc], ALU.mult, ALU.add))
                        em.drain("vector")
                    em.op("vector", lambda e, g=g: e.tensor_scalar(
                        sq[:, 0:Tc], cacc[:, 0:Tc], 1.0,
                        pc[:, 40 + g:41 + g], ALU.mult, ALU.add))
                    em.drain("vector")
                    em.op("vector", lambda e, hb=hb, g=g: e.tensor_copy(
                        halo[:, g * 3:(g + 1) * 3], xh_rot[:, hb + Tc:hb + Tc + 3]),
                        inc="dv")
                    em.drain("vector")
                    em.wait("scalar", "dv", em.cnt["dv"])
                    em.wait("scalar", "ev2", em.cnt["ev2"])
                    em.op("scalar", lambda e: e.activation(
                        da_b[:, 0:Tc], sq[:, 0:Tc], AF.Tanh, scale=0.5),
                        inc="ev")
                    em.drain("scalar")
                    em.wait("vector", "ev", em.cnt["ev"])
                    em.op("vector", lambda e: e.tensor_scalar(
                        da_b[:, Tc:2 * Tc], da_b[:, 0:Tc], 0.5, 0.5,
                        ALU.mult, ALU.add))
                    em.drain("vector")
                    em.op("vector", lambda e, g=g: e.tensor_mul(
                        xc[:, g * Tc:(g + 1) * Tc], sq[:, 0:Tc],
                        da_b[:, Tc:2 * Tc]), inc="ev2")
                    em.drain("vector")
                    em.wait("tensor", "ev2", em.cnt["ev2"])
                    em.op("tensor", lambda e, g=g: e.matmul(
                        pY[2][0:65, 0:Tc], wxp_sb[:, g * 65:(g + 1) * 65],
                        xc[:, g * Tc:(g + 1) * Tc],
                        start=(g == 0), stop=(g == NG - 1)),
                        inc=("mm" if g == NG - 1 else None))
                    if g == NG - 1:
                        mm_xpj = em.cnt["mm"]
                        em.wait("scalar", "mm", mm_xpj)
                        em.op("scalar", lambda e: e.copy(xpjp, pY[2][0:65, 0:Tc]), inc="ev")
                        ev_xpj = em.cnt["ev"]
                        em.wait("sync", "ev", ev_xpj)
                        em.op("sync", lambda e, c=c: e.dma_start(
                            out=xpj_snd.ap()[c, :, :], in_=xpjp), inc="io", by=16)
                        io_snd = em.cnt["io"]
                        em.wait("gpsimd", "io", io_snd)
                        em.op("gpsimd", lambda e, c=c: e.collective_compute(
                            "AllReduce", ALU.add, replica_groups=GROUPS,
                            ins=[xpj_snd.ap()[c, :, :]], outs=[xpj_rcv.ap()[c, :, :]]), inc="cc")
                        cc_xpj = em.cnt["cc"]
                else:
                    g = m - NG
                    em.wait("scalar", "ev2", em.cnt["ev2"])
                    em.op("scalar", lambda e, pb=pb: e.activation(
                        da_b[:, 0:Tc], pb[:, 0:Tc], AF.Tanh, scale=0.5), inc="ev")
                    em.drain("scalar")
                    em.wait("vector", "ev", em.cnt["ev"])
                    em.op("vector", lambda e: e.tensor_scalar(
                        da_b[:, Tc:2 * Tc], da_b[:, 0:Tc], 0.5, 0.5,
                        ALU.mult, ALU.add))
                    em.drain("vector")
                    em.op("vector", lambda e, pb=pb, g=g: e.tensor_mul(
                        sz[:, g * Tc:(g + 1) * Tc], pb[:, 0:Tc],
                        da_b[:, Tc:2 * Tc]), inc="ev2")
                    em.drain("vector")
            ev_p2 = em.cnt["ev"]
            em.barrier()

            em.phase = "p3"
            load_w1(c)
            load_w1(c)
            load_w2_wout(c)

            em.wait("sync", "cc", cc_xpj)
            em.op("sync", lambda e, c=c: e.dma_start(
                out=xpjB[:, :], in_=xpj_rcv.ap()[c, 0:16, :]), inc="io", by=16)
            em.op("sync", lambda e, c=c: e.dma_start(
                out=xpjC[:, :], in_=xpj_rcv.ap()[c, 32:48, :]), inc="io", by=16)
            em.op("sync", lambda e, c=c: e.dma_start(
                out=bst[0:1, :], in_=xpj_rcv.ap()[c, 64:65, :]), inc="io", by=16)
            io_rcv = em.cnt["io"]
            em.wait("tensor", "io", io_rcv)
            em.op("tensor", lambda e: e.matmul(
                pT[:, 0:Tc], sel16[:, 0:128], bst[:, 0:Tc],
                start=True, stop=True), inc="mm")
            mm_xi = em.cnt["mm"]
            for g in range(NG):
                zs = g % 2
                em.wait("vector", "mm", mm_xi)
                em.op("vector", lambda e, g=g, zs=zs: e.tensor_scalar(
                    da_b[:, zs * Tc:(zs + 1) * Tc], pT[:, 0:Tc],
                    pc[:, 16 + g:17 + g], pc[:, 24 + g:25 + g],
                    ALU.mult, ALU.add), inc="dv")
                em.drain("vector")
                em.op("vector", lambda e, zs=zs: e.scalar_tensor_tensor(
                    dbx_b[:, zs * Tc:(zs + 1) * Tc], da_b[:, zs * Tc:(zs + 1) * Tc],
                    -1.0, da_b[:, zs * Tc:(zs + 1) * Tc], ALU.mult, ALU.max), inc="dv")
                em.drain("vector")
                em.wait("scalar", "dv", em.cnt["dv"])
                em.op("scalar", lambda e, zs=zs: e.activation(
                    dbx_b[:, zs * Tc:(zs + 1) * Tc], dbx_b[:, zs * Tc:(zs + 1) * Tc],
                    AF.Exp, scale=-1.0), inc="ev")
                em.drain("scalar")
                em.op("scalar", lambda e, zs=zs: e.activation(
                    dbx_b[:, zs * Tc:(zs + 1) * Tc], dbx_b[:, zs * Tc:(zs + 1) * Tc],
                    AF.Ln, bias=1.0), inc="ev")
                em.wait("vector", "ev", em.cnt["ev"])
                em.op("vector", lambda e, g=g, zs=zs: e.scalar_tensor_tensor(
                    dt_sb[:, g * Tc:(g + 1) * Tc], da_b[:, zs * Tc:(zs + 1) * Tc],
                    0.0, dbx_b[:, zs * Tc:(zs + 1) * Tc], ALU.max, ALU.add), inc="dv")
                em.drain("vector")
                em.op("vector", lambda e, g=g: e.tensor_mul(
                    xn2[:, g * Tc:(g + 1) * Tc], dt_sb[:, g * Tc:(g + 1) * Tc],
                    xc[:, g * Tc:(g + 1) * Tc]), inc="dv")
                em.drain("vector")
            em.barrier()

            em.phase = "p5"
            b_sc = em.cnt["sc"]
            b_yx = em.cnt["yx"]
            yac_done = lambda ip: b_yx + (ip - 2 * (1 + ip // 32)) + 1 \
                if False else None
            yac_mark = {}
            dvq_gate = 0
            ffn1_q = []
            for m in range(NM1):
                ffn1_q.append(m)
            ffn1_mm = {}
            ffs_marks = []
            a_base_fb0 = base_a + 32

            def emit_ffn1_m(m, fb, dst, a_base):
                mg, mi = divmod(m, 4)
                pb = (pY[2], pY[3])[m % 2]
                em.wait("tensor", "ldw", (a_base + (mg + 1) * 8) * 16)
                if len(ffs_marks) >= 2:
                    em.wait("tensor", "ffs", ffs_marks[-2])
                for k in range(8):
                    sl = (a_base + mg * 8 + k) % NA_SLOT
                    em.op("tensor", lambda e, pb=pb, sl=sl, mi=mi, k=k, fb=fb: e.matmul(
                        pb[:, 0:HB],
                        wpa[:, sl * 512 + mi * 128: sl * 512 + (mi + 1) * 128],
                        xn[:, k * Tc + fb * HB: k * Tc + fb * HB + HB],
                        start=(k == 0), stop=(k == 7)),
                        inc=("mm" if k == 7 else None))
                mmv = em.cnt["mm"]
                if m == 4 * mg + 3:
                    qa.record(mmv)
                    qa.record(mmv)
                em.wait("scalar", "mm", mmv)
                em.op("scalar", lambda e, pb=pb, m=m, dst=dst: e.activation(
                    dst[:, m * HB:(m + 1) * HB], pb[:, 0:HB], AF.Relu,
                    bias=pc[:, 88 + m:89 + m]), inc="ffs")
                ffs_marks.append(em.cnt["ffs"])

            for hf in range(2):
                for qp in range(2):
                    blk = hf * 2 + qp
                    if c >= 1:
                        em.phase = "p9"
                        emit_p9_ts(c - 1, blk)
                        em.phase = "p5"
                    for n in range(D_STATE):
                        i0 = blk * 32 + n * 2
                        if i0 > 0:
                            em.wait("tensor", "sc", b_sc + i0)
                        em.op("tensor", lambda e, n=n: e.matmul(
                            pA[:, 0:Tc], sel16[:, n * 128:(n + 1) * 128],
                            xpjB[:, :], start=True, stop=True), inc="bc")
                        pCb = (pB, pT)[n % 2]
                        if i0 >= 4:
                            em.wait("tensor", "sc", b_sc + i0 - 2)
                        em.op("tensor", lambda e, n=n, pCb=pCb: e.matmul(
                            pCb[:, 0:Tc], sel16[:, n * 128:(n + 1) * 128],
                            xpjC[:, :], start=True, stop=True), inc="bc")
                        bcv = em.cnt["bc"]
                        if n >= 1:
                            if n == 1 and dvq_gate:
                                em.wait("tensor", "dv", dvq_gate)
                            for g2 in range(2):
                                ip = blk * 32 + (n - 1) * 2 + g2
                                em.wait("tensor", "sc", b_sc + ip + 1)
                                em.op("tensor", lambda e, g2=g2, ip=ip, nn=n - 1: e.matmul(
                                    pY[g2][:, 0:Tc], identB[:],
                                    hc_b[:, (ip % 4) * Tc:(ip % 4 + 1) * Tc],
                                    start=(nn == 0), stop=(nn == D_STATE - 1)), inc="yx")
                                yac_mark[ip] = em.cnt["yx"]
                        if n % 2 == 0 and ffn1_q:
                            emit_ffn1_m(ffn1_q.pop(0), 0, scr_r, a_base_fb0)
                        for g2 in range(2):
                            g = hf * 4 + qp * 2 + g2
                            i = blk * 32 + n * 2 + g2
                            sl = i % 3
                            s2 = i % 2
                            s4 = i % 4
                            acol = g * 16 + n
                            if i >= 3:
                                em.wait("scalar", "sc", b_sc + i - 2)
                            em.op("scalar", lambda e, g=g, acol=acol, sl=sl: e.activation(
                                da_b[:, sl * Tc:(sl + 1) * Tc],
                                dt_sb[:, g * Tc:(g + 1) * Tc], AF.Exp,
                                scale=aneg[:, acol:acol + 1]), inc="da")
                            dav = em.cnt["da"]
                            em.wait("vector", "bc", bcv)
                            em.op("vector", lambda e, g=g, s2=s2: e.tensor_mul(
                                dbx_b[:, s2 * Tc:(s2 + 1) * Tc],
                                xn2[:, g * Tc:(g + 1) * Tc], pA[:, 0:Tc]))
                            em.drain("vector")
                            em.wait("vector", "da", dav)
                            em.op("vector", lambda e, sl=sl, s2=s2, i=i: e.tensor_tensor_scan(
                                h32_b[:, s2 * Tc:(s2 + 1) * Tc],
                                da_b[:, sl * Tc:(sl + 1) * Tc],
                                dbx_b[:, s2 * Tc:(s2 + 1) * Tc],
                                state[:, i:i + 1], ALU.mult, ALU.add))
                            em.drain("vector")
                            if i >= 4:
                                em.wait("vector", "yx", yac_mark[i - 4])
                            em.op("vector", lambda e, s2=s2, s4=s4, pCb=pCb: e.tensor_mul(
                                hc_b[:, s4 * Tc:(s4 + 1) * Tc],
                                h32_b[:, s2 * Tc:(s2 + 1) * Tc], pCb[:, 0:Tc]))
                            em.drain("vector")
                            em.op("vector", lambda e, s2=s2, i=i: e.tensor_copy(
                                state[:, i:i + 1],
                                h32_b[:, s2 * Tc + Tc - 1: s2 * Tc + Tc]), inc="sc")
                            em.drain("vector")
                    for g2 in range(2):
                        ip = blk * 32 + 15 * 2 + g2
                        em.wait("tensor", "sc", b_sc + ip + 1)
                        em.op("tensor", lambda e, g2=g2, ip=ip: e.matmul(
                            pY[g2][:, 0:Tc], identB[:],
                            hc_b[:, (ip % 4) * Tc:(ip % 4 + 1) * Tc],
                            start=False, stop=True), inc="yx")
                        yac_mark[ip] = em.cnt["yx"]
                    yx_blk = em.cnt["yx"]
                    for g2 in range(2):
                        g = hf * 4 + qp * 2 + g2
                        em.wait("vector", "yx", yx_blk)
                        em.op("vector", lambda e, g=g: e.tensor_scalar(
                            cacc[:, 0:Tc], xc[:, g * Tc:(g + 1) * Tc],
                            pc[:, 32 + g:33 + g], None, ALU.mult))
                        em.drain("vector")
                        em.op("vector", lambda e, g=g, g2=g2: e.scalar_tensor_tensor(
                            sq[:, 0:Tc], cacc[:, 0:Tc], 1.0,
                            pY[g2][:, 0:Tc], ALU.mult, ALU.add))
                        em.drain("vector")
                        em.op("vector", lambda e, g=g: e.tensor_mul(
                            ysz[:, g * Tc:(g + 1) * Tc], sq[:, 0:Tc],
                            sz[:, g * Tc:(g + 1) * Tc]), inc="dv")
                        em.drain("vector")
                    dvq_gate = em.cnt["dv"]
            em.barrier()

            em.phase = "p7"
            base_b = c * 96
            obank = (pA, pB, pT, pS)

            def emit_out_pass_both(ffs_tgt):
                bb = base_b
                em.wait("tensor", "ffs", ffs_tgt)
                obk = (pA, pB, pT, pS, pY[0], pY[1], pY[2], pY[3])
                def bank_of(fb, dm):
                    return obk[fb * 4 + dm % 4], (0 if dm < 4 else HB)
                for k in range(16):
                    so = bb + 4 * k
                    em.wait("tensor", "ldwb", (so + 4) * 16)
                    for dm in range(NDM):
                        sl = (so + dm // 2) % NB_SLOT
                        for fb, hsrc in ((0, scr_r), (1, xn2)):
                            bank, colo = bank_of(fb, dm)
                            em.op("tensor", lambda e, k=k, dm=dm, bank=bank, colo=colo, sl=sl, hsrc=hsrc: e.matmul(
                                bank[:, colo:colo + HB],
                                wpb[:, sl * 256 + (dm % 2) * 128:sl * 256 + (dm % 2 + 1) * 128],
                                hsrc[:, k * HB:(k + 1) * HB],
                                start=(k == 0 and dm < 4), stop=False),
                                inc=("mm" if (dm == NDM - 1 and fb == 1) else None))
                    qb.record(em.cnt["mm"])
                for k in range(8):
                    so = bb + 64 + 4 * k
                    em.wait("tensor", "ldwb", (so + 4) * 16)
                    for dm in range(NDM):
                        sl = (so + dm // 2) % NB_SLOT
                        for fb in (0, 1):
                            bank, colo = bank_of(fb, dm)
                            em.op("tensor", lambda e, k=k, dm=dm, bank=bank, colo=colo, sl=sl, fb=fb: e.matmul(
                                bank[:, colo:colo + HB],
                                wpb[:, sl * 256 + (dm % 2) * 128:sl * 256 + (dm % 2 + 1) * 128],
                                ysz[:, k * Tc + fb * HB: k * Tc + fb * HB + HB],
                                start=False, stop=(k == 7 and dm >= 4)),
                                inc=("mm" if (dm == NDM - 1 and fb == 1) else None))
                    qb.record(em.cnt["mm"])
                mm_out = em.cnt["mm"]
                em.wait("vector", "mm", mm_out)
                for fb in (0, 1):
                    for dm in range(NDM):
                        bank, colo = bank_of(fb, dm)
                        em.op("vector", lambda e, dm=dm, fb=fb, bank=bank, colo=colo: e.scalar_tensor_tensor(
                            dt_sb[:, dm * Tc + fb * HB: dm * Tc + fb * HB + HB],
                            r1h[:, dm * Tc + fb * HB: dm * Tc + fb * HB + HB],
                            1.0, bank[:, colo:colo + HB], ALU.mult, ALU.add), inc="dv")
                        em.drain("vector")
                return em.cnt["dv"]

            a_base_fb1 = base_a + 64
            for m in range(NM1):
                emit_ffn1_m(m, 1, xn2, a_base_fb1)
            ffs_fb1 = em.cnt["ffs"]
            dv1 = emit_out_pass_both(ffs_fb1)
            em.wait("sync", "dv", dv1)
            em.op("sync", lambda e, c=c: e.dma_start(
                out=op_snd.ap()[c, :, :], in_=dt_sb[:, 0:NDM * Tc]), inc="io", by=16)
            io_op = em.cnt["io"]
            em.wait("gpsimd", "io", io_op)
            em.op("gpsimd", lambda e, c=c: e.collective_compute(
                "AllReduce", ALU.add, replica_groups=GROUPS,
                ins=[op_snd.ap()[c, :, :]], outs=[op_rcv.ap()[c, :, :]]), inc="cc")
            cc_op_marks[c] = em.cnt["cc"]
            em.barrier()
            if c == NCH - 1:
                emit_p9(c)
            if c + 1 < NCH:
                em.wait("sync", "st", em.cnt["st"])
                load_x(c + 1)
                load_w_in(c + 1)
            em.barrier()

        em.wait("sync", "st", em.cnt["st"])

        with nc.Block() as blk:
            @blk.sync
            def _(eng):
                for fn in em.streams["sync"]:
                    fn(eng)

            @blk.gpsimd
            def _(eng):
                for fn in em.streams["gpsimd"]:
                    fn(eng)

            @blk.tensor
            def _(eng):
                for fn in em.streams["tensor"]:
                    fn(eng)

            @blk.vector
            def _(eng):
                for fn in em.streams["vector"]:
                    fn(eng)

            @blk.scalar
            def _(eng):
                for fn in em.streams["scalar"]:
                    fn(eng)

    return nc


NSCAN = 128


def kernel(**inputs):
    x = np.ascontiguousarray(np.asarray(inputs["L0_attn_x"], dtype=np.float32))
    B, L, _ = x.shape
    w_in_full = np.asarray(inputs["L0_attn_W_in"], dtype=np.float32)
    conv_w = np.asarray(inputs["L0_attn_conv_w"], dtype=np.float32)
    conv_b = np.asarray(inputs["L0_attn_conv_b"], dtype=np.float32)
    w_xp_full = np.asarray(inputs["L0_attn_W_xp"], dtype=np.float32)
    w_dt = np.asarray(inputs["L0_attn_W_dt"], dtype=np.float32).reshape(-1)
    b_dt = np.asarray(inputs["L0_attn_b_dt"], dtype=np.float32)
    a_log = np.asarray(inputs["L0_attn_A_log"], dtype=np.float32)
    d_par = np.asarray(inputs["L0_attn_D"], dtype=np.float32)
    w_out_full = np.asarray(inputs["L0_attn_W_out"], dtype=np.float32)
    ln_g = np.asarray(inputs["L0_attn_ln_g"], dtype=np.float32)
    ln_b = np.asarray(inputs["L0_attn_ln_b"], dtype=np.float32)
    w1_full = np.asarray(inputs["L0_ffn_W1"], dtype=np.float32)
    b1 = np.asarray(inputs["L0_ffn_b1"], dtype=np.float32)
    w2_full = np.asarray(inputs["L0_ffn_W2"], dtype=np.float32)
    b2 = np.asarray(inputs["L0_ffn_b2"], dtype=np.float32)

    nc = build_program(L)

    ident = np.eye(128, dtype=np.float32)
    sel16 = np.zeros((16, 2048), dtype=np.float32)
    for n in range(16):
        sel16[n, n * 128:(n + 1) * 128] = 1.0

    in_maps = []
    for r in range(NCORES):
        b = r // 2
        H = r % 2
        ch = slice(H * DH, (H + 1) * DH)
        ih = slice(H * IH, (H + 1) * IH)
        w_in_c = np.concatenate(
            [w_in_full[:, H * DH:(H + 1) * DH],
             w_in_full[:, 2 * DH + H * DH: 2 * DH + (H + 1) * DH]], axis=1)
        pco = np.zeros((128, 104), dtype=np.float32)
        pco[:, 0:8] = ln_g.reshape(8, 128).T
        pco[:, 8:16] = ln_b.reshape(8, 128).T
        pco[:, 16:24] = w_dt[ch].reshape(8, 128).T
        pco[:, 24:32] = b_dt[ch].reshape(8, 128).T
        pco[:, 32:40] = d_par[ch].reshape(8, 128).T
        pco[:, 40:48] = conv_b[ch].reshape(8, 128).T
        pco[:, 48:56] = (0.5 * b2).reshape(8, 128).T
        cw = conv_w[ch].reshape(8, 128, D_CONV)
        for g in range(8):
            pco[:, 56 + g * 4:56 + (g + 1) * 4] = cw[g]
        pco[:, 88:104] = b1[ih].reshape(16, 128).T
        alog_c = np.zeros((128, 128), dtype=np.float32)
        al = a_log[ch].reshape(8, 128, D_STATE)
        for g in range(8):
            alog_c[:, g * 16:(g + 1) * 16] = al[g]
        wxp_pad = np.zeros((DH, 65), dtype=np.float32)
        wxp_pad[:, 0:16] = w_xp_full[ch, 0:16]
        wxp_pad[:, 32:48] = w_xp_full[ch, 16:32]
        wxp_pad[:, 64] = w_xp_full[ch, 32]
        in_maps.append({
            "x": np.ascontiguousarray(x[b]),
            "w_in": np.ascontiguousarray(w_in_c),
            "w1": np.ascontiguousarray(w1_full[:, ih]),
            "w2": np.ascontiguousarray(w2_full[ih, :]),
            "w_out": np.ascontiguousarray(w_out_full[ch, :]),
            "w_xp": wxp_pad,
            "pcons": pco,
            "a_log": alog_c,
            "identc": ident,
            "identrc": ident,
            "identbc": ident.astype(ml_dtypes.bfloat16),
            "sel16c": sel16,
            "onesc": np.ones((128, 1), dtype=np.float32),
            "onesrc": np.ones((128, 1), dtype=np.float32),
        })

    res = run_bass_kernel_spmd(nc, in_maps, core_ids=list(range(NCORES)))
    out = np.empty((B, L, D_MODEL), dtype=np.float32)
    for b in range(B):
        out[b] = res.results[2 * b]["out"]
    return out


# version 21
# speedup vs baseline: 1.0972x; 1.0041x over previous
import os
import numpy as np
from contextlib import ExitStack

EMIT_DRAINS = os.environ.get("BASS_SIM_FP64") == "1" or os.environ.get("KERNEL_SIM_DRAINS") == "1"
HW_DRAINS = set(os.environ.get("KERNEL_HW_DRAINS", "p1").split(","))

import concourse.bass as bass
import concourse.mybir as mybir
from concourse.bass_utils import run_bass_kernel_spmd

F32 = mybir.dt.float32
F32R = mybir.dt.float32r
ALU = mybir.AluOpType
AF = mybir.ActivationFunctionType

D_MODEL = 1024
D_STATE = 16
D_CONV = 4
DH = 1024
IH = 2048
NCORES = 8
EPS = 1e-5
INV_D = 1.0 / D_MODEL

ENGS = ("sync", "gpsimd", "tensor", "vector", "scalar")


class Emitter:
    def __init__(self, nc, es):
        self.nc = nc
        self.streams = {e: [] for e in ENGS}
        self.cnt = {}
        self.sems = {}
        self.nbar = 0
        self.phase = "pro"
        self._es = es

    def sem(self, name):
        self.sems[name] = self._es.enter_context(self.nc.semaphore(name))
        self.cnt[name] = 0

    def op(self, eng, fn, inc=None, by=1):
        if inc is None:
            self.streams[eng].append(fn)
            return None
        s = self.sems[inc]
        self.cnt[inc] += by
        def wrapped(e):
            fn(e).then_inc(s, by)
        self.streams[eng].append(wrapped)
        return self.cnt[inc]

    def wait(self, eng, name, val, force=False):
        if val <= 0 and not force:
            return
        s = self.sems[name]
        self.streams[eng].append(lambda e: e.wait_ge(s, max(val, 0)))

    BAR_ENGS = ("gpsimd", "tensor", "vector", "scalar")

    def barrier(self):
        self.nbar += 1
        tgt = self.nbar * len(self.BAR_ENGS)
        s = self.sems["bar"]
        for eng in self.BAR_ENGS:
            self.streams[eng].append(lambda e: e.nop().then_inc(s, 1))
            self.streams[eng].append(lambda e: e.wait_ge(s, tgt))

    def drain(self, eng, hw=False):
        if EMIT_DRAINS or eng == "scalar" or hw or "all" in HW_DRAINS \
                or self.phase in HW_DRAINS:
            self.streams[eng].append(lambda e: e.drain())


def build_program(L):
    Tc = min(512, L)
    NCH = L // Tc
    HB = Tc // 2
    NG = DH // 128
    NDM = D_MODEL // 128
    NM1 = IH // 128
    NTS = Tc // 128

    NA_SLOT = 8
    NB_SLOT = 8

    nc = bass.Bass("TRN2", target_bir_lowering=False)

    x_hbm = nc.dram_tensor("x", [L, D_MODEL], F32, kind="ExternalInput").ap()
    w_in = nc.dram_tensor("w_in", [D_MODEL, 2 * DH], F32R, kind="ExternalInput").ap()
    w1 = nc.dram_tensor("w1", [D_MODEL, IH], F32R, kind="ExternalInput").ap()
    w2 = nc.dram_tensor("w2", [IH, D_MODEL], F32R, kind="ExternalInput").ap()
    w_out = nc.dram_tensor("w_out", [DH, D_MODEL], F32R, kind="ExternalInput").ap()
    w_xp = nc.dram_tensor("w_xp", [DH, 65], F32R, kind="ExternalInput").ap()
    pcons = nc.dram_tensor("pcons", [128, 104], F32, kind="ExternalInput").ap()
    a_log = nc.dram_tensor("a_log", [128, 128], F32, kind="ExternalInput").ap()
    ident_in = nc.dram_tensor("identc", [128, 128], F32, kind="ExternalInput").ap()
    sel16_in = nc.dram_tensor("sel16c", [16, 2048], F32R, kind="ExternalInput").ap()
    identr_in = nc.dram_tensor("identrc", [128, 128], F32R, kind="ExternalInput").ap()
    onesr_in = nc.dram_tensor("onesrc", [128, 1], F32R, kind="ExternalInput").ap()
    zro16_in = nc.dram_tensor("zro16c", [16, 128], F32R, kind="ExternalInput").ap()
    ones_in = nc.dram_tensor("onesc", [128, 1], F32, kind="ExternalInput").ap()
    out_hbm = nc.dram_tensor("out", [L, D_MODEL], F32, kind="ExternalOutput").ap()

    xpj_snd = nc.dram_tensor("xpj_snd", [NCH, 65, Tc], F32R)
    xpj_rcv = nc.dram_tensor("xpj_rcv", [NCH, 65, Tc], F32R)
    op_snd = nc.dram_tensor("op_snd", [NCH, 128, NDM * Tc], F32)
    op_rcv = nc.dram_tensor("op_rcv", [NCH, 128, NDM * Tc], F32)
    GROUPS = [[0, 1], [2, 3], [4, 5], [6, 7]]

    es = ExitStack()
    sb = lambda nm, shp, dt=F32: es.enter_context(nc.sbuf_tensor(nm, shp, dt))
    ps = lambda nm, shp: es.enter_context(nc.psum_tensor(nm, shp, F32))

    with es:
        pc = sb("pc", [128, 104])
        aneg = sb("aneg", [128, 128])
        identT = sb("identT", [128, 128])
        identR = sb("identR", [128, 128], F32R)
        sel16 = sb("sel16", [16, 2048], F32R)
        zro16 = sb("zro16", [16, 128], F32R)
        onesT = sb("onesT", [128, 1])
        onesR = sb("onesR", [128, 1], F32R)
        wxp_sb = sb("wxp_sb", [128, 8 * 65], F32R)

        wpa = sb("wpa", [128, NA_SLOT * 512], F32R)
        wpb = sb("wpb", [128, NB_SLOT * 256], F32R)

        tok = sb("tok", [128, 2 * D_MODEL])
        scr_r = sb("scr_r", [128, NDM * Tc], F32R)
        xn = sb("xn", [128, NDM * Tc], F32R)
        xn2 = sb("xn2", [128, NG * Tc], F32R)
        r1h = sb("r1h", [128, NDM * Tc])
        sq = sb("sq", [128, Tc])

        xh_rot = sb("xh_rot", [128, Tc + 3])
        halo = sb("halo", [128, NG * 3])
        cacc = sb("cacc", [128, Tc])
        row1 = cacc[0:1, 0:Tc]
        row2 = sq[0:1, 0:Tc]
        xc = sb("xc", [128, NG * Tc], F32R)
        sz = sb("sz", [128, NG * Tc])
        dt_sb = sb("dt_sb", [128, NG * Tc])
        xpjB = sb("xpjB", [16, Tc], F32R)
        xpjC = sb("xpjC", [16, Tc], F32R)

        da_b = sb("da_b", [128, 3 * Tc])
        dbx_b = sb("dbx_b", [128, 2 * Tc])
        h32_b = sb("h32_b", [128, 2 * Tc])
        hc_b = sb("hc_b", [128, 4 * Tc], F32R)
        state = sb("state", [128, 128])

        cst = sb("cst", [128, 4])
        ysz = sb("ysz", [128, NG * Tc], F32R)
        bst = ysz[0:16, 0:Tc]
        xpjp = ysz[0:65, Tc:2 * Tc]

        pA = ps("pA", [128, 512])
        pB = ps("pB", [128, 512])
        pT = ps("pT", [128, 512])
        pS = ps("pS", [128, 512])
        pY = [ps(f"pY{i}", [128, 512]) for i in range(4)]

        nc.const_aps.register(F32, EPS, cst[:, 0:1])
        nc.const_aps.register(F32, 1.0, cst[:, 1:2])
        nc.const_aps.register(F32, -0.5, cst[:, 2:3])
        nc.const_aps.register(F32, -1.0, cst[:, 3:4])

        em = Emitter(nc, es)
        for s in ("bar", "ldw", "ldwb", "ldx", "lda", "mm", "ev", "ev2", "dv", "da",
                  "bc", "sc", "hc", "dbxs", "yx", "cc", "io", "st", "ffs"):
            em.sem(s)

        for t, src in ((pc, pcons), (aneg, a_log), (identT, ident_in),
                       (identR, identr_in), (sel16, sel16_in), (zro16, zro16_in)):
            em.op("sync", lambda e, t=t, src=src: e.dma_start(out=t[:], in_=src),
                  inc="lda", by=16)
        em.op("sync", lambda e: e.dma_start(out=onesT[:], in_=ones_in),
              inc="lda", by=16)
        em.op("sync", lambda e: e.dma_start(out=onesR[:], in_=onesr_in[:, 0:1]),
              inc="lda", by=16)
        em.op("sync", lambda e: e.dma_start(
            out=wxp_sb[:, 0:8 * 65].rearrange("p (g c) -> p g c", c=65),
            in_=w_xp.rearrange("(g p) c -> p g c", p=128)), inc="lda", by=16)
        v_lda = em.cnt["lda"]
        em.wait("scalar", "lda", v_lda)
        em.op("scalar", lambda e: e.activation(aneg[:], aneg[:], AF.Exp), inc="ev")
        ev0 = em.cnt["ev"]
        em.wait("vector", "ev", ev0)
        em.op("vector", lambda e: e.tensor_scalar(aneg[:], aneg[:], -1.0, None, ALU.mult), inc="dv")
        em.drain("vector")
        em.op("vector", lambda e: e.memset(state[:], 0.0), inc="dv")
        em.drain("vector")
        em.op("vector", lambda e: e.memset(halo[:], 0.0), inc="dv")
        em.drain("vector")
        em.op("vector", lambda e: e.memset(cacc[:], 0.0), inc="dv")
        em.drain("vector")
        em.op("vector", lambda e: e.tensor_copy(ysz[0:16, 0:Tc], cacc[0:16, 0:Tc]), inc="dv")
        em.drain("vector")
        em.op("vector", lambda e: e.memset(cst[:, 0:1], EPS), inc="dv")
        em.op("vector", lambda e: e.memset(cst[:, 1:2], 1.0), inc="dv")
        em.op("vector", lambda e: e.memset(cst[:, 2:3], -0.5), inc="dv")
        em.op("vector", lambda e: e.memset(cst[:, 3:4], -1.0), inc="dv")
        em.drain("vector")
        em.wait("scalar", "dv", em.cnt["dv"])
        em.wait("tensor", "lda", v_lda)
        em.wait("tensor", "dv", em.cnt["dv"])

        class LoadQ:
            def __init__(self, pool, sem_name, nslot, width=512):
                self.pool = pool
                self.sem = sem_name
                self.width = width
                self.ring = nslot // 4
                self.pending = []
                self.n_loaded = 0
                self.cons = []

            def add(self, w, kslice, cslice):
                self.pending.append((w, kslice, cslice))

            def record(self, mm_val):
                self.cons.append(mm_val)
                self.pump()

            def pump(self):
                while self.pending:
                    g_ = self.n_loaded // 4
                    if g_ - self.ring >= len(self.cons):
                        break
                    if self.n_loaded % 4 == 0:
                        if g_ >= self.ring:
                            em.wait("sync", "mm", self.cons[g_ - self.ring])
                        if g_ >= 1:
                            em.wait("sync", self.sem, self.n_loaded * 16)
                    w, ks, cs = self.pending.pop(0)
                    s_ = self.n_loaded % (self.ring * 4)
                    wd = self.width
                    em.op("sync", lambda e, s_=s_, w=w, ks=ks, cs=cs, wd=wd: e.dma_start(
                        out=self.pool[:, s_ * wd:(s_ + 1) * wd],
                        in_=w[ks, cs]), inc=self.sem, by=16)
                    self.n_loaded += 1

        qa = LoadQ(wpa, "ldw", NA_SLOT)
        qb = LoadQ(wpb, "ldwb", NB_SLOT, width=256)

        def load_w_in(_c):
            for mg in range(4):
                for k in range(8):
                    qa.add(w_in, slice(k * 128, (k + 1) * 128),
                           slice(mg * 512, (mg + 1) * 512))
            qa.pump()

        def load_w1(_c):
            for mg in range(4):
                for k in range(8):
                    qa.add(w1, slice(k * 128, (k + 1) * 128),
                           slice(mg * 512, (mg + 1) * 512))
            qa.pump()

        def load_w2_wout(_c):
            for k in range(16):
                for q in range(4):
                    qb.add(w2, slice(k * 128, (k + 1) * 128),
                           slice(q * 256, (q + 1) * 256))
            for k in range(8):
                for q in range(4):
                    qb.add(w_out, slice(k * 128, (k + 1) * 128),
                           slice(q * 256, (q + 1) * 256))
            qb.pump()

        class XQ:
            def __init__(self):
                self.pending = []
                self.n_loaded = 0
                self.cons = []

            def add(self, c, ts):
                self.pending.append((c, ts))

            def record(self, mm_val):
                self.cons.append(mm_val)
                self.pump()

            def pump(self):
                while self.pending:
                    if self.n_loaded - 2 >= len(self.cons):
                        break
                    if self.n_loaded >= 2:
                        em.wait("sync", "mm", self.cons[self.n_loaded - 2])
                    if self.n_loaded >= 1:
                        em.wait("sync", "ldx", self.n_loaded * 16)
                    c_, ts_ = self.pending.pop(0)
                    sl_ = self.n_loaded % 2
                    em.op("sync", lambda e, c_=c_, ts_=ts_, sl_=sl_: e.dma_start(
                        out=tok[:, sl_ * D_MODEL:(sl_ + 1) * D_MODEL],
                        in_=x_hbm[c_ * Tc + ts_ * 128: c_ * Tc + (ts_ + 1) * 128, :]),
                        inc="ldx", by=16)
                    self.n_loaded += 1

        xq = XQ()

        def load_x(c):
            for ts_ in range(NTS):
                xq.add(c, ts_)
            xq.pump()

        load_x(0)
        load_w_in(0)
        em.barrier()

        st_marks = []
        cc_op_marks = {}

        def emit_p9_ts(cp, ts_):
            if ts_ == 0:
                em.wait("sync", "cc", cc_op_marks[cp])
            em.wait("sync", "io", em.cnt["io"])
            if st_marks:
                em.wait("sync", "st", st_marks[-1])
            em.op("sync", lambda e, cp=cp, ts_=ts_: e.dma_start(
                out=tok[:, 0:D_MODEL].rearrange("p (dm c) -> p dm c", c=128),
                in_=op_rcv.ap()[cp, :, :].rearrange(
                    "p (dm t) -> p dm t", t=Tc)[:, :, ts_ * 128:(ts_ + 1) * 128]),
                inc="io", by=16)
            io_ts = em.cnt["io"]
            em.wait("tensor", "io", io_ts)
            for half in range(2):
                if not (ts_ == 0 and half == 0):
                    em.wait("tensor", "ev", em.cnt["ev"])
                for dmq in range(4):
                    dm = half * 4 + dmq
                    em.op("tensor", lambda e, dm=dm, dmq=dmq: e.matmul(
                        pS[:, dmq * 128:(dmq + 1) * 128],
                        tok[:, dm * 128:(dm + 1) * 128],
                        identT[:], start=(dmq == 0), stop=(dmq == 3),
                        is_transpose=True),
                        inc=("mm" if dmq == 3 else None))
                mmv = em.cnt["mm"]
                em.wait("scalar", "mm", mmv)
                if len(st_marks) >= 1:
                    em.wait("scalar", "st", st_marks[-1])
                em.op("scalar", lambda e, half=half: e.copy(
                    tok[:, D_MODEL + half * 512: D_MODEL + (half + 1) * 512],
                    pS[:, 0:512]), inc="ev")
            ev_ts = em.cnt["ev"]
            em.wait("sync", "ev", ev_ts)
            em.op("sync", lambda e, cp=cp, ts_=ts_: e.dma_start(
                out=out_hbm[cp * Tc + ts_ * 128: cp * Tc + (ts_ + 1) * 128, :],
                in_=tok[:, D_MODEL:2 * D_MODEL]), inc="st", by=16)
            st_marks.append(em.cnt["st"])

        def emit_p9(cp):
            for ts_ in range(NTS):
                emit_p9_ts(cp, ts_)
            return em.cnt["mm"]

        for c in range(NCH):
            em.phase = "p1"
            for ts_ in range(NTS):
                gi = c * NTS + ts_
                sl_ = gi % 2
                bks = ((pA, pB), (pY[0], pY[1]))[ts_ % 2]
                em.wait("tensor", "ldx", (gi + 1) * 16)
                if ts_ >= 2:
                    em.wait("tensor", "ev", em.cnt["ev"])
                for dm in range(NDM):
                    bk = bks[dm // 4]
                    em.op("tensor", lambda e, bk=bk, sl_=sl_, dm=dm: e.matmul(
                        bk[:, (dm % 4) * 128:(dm % 4 + 1) * 128],
                        tok[:, sl_ * D_MODEL + dm * 128: sl_ * D_MODEL + (dm + 1) * 128],
                        identT[:], start=(dm % 4 == 0), stop=(dm % 4 == 3),
                        is_transpose=True),
                        inc=("mm" if dm == NDM - 1 else None))
                mmv = em.cnt["mm"]
                xq.record(mmv)
                em.wait("scalar", "mm", mmv)
                for dm in range(NDM):
                    bk = bks[dm // 4]
                    em.op("scalar", lambda e, bk=bk, dm=dm, ts_=ts_: e.copy(
                        scr_r[:, dm * Tc + ts_ * 128: dm * Tc + (ts_ + 1) * 128],
                        bk[:, (dm % 4) * 128:(dm % 4 + 1) * 128]), inc="ev")
            ev_xfm = em.cnt["ev"]
            em.wait("tensor", "ev", ev_xfm)
            for dm in range(NDM):
                em.op("tensor", lambda e, dm=dm: e.matmul(
                    pT[0:1, 0:Tc], onesR[:], scr_r[:, dm * Tc:(dm + 1) * Tc],
                    start=(dm == 0), stop=(dm == NDM - 1)),
                    inc=("mm" if dm == NDM - 1 else None))
            sq_mm = []
            for dm in range(NDM):
                em.wait("scalar", "ev", ev_xfm)
                if dm >= 1:
                    em.wait("scalar", "mm", sq_mm[dm - 1])
                em.op("scalar", lambda e, dm=dm: e.activation(
                    sq[:, 0:Tc],
                    scr_r[:, dm * Tc:(dm + 1) * Tc], AF.Square), inc="ev")
                em.drain("scalar")
                em.wait("tensor", "ev", em.cnt["ev"])
                em.op("tensor", lambda e, dm=dm: e.matmul(
                    pT[32:33, 0:Tc], onesT[:], sq[:, 0:Tc],
                    start=(dm == 0), stop=(dm == NDM - 1)), inc="mm")
                sq_mm.append(em.cnt["mm"])
            mm_st = em.cnt["mm"]

            def row_stats_bcast(mm_wait, psrc0, psrc1):
                em.wait("vector", "mm", mm_wait)
                em.op("vector", lambda e: e.tensor_scalar(
                    row1, psrc1, INV_D, None, ALU.mult), inc="dv")
                em.drain("vector")
                em.op("vector", lambda e: e.tensor_scalar(
                    bst[0:1, :], psrc0, INV_D, None, ALU.mult), inc="dv")
                em.drain("vector")
                dv_mu = em.cnt["dv"]
                em.wait("tensor", "dv", dv_mu)
                em.op("tensor", lambda e: e.matmul(
                    pA[:, 0:Tc], sel16[:, 0:128], bst[:, 0:Tc], start=True, stop=True), inc="mm")
                mm_mu = em.cnt["mm"]
                em.op("vector", lambda e: e.tensor_mul(
                    row2, bst[0:1, :], bst[0:1, :]), inc="dv")
                em.drain("vector")
                em.op("vector", lambda e: e.tensor_sub(row1, row1, row2), inc="dv")
                em.drain("vector")
                dvv = em.cnt["dv"]
                em.wait("scalar", "dv", dvv)
                em.op("scalar", lambda e: e.activation(row2, row1, AF.Ln, bias=EPS), inc="ev")
                em.drain("scalar")
                em.op("scalar", lambda e: e.activation(row2, row2, AF.Exp, scale=-0.5), inc="ev")
                evv = em.cnt["ev"]
                em.wait("vector", "ev", evv)
                em.wait("vector", "mm", mm_mu)
                em.op("vector", lambda e: e.tensor_copy(bst[0:1, :], row2), inc="dv")
                em.drain("vector")
                dv_r = em.cnt["dv"]
                em.wait("tensor", "dv", dv_r)
                em.op("tensor", lambda e: e.matmul(
                    pB[:, 0:Tc], sel16[:, 0:128], bst[:, 0:Tc], start=True, stop=True), inc="mm")
                return em.cnt["mm"]

            mm_bc1 = row_stats_bcast(mm_st, pT[0:1, 0:Tc], pT[32:33, 0:Tc])
            em.wait("vector", "mm", mm_bc1)
            for dm in range(NDM):
                em.op("vector", lambda e, dm=dm: e.tensor_sub(
                    cacc[:, 0:Tc], scr_r[:, dm * Tc:(dm + 1) * Tc], pA[:, 0:Tc]))
                em.drain("vector")
                em.op("vector", lambda e, dm=dm: e.tensor_mul(
                    xn[:, dm * Tc:(dm + 1) * Tc], cacc[:, 0:Tc], pB[:, 0:Tc]), inc="dv")
                em.drain("vector")
            dv_xn = em.cnt["dv"]
            em.wait("tensor", "dv", dv_xn)
            for dm in range(NDM):
                em.op("tensor", lambda e, dm=dm: e.matmul(
                    pS[0:1, 0:Tc], onesR[:], xn[:, dm * Tc:(dm + 1) * Tc],
                    start=(dm == 0), stop=(dm == NDM - 1)),
                    inc=("mm" if dm == NDM - 1 else None))
            sq_mm2 = []
            for dm in range(NDM):
                em.wait("scalar", "dv", dv_xn)
                if dm >= 1:
                    em.wait("scalar", "mm", sq_mm2[dm - 1])
                em.op("scalar", lambda e, dm=dm: e.activation(
                    sq[:, 0:Tc],
                    xn[:, dm * Tc:(dm + 1) * Tc], AF.Square), inc="ev")
                em.drain("scalar")
                em.wait("tensor", "ev", em.cnt["ev"])
                em.op("tensor", lambda e, dm=dm: e.matmul(
                    pS[32:33, 0:Tc], onesT[:], sq[:, 0:Tc],
                    start=(dm == 0), stop=(dm == NDM - 1)), inc="mm")
                sq_mm2.append(em.cnt["mm"])
            mm_st2 = em.cnt["mm"]
            mm_bc2 = row_stats_bcast(mm_st2, pS[0:1, 0:Tc], pS[32:33, 0:Tc])
            em.wait("vector", "mm", mm_bc2)
            for dm in range(NDM):
                em.op("vector", lambda e, dm=dm: e.tensor_sub(
                    cacc[:, 0:Tc], xn[:, dm * Tc:(dm + 1) * Tc], pA[:, 0:Tc]))
                em.drain("vector")
                em.op("vector", lambda e, dm=dm: e.tensor_mul(
                    sq[:, 0:Tc], cacc[:, 0:Tc], pB[:, 0:Tc]))
                em.drain("vector")
                em.op("vector", lambda e, dm=dm: e.tensor_scalar(
                    xn2[:, dm * Tc:(dm + 1) * Tc], sq[:, 0:Tc],
                    pc[:, dm:dm + 1], pc[:, 8 + dm:9 + dm], ALU.mult, ALU.add), inc="dv")
                em.drain("vector")
                em.op("vector", lambda e, dm=dm: e.tensor_add(
                    cacc[:, 0:Tc], scr_r[:, dm * Tc:(dm + 1) * Tc],
                    xn[:, dm * Tc:(dm + 1) * Tc]))
                em.drain("vector")
                em.op("vector", lambda e, dm=dm: e.tensor_scalar(
                    r1h[:, dm * Tc:(dm + 1) * Tc], cacc[:, 0:Tc],
                    0.5, pc[:, 48 + dm:49 + dm], ALU.mult, ALU.add), inc="dv")
                em.drain("vector")
            em.barrier()

            em.phase = "p2"
            base_a = c * 96
            for m in range(2 * NG):
                mg, mi = divmod(m, 4)
                pb = (pA, pB)[m % 2]
                em.wait("tensor", "ldw", (base_a + (mg + 1) * 8) * 16)
                if m >= 2:
                    em.wait("tensor", "ev", em.cnt["ev"])
                    em.wait("tensor", "ev2", em.cnt["ev2"])
                for k in range(8):
                    sl = (base_a + mg * 8 + k) % NA_SLOT
                    src = xn2
                    em.op("tensor", lambda e, pb=pb, sl=sl, mi=mi, k=k, src=src: e.matmul(
                        pb[:, 0:Tc],
                        wpa[:, sl * 512 + mi * 128: sl * 512 + (mi + 1) * 128],
                        src[:, k * Tc:(k + 1) * Tc],
                        start=(k == 0), stop=(k == 7)),
                        inc=("mm" if k == 7 else None))
                mmv = em.cnt["mm"]
                if m == 4 * mg + 3:
                    qa.record(mmv)
                    qa.record(mmv)
                em.wait("scalar", "mm", mmv)
                if m < NG:
                    g = m
                    hb = 0
                    em.wait("scalar", "dv", em.cnt["dv"]) if g >= 1 else None
                    em.op("scalar", lambda e, pb=pb, hb=hb: e.copy(
                        xh_rot[:, hb + 3:hb + 3 + Tc], pb[:, 0:Tc]), inc="ev")
                    ev_xh = em.cnt["ev"]
                    em.wait("vector", "ev", ev_xh)
                    em.op("vector", lambda e, hb=hb, g=g: e.tensor_copy(
                        xh_rot[:, hb:hb + 3], halo[:, g * 3:(g + 1) * 3]))
                    em.drain("vector")
                    em.op("vector", lambda e, hb=hb, g=g: e.tensor_scalar(
                        cacc[:, 0:Tc], xh_rot[:, hb:hb + Tc],
                        pc[:, 56 + g * 4:57 + g * 4], None, ALU.mult))
                    em.drain("vector")
                    for kk in range(1, D_CONV):
                        em.op("vector", lambda e, hb=hb, g=g, kk=kk: e.scalar_tensor_tensor(
                            cacc[:, 0:Tc], xh_rot[:, hb + kk:hb + kk + Tc],
                            pc[:, 56 + g * 4 + kk:57 + g * 4 + kk],
                            cacc[:, 0:Tc], ALU.mult, ALU.add))
                        em.drain("vector")
                    em.op("vector", lambda e, g=g: e.tensor_scalar(
                        sq[:, 0:Tc], cacc[:, 0:Tc], 1.0,
                        pc[:, 40 + g:41 + g], ALU.mult, ALU.add))
                    em.drain("vector")
                    em.op("vector", lambda e, hb=hb, g=g: e.tensor_copy(
                        halo[:, g * 3:(g + 1) * 3], xh_rot[:, hb + Tc:hb + Tc + 3]),
                        inc="dv")
                    em.drain("vector")
                    em.wait("scalar", "dv", em.cnt["dv"])
                    em.wait("scalar", "ev2", em.cnt["ev2"])
                    em.op("scalar", lambda e: e.activation(
                        da_b[:, 0:Tc], sq[:, 0:Tc], AF.Tanh, scale=0.5),
                        inc="ev")
                    em.drain("scalar")
                    em.wait("vector", "ev", em.cnt["ev"])
                    em.op("vector", lambda e: e.tensor_scalar(
                        da_b[:, Tc:2 * Tc], da_b[:, 0:Tc], 0.5, 0.5,
                        ALU.mult, ALU.add))
                    em.drain("vector")
                    em.op("vector", lambda e, g=g: e.tensor_mul(
                        xc[:, g * Tc:(g + 1) * Tc], sq[:, 0:Tc],
                        da_b[:, Tc:2 * Tc]), inc="ev2")
                    em.drain("vector")
                    em.wait("tensor", "ev2", em.cnt["ev2"])
                    em.op("tensor", lambda e, g=g: e.matmul(
                        pY[2][0:65, 0:Tc], wxp_sb[:, g * 65:(g + 1) * 65],
                        xc[:, g * Tc:(g + 1) * Tc],
                        start=(g == 0), stop=(g == NG - 1)),
                        inc=("mm" if g == NG - 1 else None))
                    if g == NG - 1:
                        mm_xpj = em.cnt["mm"]
                        em.wait("scalar", "mm", mm_xpj)
                        em.op("scalar", lambda e: e.copy(xpjp, pY[2][0:65, 0:Tc]), inc="ev")
                        ev_xpj = em.cnt["ev"]
                        em.wait("sync", "ev", ev_xpj)
                        em.op("sync", lambda e, c=c: e.dma_start(
                            out=xpj_snd.ap()[c, :, :], in_=xpjp), inc="io", by=16)
                        io_snd = em.cnt["io"]
                        em.wait("gpsimd", "io", io_snd)
                        em.op("gpsimd", lambda e, c=c: e.collective_compute(
                            "AllReduce", ALU.add, replica_groups=GROUPS,
                            ins=[xpj_snd.ap()[c, :, :]], outs=[xpj_rcv.ap()[c, :, :]]), inc="cc")
                        cc_xpj = em.cnt["cc"]
                else:
                    g = m - NG
                    em.wait("scalar", "ev2", em.cnt["ev2"])
                    em.op("scalar", lambda e, pb=pb: e.activation(
                        da_b[:, 0:Tc], pb[:, 0:Tc], AF.Tanh, scale=0.5), inc="ev")
                    em.drain("scalar")
                    em.wait("vector", "ev", em.cnt["ev"])
                    em.op("vector", lambda e: e.tensor_scalar(
                        da_b[:, Tc:2 * Tc], da_b[:, 0:Tc], 0.5, 0.5,
                        ALU.mult, ALU.add))
                    em.drain("vector")
                    em.op("vector", lambda e, pb=pb, g=g: e.tensor_mul(
                        sz[:, g * Tc:(g + 1) * Tc], pb[:, 0:Tc],
                        da_b[:, Tc:2 * Tc]), inc="ev2")
                    em.drain("vector")
            ev_p2 = em.cnt["ev"]
            em.barrier()

            em.phase = "p3"
            load_w1(c)
            load_w1(c)
            load_w2_wout(c)

            em.wait("sync", "cc", cc_xpj)
            em.op("sync", lambda e, c=c: e.dma_start(
                out=xpjB[:, :], in_=xpj_rcv.ap()[c, 0:16, :]), inc="io", by=16)
            em.op("sync", lambda e, c=c: e.dma_start(
                out=xpjC[:, :], in_=xpj_rcv.ap()[c, 32:48, :]), inc="io", by=16)
            em.op("sync", lambda e, c=c: e.dma_start(
                out=bst[0:1, :], in_=xpj_rcv.ap()[c, 64:65, :]), inc="io", by=16)
            io_rcv = em.cnt["io"]
            em.wait("tensor", "io", io_rcv)
            em.op("tensor", lambda e: e.matmul(
                pT[:, 0:Tc], sel16[:, 0:128], bst[:, 0:Tc],
                start=True, stop=True), inc="mm")
            mm_xi = em.cnt["mm"]
            for g in range(NG):
                zs = g % 2
                em.wait("vector", "mm", mm_xi)
                em.op("vector", lambda e, g=g, zs=zs: e.tensor_scalar(
                    da_b[:, zs * Tc:(zs + 1) * Tc], pT[:, 0:Tc],
                    pc[:, 16 + g:17 + g], pc[:, 24 + g:25 + g],
                    ALU.mult, ALU.add), inc="dv")
                em.drain("vector")
                em.op("vector", lambda e, zs=zs: e.scalar_tensor_tensor(
                    dbx_b[:, zs * Tc:(zs + 1) * Tc], da_b[:, zs * Tc:(zs + 1) * Tc],
                    -1.0, da_b[:, zs * Tc:(zs + 1) * Tc], ALU.mult, ALU.max), inc="dv")
                em.drain("vector")
                em.wait("scalar", "dv", em.cnt["dv"])
                em.op("scalar", lambda e, zs=zs: e.activation(
                    dbx_b[:, zs * Tc:(zs + 1) * Tc], dbx_b[:, zs * Tc:(zs + 1) * Tc],
                    AF.Exp, scale=-1.0), inc="ev")
                em.drain("scalar")
                em.op("scalar", lambda e, zs=zs: e.activation(
                    dbx_b[:, zs * Tc:(zs + 1) * Tc], dbx_b[:, zs * Tc:(zs + 1) * Tc],
                    AF.Ln, bias=1.0), inc="ev")
                em.wait("vector", "ev", em.cnt["ev"])
                em.op("vector", lambda e, g=g, zs=zs: e.scalar_tensor_tensor(
                    dt_sb[:, g * Tc:(g + 1) * Tc], da_b[:, zs * Tc:(zs + 1) * Tc],
                    0.0, dbx_b[:, zs * Tc:(zs + 1) * Tc], ALU.max, ALU.add), inc="dv")
                em.drain("vector")
                em.op("vector", lambda e, g=g: e.tensor_mul(
                    xn2[:, g * Tc:(g + 1) * Tc], dt_sb[:, g * Tc:(g + 1) * Tc],
                    xc[:, g * Tc:(g + 1) * Tc]), inc="dv")
                em.drain("vector")
            em.barrier()

            em.phase = "p5"
            b_sc = em.cnt["sc"]
            b_yx = em.cnt["yx"]
            yac_done = lambda ip: b_yx + (ip - 2 * (1 + ip // 32)) + 1 \
                if False else None
            yac_mark = {}
            dvq_gate = 0
            ffn1_q = []
            for m in range(NM1):
                ffn1_q.append(m)
            ffn1_mm = {}
            ffs_marks = []
            a_base_fb0 = base_a + 32

            def emit_ffn1_m(m, fb, dst, a_base):
                mg, mi = divmod(m, 4)
                pb = (pY[2], pY[3])[m % 2]
                em.wait("tensor", "ldw", (a_base + (mg + 1) * 8) * 16)
                if len(ffs_marks) >= 2:
                    em.wait("tensor", "ffs", ffs_marks[-2])
                for k in range(8):
                    sl = (a_base + mg * 8 + k) % NA_SLOT
                    em.op("tensor", lambda e, pb=pb, sl=sl, mi=mi, k=k, fb=fb: e.matmul(
                        pb[:, 0:HB],
                        wpa[:, sl * 512 + mi * 128: sl * 512 + (mi + 1) * 128],
                        xn[:, k * Tc + fb * HB: k * Tc + fb * HB + HB],
                        start=(k == 0), stop=(k == 7)),
                        inc=("mm" if k == 7 else None))
                mmv = em.cnt["mm"]
                if m == 4 * mg + 3:
                    qa.record(mmv)
                    qa.record(mmv)
                em.wait("scalar", "mm", mmv)
                em.op("scalar", lambda e, pb=pb, m=m, dst=dst: e.activation(
                    dst[:, m * HB:(m + 1) * HB], pb[:, 0:HB], AF.Relu,
                    bias=pc[:, 88 + m:89 + m]), inc="ffs")
                ffs_marks.append(em.cnt["ffs"])

            for hf in range(2):
                for qp in range(2):
                    blk = hf * 2 + qp
                    if c >= 1:
                        em.phase = "p9"
                        emit_p9_ts(c - 1, blk)
                        em.phase = "p5"
                    for n in range(D_STATE):
                        i0 = blk * 32 + n * 2
                        if i0 > 0:
                            em.wait("tensor", "sc", b_sc + i0)
                        em.op("tensor", lambda e, n=n: e.matmul(
                            pA[:, 0:Tc], sel16[:, n * 128:(n + 1) * 128],
                            xpjB[:, :], start=True, stop=True), inc="bc")
                        pCb = (pB, pT)[n % 2]
                        if i0 >= 4:
                            em.wait("tensor", "sc", b_sc + i0 - 2)
                        em.op("tensor", lambda e, n=n, pCb=pCb: e.matmul(
                            pCb[:, 0:Tc], sel16[:, n * 128:(n + 1) * 128],
                            xpjC[:, :], start=True, stop=True), inc="bc")
                        bcv = em.cnt["bc"]
                        if n >= 1:
                            if n == 1 and dvq_gate:
                                em.wait("tensor", "dv", dvq_gate)
                            for g2 in range(2):
                                ip = blk * 32 + (n - 1) * 2 + g2
                                em.wait("tensor", "sc", b_sc + ip + 1)
                                em.op("tensor", lambda e, g2=g2, ip=ip, nn=n - 1: e.matmul(
                                    pY[g2][:, 0:Tc], identR[:],
                                    hc_b[:, (ip % 4) * Tc:(ip % 4 + 1) * Tc],
                                    start=(nn == 0), stop=(nn == D_STATE - 1)), inc="yx")
                                yac_mark[ip] = em.cnt["yx"]
                        if n % 2 == 0 and ffn1_q:
                            emit_ffn1_m(ffn1_q.pop(0), 0, scr_r, a_base_fb0)
                        if n >= 1:
                            em.wait("tensor", "sc", b_sc + blk * 32 + n * 2 + 1)
                            em.op("tensor", lambda e: e.matmul(
                                pY[0][:, 0:256], zro16[:], sel16[:, 0:256],
                                start=False, stop=False))
                            em.drain("tensor")
                        for g2 in range(2):
                            g = hf * 4 + qp * 2 + g2
                            i = blk * 32 + n * 2 + g2
                            sl = i % 3
                            s2 = i % 2
                            s4 = i % 4
                            acol = g * 16 + n
                            if i >= 3:
                                em.wait("scalar", "sc", b_sc + i - 2)
                            em.op("scalar", lambda e, g=g, acol=acol, sl=sl: e.activation(
                                da_b[:, sl * Tc:(sl + 1) * Tc],
                                dt_sb[:, g * Tc:(g + 1) * Tc], AF.Exp,
                                scale=aneg[:, acol:acol + 1]), inc="da")
                            dav = em.cnt["da"]
                            em.wait("vector", "bc", bcv)
                            em.op("vector", lambda e, g=g, s2=s2: e.tensor_mul(
                                dbx_b[:, s2 * Tc:(s2 + 1) * Tc],
                                xn2[:, g * Tc:(g + 1) * Tc], pA[:, 0:Tc]))
                            em.drain("vector")
                            em.wait("vector", "da", dav)
                            em.op("vector", lambda e, sl=sl, s2=s2, i=i: e.tensor_tensor_scan(
                                h32_b[:, s2 * Tc:(s2 + 1) * Tc],
                                da_b[:, sl * Tc:(sl + 1) * Tc],
                                dbx_b[:, s2 * Tc:(s2 + 1) * Tc],
                                state[:, i:i + 1], ALU.mult, ALU.add))
                            em.drain("vector")
                            if i >= 4:
                                em.wait("vector", "yx", yac_mark[i - 4])
                            em.op("vector", lambda e, s2=s2, s4=s4, pCb=pCb: e.tensor_mul(
                                hc_b[:, s4 * Tc:(s4 + 1) * Tc],
                                h32_b[:, s2 * Tc:(s2 + 1) * Tc], pCb[:, 0:Tc]))
                            em.drain("vector")
                            em.op("vector", lambda e, s2=s2, i=i: e.tensor_copy(
                                state[:, i:i + 1],
                                h32_b[:, s2 * Tc + Tc - 1: s2 * Tc + Tc]), inc="sc")
                            em.drain("vector")
                    for g2 in range(2):
                        ip = blk * 32 + 15 * 2 + g2
                        em.wait("tensor", "sc", b_sc + ip + 1)
                        em.op("tensor", lambda e, g2=g2, ip=ip: e.matmul(
                            pY[g2][:, 0:Tc], identR[:],
                            hc_b[:, (ip % 4) * Tc:(ip % 4 + 1) * Tc],
                            start=False, stop=True), inc="yx")
                        yac_mark[ip] = em.cnt["yx"]
                    yx_blk = em.cnt["yx"]
                    for g2 in range(2):
                        g = hf * 4 + qp * 2 + g2
                        em.wait("vector", "yx", yx_blk)
                        em.op("vector", lambda e, g=g: e.tensor_scalar(
                            cacc[:, 0:Tc], xc[:, g * Tc:(g + 1) * Tc],
                            pc[:, 32 + g:33 + g], None, ALU.mult))
                        em.drain("vector")
                        em.op("vector", lambda e, g=g, g2=g2: e.scalar_tensor_tensor(
                            sq[:, 0:Tc], cacc[:, 0:Tc], 1.0,
                            pY[g2][:, 0:Tc], ALU.mult, ALU.add))
                        em.drain("vector")
                        em.op("vector", lambda e, g=g: e.tensor_mul(
                            ysz[:, g * Tc:(g + 1) * Tc], sq[:, 0:Tc],
                            sz[:, g * Tc:(g + 1) * Tc]), inc="dv")
                        em.drain("vector")
                    dvq_gate = em.cnt["dv"]
            em.barrier()

            em.phase = "p7"
            base_b = c * 96
            obank = (pA, pB, pT, pS)

            def emit_out_pass_both(ffs_tgt):
                bb = base_b
                em.wait("tensor", "ffs", ffs_tgt)
                obk = (pA, pB, pT, pS, pY[0], pY[1], pY[2], pY[3])
                def bank_of(fb, dm):
                    return obk[fb * 4 + dm % 4], (0 if dm < 4 else HB)
                for k in range(16):
                    so = bb + 4 * k
                    em.wait("tensor", "ldwb", (so + 4) * 16)
                    for dm in range(NDM):
                        sl = (so + dm // 2) % NB_SLOT
                        for fb, hsrc in ((0, scr_r), (1, xn2)):
                            bank, colo = bank_of(fb, dm)
                            em.op("tensor", lambda e, k=k, dm=dm, bank=bank, colo=colo, sl=sl, hsrc=hsrc: e.matmul(
                                bank[:, colo:colo + HB],
                                wpb[:, sl * 256 + (dm % 2) * 128:sl * 256 + (dm % 2 + 1) * 128],
                                hsrc[:, k * HB:(k + 1) * HB],
                                start=(k == 0 and dm < 4), stop=False),
                                inc=("mm" if (dm == NDM - 1 and fb == 1) else None))
                    qb.record(em.cnt["mm"])
                for k in range(8):
                    so = bb + 64 + 4 * k
                    em.wait("tensor", "ldwb", (so + 4) * 16)
                    for dm in range(NDM):
                        sl = (so + dm // 2) % NB_SLOT
                        for fb in (0, 1):
                            bank, colo = bank_of(fb, dm)
                            em.op("tensor", lambda e, k=k, dm=dm, bank=bank, colo=colo, sl=sl, fb=fb: e.matmul(
                                bank[:, colo:colo + HB],
                                wpb[:, sl * 256 + (dm % 2) * 128:sl * 256 + (dm % 2 + 1) * 128],
                                ysz[:, k * Tc + fb * HB: k * Tc + fb * HB + HB],
                                start=False, stop=(k == 7 and dm >= 4)),
                                inc=("mm" if (dm == NDM - 1 and fb == 1) else None))
                    qb.record(em.cnt["mm"])
                mm_out = em.cnt["mm"]
                em.wait("vector", "mm", mm_out)
                for fb in (0, 1):
                    for dm in range(NDM):
                        bank, colo = bank_of(fb, dm)
                        em.op("vector", lambda e, dm=dm, fb=fb, bank=bank, colo=colo: e.scalar_tensor_tensor(
                            dt_sb[:, dm * Tc + fb * HB: dm * Tc + fb * HB + HB],
                            r1h[:, dm * Tc + fb * HB: dm * Tc + fb * HB + HB],
                            1.0, bank[:, colo:colo + HB], ALU.mult, ALU.add), inc="dv")
                        em.drain("vector")
                return em.cnt["dv"]

            a_base_fb1 = base_a + 64
            for m in range(NM1):
                emit_ffn1_m(m, 1, xn2, a_base_fb1)
            ffs_fb1 = em.cnt["ffs"]
            dv1 = emit_out_pass_both(ffs_fb1)
            em.wait("sync", "dv", dv1)
            em.op("sync", lambda e, c=c: e.dma_start(
                out=op_snd.ap()[c, :, :], in_=dt_sb[:, 0:NDM * Tc]), inc="io", by=16)
            io_op = em.cnt["io"]
            em.wait("gpsimd", "io", io_op)
            em.op("gpsimd", lambda e, c=c: e.collective_compute(
                "AllReduce", ALU.add, replica_groups=GROUPS,
                ins=[op_snd.ap()[c, :, :]], outs=[op_rcv.ap()[c, :, :]]), inc="cc")
            cc_op_marks[c] = em.cnt["cc"]
            em.barrier()
            if c == NCH - 1:
                emit_p9(c)
            if c + 1 < NCH:
                em.wait("sync", "st", em.cnt["st"])
                load_x(c + 1)
                load_w_in(c + 1)
            em.barrier()

        em.wait("sync", "st", em.cnt["st"])

        with nc.Block() as blk:
            @blk.sync
            def _(eng):
                for fn in em.streams["sync"]:
                    fn(eng)

            @blk.gpsimd
            def _(eng):
                for fn in em.streams["gpsimd"]:
                    fn(eng)

            @blk.tensor
            def _(eng):
                for fn in em.streams["tensor"]:
                    fn(eng)

            @blk.vector
            def _(eng):
                for fn in em.streams["vector"]:
                    fn(eng)

            @blk.scalar
            def _(eng):
                for fn in em.streams["scalar"]:
                    fn(eng)

    return nc


NSCAN = 128


def kernel(**inputs):
    x = np.ascontiguousarray(np.asarray(inputs["L0_attn_x"], dtype=np.float32))
    B, L, _ = x.shape
    w_in_full = np.asarray(inputs["L0_attn_W_in"], dtype=np.float32)
    conv_w = np.asarray(inputs["L0_attn_conv_w"], dtype=np.float32)
    conv_b = np.asarray(inputs["L0_attn_conv_b"], dtype=np.float32)
    w_xp_full = np.asarray(inputs["L0_attn_W_xp"], dtype=np.float32)
    w_dt = np.asarray(inputs["L0_attn_W_dt"], dtype=np.float32).reshape(-1)
    b_dt = np.asarray(inputs["L0_attn_b_dt"], dtype=np.float32)
    a_log = np.asarray(inputs["L0_attn_A_log"], dtype=np.float32)
    d_par = np.asarray(inputs["L0_attn_D"], dtype=np.float32)
    w_out_full = np.asarray(inputs["L0_attn_W_out"], dtype=np.float32)
    ln_g = np.asarray(inputs["L0_attn_ln_g"], dtype=np.float32)
    ln_b = np.asarray(inputs["L0_attn_ln_b"], dtype=np.float32)
    w1_full = np.asarray(inputs["L0_ffn_W1"], dtype=np.float32)
    b1 = np.asarray(inputs["L0_ffn_b1"], dtype=np.float32)
    w2_full = np.asarray(inputs["L0_ffn_W2"], dtype=np.float32)
    b2 = np.asarray(inputs["L0_ffn_b2"], dtype=np.float32)

    nc = build_program(L)

    ident = np.eye(128, dtype=np.float32)
    sel16 = np.zeros((16, 2048), dtype=np.float32)
    for n in range(16):
        sel16[n, n * 128:(n + 1) * 128] = 1.0

    in_maps = []
    for r in range(NCORES):
        b = r // 2
        H = r % 2
        ch = slice(H * DH, (H + 1) * DH)
        ih = slice(H * IH, (H + 1) * IH)
        w_in_c = np.concatenate(
            [w_in_full[:, H * DH:(H + 1) * DH],
             w_in_full[:, 2 * DH + H * DH: 2 * DH + (H + 1) * DH]], axis=1)
        pco = np.zeros((128, 104), dtype=np.float32)
        pco[:, 0:8] = ln_g.reshape(8, 128).T
        pco[:, 8:16] = ln_b.reshape(8, 128).T
        pco[:, 16:24] = w_dt[ch].reshape(8, 128).T
        pco[:, 24:32] = b_dt[ch].reshape(8, 128).T
        pco[:, 32:40] = d_par[ch].reshape(8, 128).T
        pco[:, 40:48] = conv_b[ch].reshape(8, 128).T
        pco[:, 48:56] = (0.5 * b2).reshape(8, 128).T
        cw = conv_w[ch].reshape(8, 128, D_CONV)
        for g in range(8):
            pco[:, 56 + g * 4:56 + (g + 1) * 4] = cw[g]
        pco[:, 88:104] = b1[ih].reshape(16, 128).T
        alog_c = np.zeros((128, 128), dtype=np.float32)
        al = a_log[ch].reshape(8, 128, D_STATE)
        for g in range(8):
            alog_c[:, g * 16:(g + 1) * 16] = al[g]
        wxp_pad = np.zeros((DH, 65), dtype=np.float32)
        wxp_pad[:, 0:16] = w_xp_full[ch, 0:16]
        wxp_pad[:, 32:48] = w_xp_full[ch, 16:32]
        wxp_pad[:, 64] = w_xp_full[ch, 32]
        in_maps.append({
            "x": np.ascontiguousarray(x[b]),
            "w_in": np.ascontiguousarray(w_in_c),
            "w1": np.ascontiguousarray(w1_full[:, ih]),
            "w2": np.ascontiguousarray(w2_full[ih, :]),
            "w_out": np.ascontiguousarray(w_out_full[ch, :]),
            "w_xp": wxp_pad,
            "pcons": pco,
            "a_log": alog_c,
            "identc": ident,
            "identrc": ident,
            "sel16c": sel16,
            "onesc": np.ones((128, 1), dtype=np.float32),
            "onesrc": np.ones((128, 1), dtype=np.float32),
            "zro16c": np.zeros((16, 128), dtype=np.float32),
        })

    res = run_bass_kernel_spmd(nc, in_maps, core_ids=list(range(NCORES)))
    out = np.empty((B, L, D_MODEL), dtype=np.float32)
    for b in range(B):
        out[b] = res.results[2 * b]["out"]
    return out


# version 22 (final)
# speedup vs baseline: 1.1885x; 1.0832x over previous
import os
import numpy as np
from contextlib import ExitStack

EMIT_DRAINS = os.environ.get("BASS_SIM_FP64") == "1" or os.environ.get("KERNEL_SIM_DRAINS") == "1"
HW_DRAINS = set(os.environ.get("KERNEL_HW_DRAINS", "p1").split(","))

import concourse.bass as bass
import concourse.mybir as mybir
from concourse.bass_utils import run_bass_kernel_spmd

F32 = mybir.dt.float32
F32R = mybir.dt.float32r
ALU = mybir.AluOpType
AF = mybir.ActivationFunctionType

D_MODEL = 1024
D_STATE = 16
D_CONV = 4
DH = 1024
IH = 2048
NCORES = 8
EPS = 1e-5
INV_D = 1.0 / D_MODEL

ENGS = ("sync", "gpsimd", "tensor", "vector", "scalar")


class Emitter:
    def __init__(self, nc, es):
        self.nc = nc
        self.streams = {e: [] for e in ENGS}
        self.cnt = {}
        self.sems = {}
        self.nbar = 0
        self.phase = "pro"
        self._es = es

    def sem(self, name):
        self.sems[name] = self._es.enter_context(self.nc.semaphore(name))
        self.cnt[name] = 0

    def op(self, eng, fn, inc=None, by=1):
        if inc is None:
            self.streams[eng].append(fn)
            return None
        s = self.sems[inc]
        self.cnt[inc] += by
        def wrapped(e):
            fn(e).then_inc(s, by)
        self.streams[eng].append(wrapped)
        return self.cnt[inc]

    def wait(self, eng, name, val, force=False):
        if val <= 0 and not force:
            return
        s = self.sems[name]
        self.streams[eng].append(lambda e: e.wait_ge(s, max(val, 0)))

    BAR_ENGS = ("gpsimd", "tensor", "vector", "scalar")

    def barrier(self):
        self.nbar += 1
        tgt = self.nbar * len(self.BAR_ENGS)
        s = self.sems["bar"]
        for eng in self.BAR_ENGS:
            self.streams[eng].append(lambda e: e.nop().then_inc(s, 1))
            self.streams[eng].append(lambda e: e.wait_ge(s, tgt))

    def drain(self, eng, hw=False):
        if EMIT_DRAINS or eng == "scalar" or hw or "all" in HW_DRAINS \
                or self.phase in HW_DRAINS:
            self.streams[eng].append(lambda e: e.drain())


def build_program(L):
    Tc = min(512, L)
    NCH = L // Tc
    HB = Tc // 2
    NG = DH // 128
    NDM = D_MODEL // 128
    NM1 = IH // 128
    NTS = Tc // 128

    NA_SLOT = 8
    NB_SLOT = 8

    nc = bass.Bass("TRN2", target_bir_lowering=False)

    x_hbm = nc.dram_tensor("x", [L, D_MODEL], F32, kind="ExternalInput").ap()
    w_in = nc.dram_tensor("w_in", [D_MODEL, 2 * DH], F32R, kind="ExternalInput").ap()
    w1 = nc.dram_tensor("w1", [D_MODEL, IH], F32R, kind="ExternalInput").ap()
    w2 = nc.dram_tensor("w2", [IH, D_MODEL], F32R, kind="ExternalInput").ap()
    w_out = nc.dram_tensor("w_out", [DH, D_MODEL], F32R, kind="ExternalInput").ap()
    w_xp = nc.dram_tensor("w_xp", [DH, 65], F32R, kind="ExternalInput").ap()
    pcons = nc.dram_tensor("pcons", [128, 104], F32, kind="ExternalInput").ap()
    a_log = nc.dram_tensor("a_log", [128, 128], F32, kind="ExternalInput").ap()
    ident_in = nc.dram_tensor("identc", [128, 128], F32, kind="ExternalInput").ap()
    sel16_in = nc.dram_tensor("sel16c", [16, 2048], F32R, kind="ExternalInput").ap()
    identr_in = nc.dram_tensor("identrc", [128, 128], F32R, kind="ExternalInput").ap()
    onesr_in = nc.dram_tensor("onesrc", [128, 1], F32R, kind="ExternalInput").ap()
    zro16_in = nc.dram_tensor("zro16c", [16, 128], F32R, kind="ExternalInput").ap()
    ones_in = nc.dram_tensor("onesc", [128, 1], F32, kind="ExternalInput").ap()
    out_hbm = nc.dram_tensor("out", [L, D_MODEL], F32, kind="ExternalOutput").ap()

    xpj_snd = nc.dram_tensor("xpj_snd", [NCH, 65, Tc], F32R)
    xpj_rcv = nc.dram_tensor("xpj_rcv", [NCH, 65, Tc], F32R)
    op_snd = nc.dram_tensor("op_snd", [NCH, 128, NDM * Tc], F32)
    op_rcv = nc.dram_tensor("op_rcv", [NCH, 128, NDM * Tc], F32)
    GROUPS = [[0, 1], [2, 3], [4, 5], [6, 7]]

    es = ExitStack()
    sb = lambda nm, shp, dt=F32: es.enter_context(nc.sbuf_tensor(nm, shp, dt))
    ps = lambda nm, shp: es.enter_context(nc.psum_tensor(nm, shp, F32))

    with es:
        pc = sb("pc", [128, 104])
        aneg = sb("aneg", [128, 128])
        identT = sb("identT", [128, 128])
        identR = sb("identR", [128, 128], F32R)
        sel16 = sb("sel16", [16, 2048], F32R)
        zro16 = sb("zro16", [16, 128], F32R)
        onesT = sb("onesT", [128, 1])
        onesR = sb("onesR", [128, 1], F32R)
        wxp_sb = sb("wxp_sb", [128, 8 * 65], F32R)

        wpa = sb("wpa", [128, NA_SLOT * 512], F32R)
        wpb = sb("wpb", [128, NB_SLOT * 256], F32R)

        tok = sb("tok", [128, 2 * D_MODEL])
        scr_r = sb("scr_r", [128, NDM * Tc], F32R)
        xn = sb("xn", [128, NDM * Tc], F32R)
        xn2 = sb("xn2", [128, NG * Tc], F32R)
        r1h = sb("r1h", [128, NDM * Tc])
        sq = sb("sq", [128, Tc])

        xh_rot = sb("xh_rot", [128, Tc + 3])
        halo = sb("halo", [128, NG * 3])
        cacc = sb("cacc", [128, Tc])
        row1 = cacc[0:1, 0:Tc]
        row2 = sq[0:1, 0:Tc]
        xc = sb("xc", [128, NG * Tc], F32R)
        sz = sb("sz", [128, NG * Tc])
        dt_sb = sb("dt_sb", [128, NG * Tc])
        xpjB = sb("xpjB", [16, Tc], F32R)
        xpjC = sb("xpjC", [16, Tc], F32R)

        da_b = sb("da_b", [128, 3 * Tc])
        dbx_b = sb("dbx_b", [128, 2 * Tc])
        h32_b = sb("h32_b", [128, 2 * Tc])
        hc_b = sb("hc_b", [128, 4 * Tc], F32R)
        state = sb("state", [128, 128])

        cst = sb("cst", [128, 4])
        ysz = sb("ysz", [128, NG * Tc], F32R)
        bst = ysz[0:16, 0:Tc]
        xpjp = ysz[0:65, Tc:2 * Tc]

        pA = ps("pA", [128, 512])
        pB = ps("pB", [128, 512])
        pT = ps("pT", [128, 512])
        pS = ps("pS", [128, 512])
        pY = [ps(f"pY{i}", [128, 512]) for i in range(4)]

        nc.const_aps.register(F32, EPS, cst[:, 0:1])
        nc.const_aps.register(F32, 1.0, cst[:, 1:2])
        nc.const_aps.register(F32, -0.5, cst[:, 2:3])
        nc.const_aps.register(F32, -1.0, cst[:, 3:4])

        em = Emitter(nc, es)
        for s in ("bar", "ldw", "ldwb", "ldx", "lda", "mm", "ev", "ev2", "dv", "da",
                  "bc", "sc", "hc", "dbxs", "yx", "cc", "io", "st", "ffs"):
            em.sem(s)

        for t, src in ((pc, pcons), (aneg, a_log), (identT, ident_in),
                       (identR, identr_in), (sel16, sel16_in), (zro16, zro16_in)):
            em.op("sync", lambda e, t=t, src=src: e.dma_start(out=t[:], in_=src),
                  inc="lda", by=16)
        em.op("sync", lambda e: e.dma_start(out=onesT[:], in_=ones_in),
              inc="lda", by=16)
        em.op("sync", lambda e: e.dma_start(out=onesR[:], in_=onesr_in[:, 0:1]),
              inc="lda", by=16)
        em.op("sync", lambda e: e.dma_start(
            out=wxp_sb[:, 0:8 * 65].rearrange("p (g c) -> p g c", c=65),
            in_=w_xp.rearrange("(g p) c -> p g c", p=128)), inc="lda", by=16)
        v_lda = em.cnt["lda"]
        em.wait("scalar", "lda", v_lda)
        em.op("scalar", lambda e: e.activation(aneg[:], aneg[:], AF.Exp), inc="ev")
        ev0 = em.cnt["ev"]
        em.wait("vector", "ev", ev0)
        em.op("vector", lambda e: e.tensor_scalar(aneg[:], aneg[:], -1.0, None, ALU.mult), inc="dv")
        em.drain("vector")
        em.op("vector", lambda e: e.memset(state[:], 0.0), inc="dv")
        em.drain("vector")
        em.op("vector", lambda e: e.memset(halo[:], 0.0), inc="dv")
        em.drain("vector")
        em.op("vector", lambda e: e.memset(cacc[:], 0.0), inc="dv")
        em.drain("vector")
        em.op("vector", lambda e: e.tensor_copy(ysz[0:16, 0:Tc], cacc[0:16, 0:Tc]), inc="dv")
        em.drain("vector")
        em.op("vector", lambda e: e.memset(cst[:, 0:1], EPS), inc="dv")
        em.op("vector", lambda e: e.memset(cst[:, 1:2], 1.0), inc="dv")
        em.op("vector", lambda e: e.memset(cst[:, 2:3], -0.5), inc="dv")
        em.op("vector", lambda e: e.memset(cst[:, 3:4], -1.0), inc="dv")
        em.drain("vector")
        em.wait("scalar", "dv", em.cnt["dv"])
        em.wait("tensor", "lda", v_lda)
        em.wait("tensor", "dv", em.cnt["dv"])

        class LoadQ:
            def __init__(self, pool, sem_name, nslot, width=512):
                self.pool = pool
                self.sem = sem_name
                self.width = width
                self.ring = nslot // 4
                self.pending = []
                self.n_loaded = 0
                self.cons = []

            def add(self, w, kslice, cslice):
                self.pending.append((w, kslice, cslice))

            def record(self, mm_val):
                self.cons.append(mm_val)
                self.pump()

            def pump(self):
                while self.pending:
                    g_ = self.n_loaded // 4
                    if g_ - self.ring >= len(self.cons):
                        break
                    if self.n_loaded % 4 == 0:
                        if g_ >= self.ring:
                            em.wait("sync", "mm", self.cons[g_ - self.ring])
                        if g_ >= 1:
                            em.wait("sync", self.sem, self.n_loaded * 16)
                    w, ks, cs = self.pending.pop(0)
                    s_ = self.n_loaded % (self.ring * 4)
                    wd = self.width
                    em.op("sync", lambda e, s_=s_, w=w, ks=ks, cs=cs, wd=wd: e.dma_start(
                        out=self.pool[:, s_ * wd:(s_ + 1) * wd],
                        in_=w[ks, cs]), inc=self.sem, by=16)
                    self.n_loaded += 1

        qa = LoadQ(wpa, "ldw", NA_SLOT)
        qb = LoadQ(wpb, "ldwb", NB_SLOT, width=256)

        def load_w_in(_c):
            for mg in range(4):
                for k in range(8):
                    qa.add(w_in, slice(k * 128, (k + 1) * 128),
                           slice(mg * 512, (mg + 1) * 512))
            qa.pump()

        def load_w1(_c):
            for mg in range(4):
                for k in range(8):
                    qa.add(w1, slice(k * 128, (k + 1) * 128),
                           slice(mg * 512, (mg + 1) * 512))
            qa.pump()

        def load_w2_wout(_c):
            for k in range(16):
                for q in range(4):
                    qb.add(w2, slice(k * 128, (k + 1) * 128),
                           slice(q * 256, (q + 1) * 256))
            for k in range(8):
                for q in range(4):
                    qb.add(w_out, slice(k * 128, (k + 1) * 128),
                           slice(q * 256, (q + 1) * 256))
            qb.pump()

        class XQ:
            def __init__(self):
                self.pending = []
                self.n_loaded = 0
                self.cons = []

            def add(self, c, ts):
                self.pending.append((c, ts))

            def record(self, mm_val):
                self.cons.append(mm_val)
                self.pump()

            def pump(self):
                while self.pending:
                    if self.n_loaded - 2 >= len(self.cons):
                        break
                    if self.n_loaded >= 2:
                        em.wait("sync", "mm", self.cons[self.n_loaded - 2])
                    if self.n_loaded >= 1:
                        em.wait("sync", "ldx", self.n_loaded * 16)
                    c_, ts_ = self.pending.pop(0)
                    sl_ = self.n_loaded % 2
                    em.op("sync", lambda e, c_=c_, ts_=ts_, sl_=sl_: e.dma_start(
                        out=tok[:, sl_ * D_MODEL:(sl_ + 1) * D_MODEL],
                        in_=x_hbm[c_ * Tc + ts_ * 128: c_ * Tc + (ts_ + 1) * 128, :]),
                        inc="ldx", by=16)
                    self.n_loaded += 1

        xq = XQ()

        def load_x(c):
            for ts_ in range(NTS):
                xq.add(c, ts_)
            xq.pump()

        load_x(0)
        load_w_in(0)
        em.barrier()

        st_marks = []
        cc_op_marks = {}

        def emit_p9_ts(cp, ts_):
            if ts_ == 0:
                em.wait("sync", "cc", cc_op_marks[cp])
            em.wait("sync", "io", em.cnt["io"])
            if st_marks:
                em.wait("sync", "st", st_marks[-1])
            em.op("sync", lambda e, cp=cp, ts_=ts_: e.dma_start(
                out=tok[:, 0:D_MODEL].rearrange("p (dm c) -> p dm c", c=128),
                in_=op_rcv.ap()[cp, :, :].rearrange(
                    "p (dm t) -> p dm t", t=Tc)[:, :, ts_ * 128:(ts_ + 1) * 128]),
                inc="io", by=16)
            io_ts = em.cnt["io"]
            em.wait("tensor", "io", io_ts)
            for half in range(2):
                if not (ts_ == 0 and half == 0):
                    em.wait("tensor", "ev", em.cnt["ev"])
                for dmq in range(4):
                    dm = half * 4 + dmq
                    em.op("tensor", lambda e, dm=dm, dmq=dmq: e.matmul(
                        pS[:, dmq * 128:(dmq + 1) * 128],
                        tok[:, dm * 128:(dm + 1) * 128],
                        identT[:], start=(dmq == 0), stop=(dmq == 3),
                        is_transpose=True),
                        inc=("mm" if dmq == 3 else None))
                mmv = em.cnt["mm"]
                em.wait("scalar", "mm", mmv)
                if len(st_marks) >= 1:
                    em.wait("scalar", "st", st_marks[-1])
                em.op("scalar", lambda e, half=half: e.copy(
                    tok[:, D_MODEL + half * 512: D_MODEL + (half + 1) * 512],
                    pS[:, 0:512]), inc="ev")
            ev_ts = em.cnt["ev"]
            em.wait("sync", "ev", ev_ts)
            em.op("sync", lambda e, cp=cp, ts_=ts_: e.dma_start(
                out=out_hbm[cp * Tc + ts_ * 128: cp * Tc + (ts_ + 1) * 128, :],
                in_=tok[:, D_MODEL:2 * D_MODEL]), inc="st", by=16)
            st_marks.append(em.cnt["st"])

        def emit_p9(cp):
            for ts_ in range(NTS):
                emit_p9_ts(cp, ts_)
            return em.cnt["mm"]

        for c in range(NCH):
            em.phase = "p1"
            for ts_ in range(NTS):
                gi = c * NTS + ts_
                sl_ = gi % 2
                bks = ((pA, pB), (pY[0], pY[1]))[ts_ % 2]
                em.wait("tensor", "ldx", (gi + 1) * 16)
                if ts_ >= 2:
                    em.wait("tensor", "ev", em.cnt["ev"])
                for dm in range(NDM):
                    bk = bks[dm // 4]
                    em.op("tensor", lambda e, bk=bk, sl_=sl_, dm=dm: e.matmul(
                        bk[:, (dm % 4) * 128:(dm % 4 + 1) * 128],
                        tok[:, sl_ * D_MODEL + dm * 128: sl_ * D_MODEL + (dm + 1) * 128],
                        identT[:], start=(dm % 4 == 0), stop=(dm % 4 == 3),
                        is_transpose=True),
                        inc=("mm" if dm == NDM - 1 else None))
                mmv = em.cnt["mm"]
                xq.record(mmv)
                em.wait("scalar", "mm", mmv)
                for dm in range(NDM):
                    bk = bks[dm // 4]
                    em.op("scalar", lambda e, bk=bk, dm=dm, ts_=ts_: e.copy(
                        scr_r[:, dm * Tc + ts_ * 128: dm * Tc + (ts_ + 1) * 128],
                        bk[:, (dm % 4) * 128:(dm % 4 + 1) * 128]), inc="ev")
            ev_xfm = em.cnt["ev"]
            em.wait("tensor", "ev", ev_xfm)
            for dm in range(NDM):
                em.op("tensor", lambda e, dm=dm: e.matmul(
                    pT[0:1, 0:Tc], onesR[:], scr_r[:, dm * Tc:(dm + 1) * Tc],
                    start=(dm == 0), stop=(dm == NDM - 1)),
                    inc=("mm" if dm == NDM - 1 else None))
            sq_mm = []
            for dm in range(NDM):
                em.wait("scalar", "ev", ev_xfm)
                if dm >= 1:
                    em.wait("scalar", "mm", sq_mm[dm - 1])
                em.op("scalar", lambda e, dm=dm: e.activation(
                    sq[:, 0:Tc],
                    scr_r[:, dm * Tc:(dm + 1) * Tc], AF.Square), inc="ev")
                em.drain("scalar")
                em.wait("tensor", "ev", em.cnt["ev"])
                em.op("tensor", lambda e, dm=dm: e.matmul(
                    pT[32:33, 0:Tc], onesT[:], sq[:, 0:Tc],
                    start=(dm == 0), stop=(dm == NDM - 1)), inc="mm")
                sq_mm.append(em.cnt["mm"])
            mm_st = em.cnt["mm"]

            def row_stats_bcast(mm_wait, psrc0, psrc1):
                em.wait("vector", "mm", mm_wait)
                em.op("vector", lambda e: e.tensor_scalar(
                    row1, psrc1, INV_D, None, ALU.mult), inc="dv")
                em.drain("vector")
                em.op("vector", lambda e: e.tensor_scalar(
                    bst[0:1, :], psrc0, INV_D, None, ALU.mult), inc="dv")
                em.drain("vector")
                dv_mu = em.cnt["dv"]
                em.wait("tensor", "dv", dv_mu)
                em.op("tensor", lambda e: e.matmul(
                    pA[:, 0:Tc], sel16[:, 0:128], bst[:, 0:Tc], start=True, stop=True), inc="mm")
                mm_mu = em.cnt["mm"]
                em.op("vector", lambda e: e.tensor_mul(
                    row2, bst[0:1, :], bst[0:1, :]), inc="dv")
                em.drain("vector")
                em.op("vector", lambda e: e.tensor_sub(row1, row1, row2), inc="dv")
                em.drain("vector")
                dvv = em.cnt["dv"]
                em.wait("scalar", "dv", dvv)
                em.op("scalar", lambda e: e.activation(row2, row1, AF.Ln, bias=EPS), inc="ev")
                em.drain("scalar")
                em.op("scalar", lambda e: e.activation(row2, row2, AF.Exp, scale=-0.5), inc="ev")
                evv = em.cnt["ev"]
                em.wait("vector", "ev", evv)
                em.wait("vector", "mm", mm_mu)
                em.op("vector", lambda e: e.tensor_copy(bst[0:1, :], row2), inc="dv")
                em.drain("vector")
                dv_r = em.cnt["dv"]
                em.wait("tensor", "dv", dv_r)
                em.op("tensor", lambda e: e.matmul(
                    pB[:, 0:Tc], sel16[:, 0:128], bst[:, 0:Tc], start=True, stop=True), inc="mm")
                return em.cnt["mm"]

            mm_bc1 = row_stats_bcast(mm_st, pT[0:1, 0:Tc], pT[32:33, 0:Tc])
            em.wait("vector", "mm", mm_bc1)
            for dm in range(NDM):
                em.op("vector", lambda e, dm=dm: e.tensor_sub(
                    cacc[:, 0:Tc], scr_r[:, dm * Tc:(dm + 1) * Tc], pA[:, 0:Tc]))
                em.drain("vector")
                em.op("vector", lambda e, dm=dm: e.tensor_mul(
                    xn[:, dm * Tc:(dm + 1) * Tc], cacc[:, 0:Tc], pB[:, 0:Tc]), inc="dv")
                em.drain("vector")
            dv_xn = em.cnt["dv"]
            em.wait("tensor", "dv", dv_xn)
            for dm in range(NDM):
                em.op("tensor", lambda e, dm=dm: e.matmul(
                    pS[0:1, 0:Tc], onesR[:], xn[:, dm * Tc:(dm + 1) * Tc],
                    start=(dm == 0), stop=(dm == NDM - 1)),
                    inc=("mm" if dm == NDM - 1 else None))
            sq_mm2 = []
            for dm in range(NDM):
                em.wait("scalar", "dv", dv_xn)
                if dm >= 1:
                    em.wait("scalar", "mm", sq_mm2[dm - 1])
                em.op("scalar", lambda e, dm=dm: e.activation(
                    sq[:, 0:Tc],
                    xn[:, dm * Tc:(dm + 1) * Tc], AF.Square), inc="ev")
                em.drain("scalar")
                em.wait("tensor", "ev", em.cnt["ev"])
                em.op("tensor", lambda e, dm=dm: e.matmul(
                    pS[32:33, 0:Tc], onesT[:], sq[:, 0:Tc],
                    start=(dm == 0), stop=(dm == NDM - 1)), inc="mm")
                sq_mm2.append(em.cnt["mm"])
            mm_st2 = em.cnt["mm"]
            mm_bc2 = row_stats_bcast(mm_st2, pS[0:1, 0:Tc], pS[32:33, 0:Tc])
            em.wait("vector", "mm", mm_bc2)
            for dm in range(NDM):
                em.op("vector", lambda e, dm=dm: e.tensor_sub(
                    cacc[:, 0:Tc], xn[:, dm * Tc:(dm + 1) * Tc], pA[:, 0:Tc]))
                em.drain("vector")
                em.op("vector", lambda e, dm=dm: e.tensor_mul(
                    sq[:, 0:Tc], cacc[:, 0:Tc], pB[:, 0:Tc]))
                em.drain("vector")
                em.op("vector", lambda e, dm=dm: e.tensor_scalar(
                    xn2[:, dm * Tc:(dm + 1) * Tc], sq[:, 0:Tc],
                    pc[:, dm:dm + 1], pc[:, 8 + dm:9 + dm], ALU.mult, ALU.add), inc="dv")
                em.drain("vector")
                em.op("vector", lambda e, dm=dm: e.tensor_add(
                    cacc[:, 0:Tc], scr_r[:, dm * Tc:(dm + 1) * Tc],
                    xn[:, dm * Tc:(dm + 1) * Tc]))
                em.drain("vector")
                em.op("vector", lambda e, dm=dm: e.tensor_scalar(
                    r1h[:, dm * Tc:(dm + 1) * Tc], cacc[:, 0:Tc],
                    0.5, pc[:, 48 + dm:49 + dm], ALU.mult, ALU.add), inc="dv")
                em.drain("vector")
            em.barrier()

            em.phase = "p2"
            base_a = c * 96
            for m in range(2 * NG):
                mg, mi = divmod(m, 4)
                pb = (pA, pB)[m % 2]
                em.wait("tensor", "ldw", (base_a + (mg + 1) * 8) * 16)
                if m >= 2:
                    em.wait("tensor", "ev", em.cnt["ev"])
                    em.wait("tensor", "ev2", em.cnt["ev2"])
                for k in range(8):
                    sl = (base_a + mg * 8 + k) % NA_SLOT
                    src = xn2
                    em.op("tensor", lambda e, pb=pb, sl=sl, mi=mi, k=k, src=src: e.matmul(
                        pb[:, 0:Tc],
                        wpa[:, sl * 512 + mi * 128: sl * 512 + (mi + 1) * 128],
                        src[:, k * Tc:(k + 1) * Tc],
                        start=(k == 0), stop=(k == 7)),
                        inc=("mm" if k == 7 else None))
                mmv = em.cnt["mm"]
                if m == 4 * mg + 3:
                    qa.record(mmv)
                    qa.record(mmv)
                em.wait("scalar", "mm", mmv)
                if m < NG:
                    g = m
                    hb = 0
                    em.wait("scalar", "dv", em.cnt["dv"]) if g >= 1 else None
                    em.op("scalar", lambda e, pb=pb, hb=hb: e.copy(
                        xh_rot[:, hb + 3:hb + 3 + Tc], pb[:, 0:Tc]), inc="ev")
                    ev_xh = em.cnt["ev"]
                    em.wait("vector", "ev", ev_xh)
                    em.op("vector", lambda e, hb=hb, g=g: e.tensor_copy(
                        xh_rot[:, hb:hb + 3], halo[:, g * 3:(g + 1) * 3]))
                    em.drain("vector")
                    em.op("vector", lambda e, hb=hb, g=g: e.tensor_scalar(
                        cacc[:, 0:Tc], xh_rot[:, hb:hb + Tc],
                        pc[:, 56 + g * 4:57 + g * 4], None, ALU.mult))
                    em.drain("vector")
                    for kk in range(1, D_CONV):
                        em.op("vector", lambda e, hb=hb, g=g, kk=kk: e.scalar_tensor_tensor(
                            cacc[:, 0:Tc], xh_rot[:, hb + kk:hb + kk + Tc],
                            pc[:, 56 + g * 4 + kk:57 + g * 4 + kk],
                            cacc[:, 0:Tc], ALU.mult, ALU.add))
                        em.drain("vector")
                    em.op("vector", lambda e, g=g: e.tensor_scalar(
                        sq[:, 0:Tc], cacc[:, 0:Tc], 1.0,
                        pc[:, 40 + g:41 + g], ALU.mult, ALU.add))
                    em.drain("vector")
                    em.op("vector", lambda e, hb=hb, g=g: e.tensor_copy(
                        halo[:, g * 3:(g + 1) * 3], xh_rot[:, hb + Tc:hb + Tc + 3]),
                        inc="dv")
                    em.drain("vector")
                    em.wait("scalar", "dv", em.cnt["dv"])
                    em.wait("scalar", "ev2", em.cnt["ev2"])
                    em.op("scalar", lambda e: e.activation(
                        da_b[:, 0:Tc], sq[:, 0:Tc], AF.Tanh, scale=0.5),
                        inc="ev")
                    em.drain("scalar")
                    em.wait("vector", "ev", em.cnt["ev"])
                    em.op("vector", lambda e: e.tensor_scalar(
                        da_b[:, Tc:2 * Tc], da_b[:, 0:Tc], 0.5, 0.5,
                        ALU.mult, ALU.add))
                    em.drain("vector")
                    em.op("vector", lambda e, g=g: e.tensor_mul(
                        xc[:, g * Tc:(g + 1) * Tc], sq[:, 0:Tc],
                        da_b[:, Tc:2 * Tc]), inc="ev2")
                    em.drain("vector")
                    em.wait("tensor", "ev2", em.cnt["ev2"])
                    em.op("tensor", lambda e, g=g: e.matmul(
                        pY[2][0:65, 0:Tc], wxp_sb[:, g * 65:(g + 1) * 65],
                        xc[:, g * Tc:(g + 1) * Tc],
                        start=(g == 0), stop=(g == NG - 1)),
                        inc=("mm" if g == NG - 1 else None))
                    if g == NG - 1:
                        mm_xpj = em.cnt["mm"]
                        em.wait("scalar", "mm", mm_xpj)
                        em.op("scalar", lambda e: e.copy(xpjp, pY[2][0:65, 0:Tc]), inc="ev")
                        ev_xpj = em.cnt["ev"]
                        em.wait("sync", "ev", ev_xpj)
                        em.op("sync", lambda e, c=c: e.dma_start(
                            out=xpj_snd.ap()[c, :, :], in_=xpjp), inc="io", by=16)
                        io_snd = em.cnt["io"]
                        em.wait("gpsimd", "io", io_snd)
                        em.op("gpsimd", lambda e, c=c: e.collective_compute(
                            "AllReduce", ALU.add, replica_groups=GROUPS,
                            ins=[xpj_snd.ap()[c, :, :]], outs=[xpj_rcv.ap()[c, :, :]]), inc="cc")
                        cc_xpj = em.cnt["cc"]
                else:
                    g = m - NG
                    em.wait("scalar", "ev2", em.cnt["ev2"])
                    em.op("scalar", lambda e, pb=pb: e.activation(
                        da_b[:, 0:Tc], pb[:, 0:Tc], AF.Tanh, scale=0.5), inc="ev")
                    em.drain("scalar")
                    em.wait("vector", "ev", em.cnt["ev"])
                    em.op("vector", lambda e: e.tensor_scalar(
                        da_b[:, Tc:2 * Tc], da_b[:, 0:Tc], 0.5, 0.5,
                        ALU.mult, ALU.add))
                    em.drain("vector")
                    em.op("vector", lambda e, pb=pb, g=g: e.tensor_mul(
                        sz[:, g * Tc:(g + 1) * Tc], pb[:, 0:Tc],
                        da_b[:, Tc:2 * Tc]), inc="ev2")
                    em.drain("vector")
            ev_p2 = em.cnt["ev"]
            em.barrier()

            em.phase = "p3"
            load_w1(c)
            load_w1(c)
            load_w2_wout(c)

            em.wait("sync", "cc", cc_xpj)
            em.op("sync", lambda e, c=c: e.dma_start(
                out=xpjB[:, :], in_=xpj_rcv.ap()[c, 0:16, :]), inc="io", by=16)
            em.op("sync", lambda e, c=c: e.dma_start(
                out=xpjC[:, :], in_=xpj_rcv.ap()[c, 32:48, :]), inc="io", by=16)
            em.op("sync", lambda e, c=c: e.dma_start(
                out=bst[0:1, :], in_=xpj_rcv.ap()[c, 64:65, :]), inc="io", by=16)
            io_rcv = em.cnt["io"]
            em.wait("tensor", "io", io_rcv)
            em.op("tensor", lambda e: e.matmul(
                pT[:, 0:Tc], sel16[:, 0:128], bst[:, 0:Tc],
                start=True, stop=True), inc="mm")
            mm_xi = em.cnt["mm"]
            for g in range(NG):
                zs = g % 2
                em.wait("vector", "mm", mm_xi)
                em.op("vector", lambda e, g=g, zs=zs: e.tensor_scalar(
                    da_b[:, zs * Tc:(zs + 1) * Tc], pT[:, 0:Tc],
                    pc[:, 16 + g:17 + g], pc[:, 24 + g:25 + g],
                    ALU.mult, ALU.add), inc="dv")
                em.drain("vector")
                em.op("vector", lambda e, zs=zs: e.scalar_tensor_tensor(
                    dbx_b[:, zs * Tc:(zs + 1) * Tc], da_b[:, zs * Tc:(zs + 1) * Tc],
                    -1.0, da_b[:, zs * Tc:(zs + 1) * Tc], ALU.mult, ALU.max), inc="dv")
                em.drain("vector")
                em.wait("scalar", "dv", em.cnt["dv"])
                em.op("scalar", lambda e, zs=zs: e.activation(
                    dbx_b[:, zs * Tc:(zs + 1) * Tc], dbx_b[:, zs * Tc:(zs + 1) * Tc],
                    AF.Exp, scale=-1.0), inc="ev")
                em.drain("scalar")
                em.op("scalar", lambda e, zs=zs: e.activation(
                    dbx_b[:, zs * Tc:(zs + 1) * Tc], dbx_b[:, zs * Tc:(zs + 1) * Tc],
                    AF.Ln, bias=1.0), inc="ev")
                em.wait("vector", "ev", em.cnt["ev"])
                em.op("vector", lambda e, g=g, zs=zs: e.scalar_tensor_tensor(
                    dt_sb[:, g * Tc:(g + 1) * Tc], da_b[:, zs * Tc:(zs + 1) * Tc],
                    0.0, dbx_b[:, zs * Tc:(zs + 1) * Tc], ALU.max, ALU.add), inc="dv")
                em.drain("vector")
                em.op("vector", lambda e, g=g: e.tensor_mul(
                    xn2[:, g * Tc:(g + 1) * Tc], dt_sb[:, g * Tc:(g + 1) * Tc],
                    xc[:, g * Tc:(g + 1) * Tc]), inc="dv")
                em.drain("vector")
            em.barrier()

            em.phase = "p5"
            b_sc = em.cnt["sc"]
            b_dx = em.cnt["dbxs"]
            b_yx = em.cnt["yx"]
            yac_done = lambda ip: b_yx + (ip - 2 * (1 + ip // 32)) + 1 \
                if False else None
            yac_mark = {}
            dvq_gate = 0
            ffn1_q = []
            for m in range(NM1):
                ffn1_q.append(m)
            ffn1_mm = {}
            ffs_marks = []
            a_base_fb0 = base_a + 32

            def emit_ffn1_m(m, fb, dst, a_base):
                mg, mi = divmod(m, 4)
                pb = (pY[2], pY[3])[m % 2]
                em.wait("tensor", "ldw", (a_base + (mg + 1) * 8) * 16)
                if len(ffs_marks) >= 2:
                    em.wait("tensor", "ffs", ffs_marks[-2])
                for k in range(8):
                    sl = (a_base + mg * 8 + k) % NA_SLOT
                    em.op("tensor", lambda e, pb=pb, sl=sl, mi=mi, k=k, fb=fb: e.matmul(
                        pb[:, 0:HB],
                        wpa[:, sl * 512 + mi * 128: sl * 512 + (mi + 1) * 128],
                        xn[:, k * Tc + fb * HB: k * Tc + fb * HB + HB],
                        start=(k == 0), stop=(k == 7)),
                        inc=("mm" if k == 7 else None))
                mmv = em.cnt["mm"]
                if m == 4 * mg + 3:
                    qa.record(mmv)
                    qa.record(mmv)
                em.wait("scalar", "mm", mmv)
                em.op("scalar", lambda e, pb=pb, m=m, dst=dst: e.activation(
                    dst[:, m * HB:(m + 1) * HB], pb[:, 0:HB], AF.Relu,
                    bias=pc[:, 88 + m:89 + m]), inc="ffs")
                ffs_marks.append(em.cnt["ffs"])

            for hf in range(2):
                for qp in range(2):
                    blk = hf * 2 + qp
                    if c >= 1:
                        em.phase = "p9"
                        emit_p9_ts(c - 1, blk)
                        em.phase = "p5"
                    for n in range(D_STATE):
                        i0 = blk * 32 + n * 2
                        em.wait("tensor", "dbxs", b_dx + blk * 16 + n)
                        em.op("tensor", lambda e, n=n: e.matmul(
                            pA[:, 0:Tc], sel16[:, n * 128:(n + 1) * 128],
                            xpjB[:, :], start=True, stop=True), inc="bc")
                        pCb = (pB, pT)[n % 2]
                        if i0 >= 4:
                            em.wait("tensor", "sc", b_sc + i0 - 2)
                        em.op("tensor", lambda e, n=n, pCb=pCb: e.matmul(
                            pCb[:, 0:Tc], sel16[:, n * 128:(n + 1) * 128],
                            xpjC[:, :], start=True, stop=True), inc="bc")
                        bcv = em.cnt["bc"]
                        if n >= 1:
                            if n == 1 and dvq_gate:
                                em.wait("tensor", "dv", dvq_gate)
                            for g2 in range(2):
                                ip = blk * 32 + (n - 1) * 2 + g2
                                em.wait("tensor", "sc", b_sc + ip + 1)
                                em.op("tensor", lambda e, g2=g2, ip=ip, nn=n - 1: e.matmul(
                                    pY[g2][:, 0:Tc], identR[:],
                                    hc_b[:, (ip % 4) * Tc:(ip % 4 + 1) * Tc],
                                    start=(nn == 0), stop=(nn == D_STATE - 1)), inc="yx")
                                yac_mark[ip] = em.cnt["yx"]
                        if n % 2 == 0 and ffn1_q:
                            emit_ffn1_m(ffn1_q.pop(0), 0, scr_r, a_base_fb0)
                        if n >= 1:
                            em.wait("tensor", "sc", b_sc + blk * 32 + n * 2 + 1)
                            em.op("tensor", lambda e: e.matmul(
                                pY[0][:, 0:256], zro16[:], sel16[:, 0:256],
                                start=False, stop=False))
                            em.drain("tensor")
                        for g2 in range(2):
                            g = hf * 4 + qp * 2 + g2
                            i = blk * 32 + n * 2 + g2
                            sl = i % 3
                            s2 = i % 2
                            s4 = i % 4
                            acol = g * 16 + n
                            if i >= 3:
                                em.wait("scalar", "sc", b_sc + i - 2)
                            em.op("scalar", lambda e, g=g, acol=acol, sl=sl: e.activation(
                                da_b[:, sl * Tc:(sl + 1) * Tc],
                                dt_sb[:, g * Tc:(g + 1) * Tc], AF.Exp,
                                scale=aneg[:, acol:acol + 1]), inc="da")
                            dav = em.cnt["da"]
                            em.wait("vector", "bc", bcv)
                            em.op("vector", lambda e, g=g, s2=s2: e.tensor_mul(
                                dbx_b[:, s2 * Tc:(s2 + 1) * Tc],
                                xn2[:, g * Tc:(g + 1) * Tc], pA[:, 0:Tc]),
                                inc=("dbxs" if g2 == 1 else None))
                            em.drain("vector")
                            em.wait("vector", "da", dav)
                            em.op("vector", lambda e, sl=sl, s2=s2, i=i: e.tensor_tensor_scan(
                                h32_b[:, s2 * Tc:(s2 + 1) * Tc],
                                da_b[:, sl * Tc:(sl + 1) * Tc],
                                dbx_b[:, s2 * Tc:(s2 + 1) * Tc],
                                state[:, i:i + 1], ALU.mult, ALU.add))
                            em.drain("vector")
                            if i >= 4:
                                em.wait("vector", "yx", yac_mark[i - 4])
                            em.op("vector", lambda e, s2=s2, s4=s4, pCb=pCb: e.tensor_mul(
                                hc_b[:, s4 * Tc:(s4 + 1) * Tc],
                                h32_b[:, s2 * Tc:(s2 + 1) * Tc], pCb[:, 0:Tc]))
                            em.drain("vector")
                            em.op("vector", lambda e, s2=s2, i=i: e.tensor_copy(
                                state[:, i:i + 1],
                                h32_b[:, s2 * Tc + Tc - 1: s2 * Tc + Tc]), inc="sc")
                            em.drain("vector")
                    for g2 in range(2):
                        ip = blk * 32 + 15 * 2 + g2
                        em.wait("tensor", "sc", b_sc + ip + 1)
                        em.op("tensor", lambda e, g2=g2, ip=ip: e.matmul(
                            pY[g2][:, 0:Tc], identR[:],
                            hc_b[:, (ip % 4) * Tc:(ip % 4 + 1) * Tc],
                            start=False, stop=True), inc="yx")
                        yac_mark[ip] = em.cnt["yx"]
                    yx_blk = em.cnt["yx"]
                    for g2 in range(2):
                        g = hf * 4 + qp * 2 + g2
                        em.wait("vector", "yx", yx_blk)
                        em.op("vector", lambda e, g=g: e.tensor_scalar(
                            cacc[:, 0:Tc], xc[:, g * Tc:(g + 1) * Tc],
                            pc[:, 32 + g:33 + g], None, ALU.mult))
                        em.drain("vector")
                        em.op("vector", lambda e, g=g, g2=g2: e.scalar_tensor_tensor(
                            sq[:, 0:Tc], cacc[:, 0:Tc], 1.0,
                            pY[g2][:, 0:Tc], ALU.mult, ALU.add))
                        em.drain("vector")
                        em.op("vector", lambda e, g=g: e.tensor_mul(
                            ysz[:, g * Tc:(g + 1) * Tc], sq[:, 0:Tc],
                            sz[:, g * Tc:(g + 1) * Tc]), inc="dv")
                        em.drain("vector")
                    dvq_gate = em.cnt["dv"]
            em.barrier()

            em.phase = "p7"
            base_b = c * 96
            obank = (pA, pB, pT, pS)

            def emit_out_pass_both(ffs_tgt):
                bb = base_b
                em.wait("tensor", "ffs", ffs_tgt)
                obk = (pA, pB, pT, pS, pY[0], pY[1], pY[2], pY[3])
                def bank_of(fb, dm):
                    return obk[fb * 4 + dm % 4], (0 if dm < 4 else HB)
                for k in range(16):
                    so = bb + 4 * k
                    em.wait("tensor", "ldwb", (so + 4) * 16)
                    for dm in range(NDM):
                        sl = (so + dm // 2) % NB_SLOT
                        for fb, hsrc in ((0, scr_r), (1, xn2)):
                            bank, colo = bank_of(fb, dm)
                            em.op("tensor", lambda e, k=k, dm=dm, bank=bank, colo=colo, sl=sl, hsrc=hsrc: e.matmul(
                                bank[:, colo:colo + HB],
                                wpb[:, sl * 256 + (dm % 2) * 128:sl * 256 + (dm % 2 + 1) * 128],
                                hsrc[:, k * HB:(k + 1) * HB],
                                start=(k == 0 and dm < 4), stop=False),
                                inc=("mm" if (dm == NDM - 1 and fb == 1) else None))
                    qb.record(em.cnt["mm"])
                for k in range(8):
                    so = bb + 64 + 4 * k
                    em.wait("tensor", "ldwb", (so + 4) * 16)
                    for dm in range(NDM):
                        sl = (so + dm // 2) % NB_SLOT
                        for fb in (0, 1):
                            bank, colo = bank_of(fb, dm)
                            em.op("tensor", lambda e, k=k, dm=dm, bank=bank, colo=colo, sl=sl, fb=fb: e.matmul(
                                bank[:, colo:colo + HB],
                                wpb[:, sl * 256 + (dm % 2) * 128:sl * 256 + (dm % 2 + 1) * 128],
                                ysz[:, k * Tc + fb * HB: k * Tc + fb * HB + HB],
                                start=False, stop=(k == 7 and dm >= 4)),
                                inc=("mm" if (dm == NDM - 1 and fb == 1) else None))
                    qb.record(em.cnt["mm"])
                mm_out = em.cnt["mm"]
                em.wait("vector", "mm", mm_out)
                for fb in (0, 1):
                    for dm in range(NDM):
                        bank, colo = bank_of(fb, dm)
                        em.op("vector", lambda e, dm=dm, fb=fb, bank=bank, colo=colo: e.scalar_tensor_tensor(
                            dt_sb[:, dm * Tc + fb * HB: dm * Tc + fb * HB + HB],
                            r1h[:, dm * Tc + fb * HB: dm * Tc + fb * HB + HB],
                            1.0, bank[:, colo:colo + HB], ALU.mult, ALU.add), inc="dv")
                        em.drain("vector")
                return em.cnt["dv"]

            a_base_fb1 = base_a + 64
            for m in range(NM1):
                emit_ffn1_m(m, 1, xn2, a_base_fb1)
            ffs_fb1 = em.cnt["ffs"]
            dv1 = emit_out_pass_both(ffs_fb1)
            em.wait("sync", "dv", dv1)
            em.op("sync", lambda e, c=c: e.dma_start(
                out=op_snd.ap()[c, :, :], in_=dt_sb[:, 0:NDM * Tc]), inc="io", by=16)
            io_op = em.cnt["io"]
            em.wait("gpsimd", "io", io_op)
            em.op("gpsimd", lambda e, c=c: e.collective_compute(
                "AllReduce", ALU.add, replica_groups=GROUPS,
                ins=[op_snd.ap()[c, :, :]], outs=[op_rcv.ap()[c, :, :]]), inc="cc")
            cc_op_marks[c] = em.cnt["cc"]
            em.barrier()
            if c == NCH - 1:
                emit_p9(c)
            if c + 1 < NCH:
                em.wait("sync", "st", em.cnt["st"])
                load_x(c + 1)
                load_w_in(c + 1)
            em.barrier()

        em.wait("sync", "st", em.cnt["st"])

        with nc.Block() as blk:
            @blk.sync
            def _(eng):
                for fn in em.streams["sync"]:
                    fn(eng)

            @blk.gpsimd
            def _(eng):
                for fn in em.streams["gpsimd"]:
                    fn(eng)

            @blk.tensor
            def _(eng):
                for fn in em.streams["tensor"]:
                    fn(eng)

            @blk.vector
            def _(eng):
                for fn in em.streams["vector"]:
                    fn(eng)

            @blk.scalar
            def _(eng):
                for fn in em.streams["scalar"]:
                    fn(eng)

    return nc


NSCAN = 128


def kernel(**inputs):
    x = np.ascontiguousarray(np.asarray(inputs["L0_attn_x"], dtype=np.float32))
    B, L, _ = x.shape
    w_in_full = np.asarray(inputs["L0_attn_W_in"], dtype=np.float32)
    conv_w = np.asarray(inputs["L0_attn_conv_w"], dtype=np.float32)
    conv_b = np.asarray(inputs["L0_attn_conv_b"], dtype=np.float32)
    w_xp_full = np.asarray(inputs["L0_attn_W_xp"], dtype=np.float32)
    w_dt = np.asarray(inputs["L0_attn_W_dt"], dtype=np.float32).reshape(-1)
    b_dt = np.asarray(inputs["L0_attn_b_dt"], dtype=np.float32)
    a_log = np.asarray(inputs["L0_attn_A_log"], dtype=np.float32)
    d_par = np.asarray(inputs["L0_attn_D"], dtype=np.float32)
    w_out_full = np.asarray(inputs["L0_attn_W_out"], dtype=np.float32)
    ln_g = np.asarray(inputs["L0_attn_ln_g"], dtype=np.float32)
    ln_b = np.asarray(inputs["L0_attn_ln_b"], dtype=np.float32)
    w1_full = np.asarray(inputs["L0_ffn_W1"], dtype=np.float32)
    b1 = np.asarray(inputs["L0_ffn_b1"], dtype=np.float32)
    w2_full = np.asarray(inputs["L0_ffn_W2"], dtype=np.float32)
    b2 = np.asarray(inputs["L0_ffn_b2"], dtype=np.float32)

    nc = build_program(L)

    ident = np.eye(128, dtype=np.float32)
    sel16 = np.zeros((16, 2048), dtype=np.float32)
    for n in range(16):
        sel16[n, n * 128:(n + 1) * 128] = 1.0

    in_maps = []
    for r in range(NCORES):
        b = r // 2
        H = r % 2
        ch = slice(H * DH, (H + 1) * DH)
        ih = slice(H * IH, (H + 1) * IH)
        w_in_c = np.concatenate(
            [w_in_full[:, H * DH:(H + 1) * DH],
             w_in_full[:, 2 * DH + H * DH: 2 * DH + (H + 1) * DH]], axis=1)
        pco = np.zeros((128, 104), dtype=np.float32)
        pco[:, 0:8] = ln_g.reshape(8, 128).T
        pco[:, 8:16] = ln_b.reshape(8, 128).T
        pco[:, 16:24] = w_dt[ch].reshape(8, 128).T
        pco[:, 24:32] = b_dt[ch].reshape(8, 128).T
        pco[:, 32:40] = d_par[ch].reshape(8, 128).T
        pco[:, 40:48] = conv_b[ch].reshape(8, 128).T
        pco[:, 48:56] = (0.5 * b2).reshape(8, 128).T
        cw = conv_w[ch].reshape(8, 128, D_CONV)
        for g in range(8):
            pco[:, 56 + g * 4:56 + (g + 1) * 4] = cw[g]
        pco[:, 88:104] = b1[ih].reshape(16, 128).T
        alog_c = np.zeros((128, 128), dtype=np.float32)
        al = a_log[ch].reshape(8, 128, D_STATE)
        for g in range(8):
            alog_c[:, g * 16:(g + 1) * 16] = al[g]
        wxp_pad = np.zeros((DH, 65), dtype=np.float32)
        wxp_pad[:, 0:16] = w_xp_full[ch, 0:16]
        wxp_pad[:, 32:48] = w_xp_full[ch, 16:32]
        wxp_pad[:, 64] = w_xp_full[ch, 32]
        in_maps.append({
            "x": np.ascontiguousarray(x[b]),
            "w_in": np.ascontiguousarray(w_in_c),
            "w1": np.ascontiguousarray(w1_full[:, ih]),
            "w2": np.ascontiguousarray(w2_full[ih, :]),
            "w_out": np.ascontiguousarray(w_out_full[ch, :]),
            "w_xp": wxp_pad,
            "pcons": pco,
            "a_log": alog_c,
            "identc": ident,
            "identrc": ident,
            "sel16c": sel16,
            "onesc": np.ones((128, 1), dtype=np.float32),
            "onesrc": np.ones((128, 1), dtype=np.float32),
            "zro16c": np.zeros((16, 128), dtype=np.float32),
        })

    res = run_bass_kernel_spmd(nc, in_maps, core_ids=list(range(NCORES)))
    out = np.empty((B, L, D_MODEL), dtype=np.float32)
    for b in range(B):
        out[b] = res.results[2 * b]["out"]
    return out
